# version 03
# baseline (speedup 1.0000x reference)
import contextlib
import numpy as np
import ml_dtypes
import concourse.bass as bass
import concourse.mybir as mybir
from concourse.bass_utils import run_bass_kernel_spmd

F32 = mybir.dt.float32
BF16 = mybir.dt.bfloat16
U8 = mybir.dt.uint8
AF = mybir.ActivationFunctionType
ALU = mybir.AluOpType
AX = mybir.AxisListType
BF = ml_dtypes.bfloat16

ENGINES = ("t", "v", "s", "g", "q")
N_CORES = 8
D = 1024
T_CORE = 2048
NT = T_CORE // 128


class Prog:
    def __init__(self, nc):
        self.nc = nc
        self.stack = contextlib.ExitStack()
        self.sb_tensors = {}
        self.dram_tensors = {}
        self.sbuf_bytes = 0
        self.csem = {e: self.stack.enter_context(nc.semaphore(f"c_{e}")) for e in ENGINES}
        self.dsem = {e: self.stack.enter_context(nc.semaphore(f"d_{e}")) for e in ("q", "g", "s")}
        self.ccnt = {e: 0 for e in ENGINES}
        self.dcnt = {e: 0 for e in ("q", "g", "s")}
        self.last_write = {}
        self.readers = {}
        self.waited = {e: {} for e in ENGINES}
        self.steps = {e: [] for e in ENGINES}
        self.n_steps = 0
        self.konst_vals = []
        self.konst_tile = None
        self.KONST_CAP = 64
        self._building = False

    def K(self, val):
        assert not self._building, "p.K() called inside step lambda"
        val = float(val)
        if self.konst_tile is None:
            self.konst_tile = self.sb("konst", [128, self.KONST_CAP], F32)
            kh = self.dram("c_konst", [128, self.KONST_CAP], F32, kind="ExternalInput")
            self.dma("q", ["konst"], ["IN"],
                     lambda e: e.dma_start(out=self.konst_tile[:], in_=kh), tag="konst_load")
        if val in self.konst_vals:
            idx = self.konst_vals.index(val)
        else:
            idx = len(self.konst_vals)
            assert idx < self.KONST_CAP
            self.konst_vals.append(val)
        return self.konst_tile[:, idx:idx + 1]

    def konst_input(self):
        a = np.zeros((128, self.KONST_CAP), dtype=np.float32)
        for i, v in enumerate(self.konst_vals):
            a[:, i] = v
        return a

    def sb(self, name, shape, dtype):
        t = self.stack.enter_context(self.nc.sbuf_tensor(name, list(shape), dtype))
        self.sb_tensors[name] = t
        esz = 2 if dtype in (BF16, mybir.dt.float16) else (1 if dtype == U8 else 4)
        self.sbuf_bytes += int(np.prod(shape[1:])) * esz
        return t

    def ps(self, name, shape, dtype=F32):
        return self.stack.enter_context(self.nc.psum_tensor(name, list(shape), dtype))

    def dram(self, name, shape, dtype, kind="Internal"):
        t = self.nc.dram_tensor(name, list(shape), dtype, kind=kind)
        self.dram_tensors[name] = t
        return t.ap()

    def _wait_needed(self, e, dep):
        kind, src, cnt = dep
        if self.waited[e].get((kind, src), 0) >= cnt:
            return None
        return dep

    def step(self, e, reads, writes, fn, dma=False, tag=""):
        assert e in ENGINES
        reads = list(reads)
        if self.konst_tile is not None and "konst" not in reads and "konst" not in writes:
            reads.append("konst")
        deps = []
        for r in reads:
            lw = self.last_write.get(r)
            if lw is not None:
                deps.append(self._wait_needed(e, lw))
        for w in writes:
            lw = self.last_write.get(w)
            if lw is not None:
                deps.append(self._wait_needed(e, lw))
            for rd in self.readers.get(w, ()):
                deps.append(self._wait_needed(e, rd))
        best = {}
        for dd in deps:
            if dd is None:
                continue
            kind, src, cnt = dd
            if best.get((kind, src), 0) < cnt:
                best[(kind, src)] = cnt
        waits = []
        for (kind, src), cnt in best.items():
            if self.waited[e].get((kind, src), 0) >= cnt:
                continue
            self.waited[e][(kind, src)] = cnt
            sem = self.csem[src] if kind == "c" else self.dsem[src]
            waits.append((sem, cnt if kind == "c" else 16 * cnt))
        if dma:
            assert e in self.dsem
            self.dcnt[e] += 1
            me = ("d", e, self.dcnt[e])
            sem, inc = self.dsem[e], 16
        else:
            self.ccnt[e] += 1
            me = ("c", e, self.ccnt[e])
            sem, inc = self.csem[e], 1
        for w in writes:
            self.last_write[w] = me
            self.readers[w] = []
        for r in reads:
            self.readers.setdefault(r, []).append(me)
        self.steps[e].append((waits, fn, sem, inc, tag))
        self.n_steps += 1
        return me

    def dma(self, q, dst_keys, src_keys, fn, tag=""):
        return self.step(q, src_keys, dst_keys, fn, dma=True, tag=tag)

    def build(self, final_engine="q"):
        nc = self.nc
        self._building = True
        with nc.Block() as block:
            def make(e, eng_obj):
                for waits, fn, sem, inc, tag in self.steps[e]:
                    for s_, v in waits:
                        eng_obj.wait_ge(s_, v)
                    last = fn(eng_obj)
                    if isinstance(last, list):
                        last = last[-1]
                    if last is None:
                        raise RuntimeError(f"step returned None on {e} tag={tag}")
                    last.then_inc(sem, inc)
                if e == final_engine:
                    for e2 in ENGINES:
                        if self.ccnt[e2] and e2 != e:
                            eng_obj.wait_ge(self.csem[e2], self.ccnt[e2])
                    for q2 in self.dcnt:
                        if self.dcnt[q2]:
                            eng_obj.wait_ge(self.dsem[q2], 16 * self.dcnt[q2])

            @block.tensor
            def _(eng):
                make("t", eng)

            @block.vector
            def _(eng):
                make("v", eng)

            @block.scalar
            def _(eng):
                make("s", eng)

            @block.gpsimd
            def _(eng):
                make("g", eng)

            @block.sync
            def _(eng):
                make("q", eng)
        return nc


def make_nc():
    return bass.Bass("TRN2", target_bir_lowering=False, debug=False)


def setup_common(p, T=T_CORE):
    c = {"T": T, "nt": T // 128}
    idh = p.dram("c_ident_bf", [128, 128], BF16, kind="ExternalInput")
    c["ident_bf"] = p.sb("ident_bf", [128, 128], BF16)
    p.dma("q", ["ident_bf"], ["IN"], lambda e: e.dma_start(out=c["ident_bf"][:], in_=idh))
    onh = p.dram("c_ones_bf", [1, 512], BF16, kind="ExternalInput")
    c["ones_bf"] = p.sb("ones_bf", [1, 512], BF16)
    p.dma("q", ["ones_bf"], ["IN"], lambda e: e.dma_start(out=c["ones_bf"][:], in_=onh))
    c["ps_T"] = p.ps("ps_T", [128, 512], BF16)
    c["ps_a"] = p.ps("ps_a", [128, 512], F32)
    c["ps_h1"] = p.ps("ps_h1", [128, 512], F32)
    c["ps_h2"] = p.ps("ps_h2", [128, 512], F32)
    c["ps_y1"] = p.ps("ps_y1", [128, 512], F32)
    c["ps_y2"] = p.ps("ps_y2", [128, 512], F32)
    return c


def common_const_inputs(p):
    return {
        "c_ident_bf": np.eye(128, dtype=BF),
        "c_ones_bf": np.ones((1, 512), dtype=BF),
        "c_konst": p.konst_input(),
    }


def alloc_x(p, c):
    c["xt"] = [p.sb(f"xt{i}", [128, D], F32) for i in range(c["nt"])]


def load_x(p, c, xh):
    for i in range(c["nt"]):
        p.dma("q", [f"xt{i}"], ["IN"],
              lambda e, i=i: e.dma_start(out=c["xt"][i][:], in_=xh[i * 128:(i + 1) * 128, :]))


def store_x(p, c, oh):
    for i in range(c["nt"]):
        p.dma("q", ["OUT"], [f"xt{i}"],
              lambda e, i=i: e.dma_start(out=oh[i * 128:(i + 1) * 128, :], in_=c["xt"][i][:]))


def emit_ln_fm(p, c, want_tm=False, want_z=False, store_dram=None, tmp_tag=""):
    nt, T = c["nt"], c["T"]
    if "x_fm" not in c:
        c["x_fm"] = p.sb("x_fm", [128, 8 * T], BF16)
        c["ln_stats"] = p.sb("ln_stats", [128, 8], F32)
        c["ln_sq"] = p.sb("ln_sq", [128, D], BF16)
        c["ln_xc"] = p.sb("ln_xc", [128, D], BF16)
    if want_tm and "xc" not in c:
        c["xc"] = [p.sb(f"xc{i}", [128, D], BF16) for i in range(nt)]
    if want_z and "zt" not in c:
        c["zt"] = [p.sb(f"zt{i}", [128, D], F32) for i in range(nt)]
    eps = p.K(1e-5)
    st = c["ln_stats"]
    for i in range(nt):
        X = f"xt{i}"
        xt = c["xt"][i]
        p.step("v", [X], ["lnsx"], lambda e, xt=xt: e.tensor_reduce(
            out=st[:, 0:1], in_=xt[:], axis=AX.X, op=ALU.add))
        p.step("s", [X], ["ln_sq", "lnsxx"], lambda e, xt=xt: e.activation(
            out=c["ln_sq"][:], in_=xt[:], func=AF.Square, accum_out=st[:, 1:2]))
        p.step("v", ["lnsx"], ["lnmu"], lambda e: e.tensor_scalar(
            out=st[:, 2:3], in0=st[:, 0:1], scalar1=1.0 / D, scalar2=None, op0=ALU.mult))
        p.step("v", ["lnsxx"], ["ln3"], lambda e: e.tensor_scalar(
            out=st[:, 3:4], in0=st[:, 1:2], scalar1=1.0 / D, scalar2=None, op0=ALU.mult))
        p.step("v", ["ln3", "lnmu"], ["ln3"], lambda e: e.scalar_tensor_tensor(
            out=st[:, 3:4], in0=st[:, 2:3], scalar=st[:, 2:3], in1=st[:, 3:4],
            op0=ALU.mult, op1=ALU.subtract))
        p.step("s", ["ln3"], ["ln4"], lambda e, b=eps: e.activation(
            out=st[:, 4:5], in_=st[:, 3:4], func=AF.Sqrt, bias=b, scale=-1.0))
        p.step("v", ["ln4"], ["ln4"], lambda e: e.reciprocal(out=st[:, 4:5], in_=st[:, 4:5]))
        p.step("v", ["ln4", "lnmu"], ["ln5"], lambda e: e.tensor_scalar(
            out=st[:, 5:6], in0=st[:, 2:3], scalar1=st[:, 4:5], scalar2=-1.0,
            op0=ALU.mult, op1=ALU.mult))
        dst_key = f"xc{i}" if want_tm else "ln_xc"
        dst = c["xc"][i] if want_tm else c["ln_xc"]
        if want_z:
            p.step("v", [X, "ln4", "ln5"], [f"zt{i}"], lambda e, xt=xt, i=i: e.tensor_scalar(
                out=c["zt"][i][:], in0=xt[:], scalar1=st[:, 4:5], scalar2=st[:, 5:6],
                op0=ALU.mult, op1=ALU.add))
            p.step("v", [f"zt{i}"], [dst_key], lambda e, i=i, dst=dst: e.tensor_copy(
                out=dst[:], in_=c["zt"][i][:]))
        else:
            p.step("v", [X, "ln4", "ln5"], [dst_key], lambda e, xt=xt, dst=dst: e.tensor_scalar(
                out=dst[:], in0=xt[:], scalar1=st[:, 4:5], scalar2=st[:, 5:6],
                op0=ALU.mult, op1=ALU.add))
        if store_dram is not None:
            p.dma("q", ["xld"], [dst_key], lambda e, i=i, dst=dst: e.dma_start(
                out=store_dram[i * 128:(i + 1) * 128, :], in_=dst[:]), tag=f"xst{i}")
        for ch in range(8):
            p.step("t", [dst_key, "ident_bf"], ["ps_T"], lambda e, dst=dst, ch=ch: e.transpose(
                out=c["ps_T"][:, 0:128], in_=dst[:, ch * 128:(ch + 1) * 128],
                identity=c["ident_bf"][:]))
            p.step("v", ["ps_T"], ["x_fm"], lambda e, i=i, ch=ch: e.tensor_copy(
                out=c["x_fm"][:, ch * T + i * 128: ch * T + (i + 1) * 128],
                in_=c["ps_T"][:, 0:128]))


def residual_add_fm(p, c, ps_o, ps_o_key, o_sb, o_sb_key, tau, scale=1.0):
    p.step("v", [ps_o_key], [o_sb_key], lambda e: e.tensor_copy(out=o_sb[:], in_=ps_o[:]))
    for m2 in range(8):
        for half in range(2):
            i = tau * 2 + half
            p.step("t", [o_sb_key, "ident_bf"], ["ps_T"], lambda e, m2=m2, half=half: e.transpose(
                out=c["ps_T"][:, 0:128], in_=o_sb[:, m2 * 256 + half * 128: m2 * 256 + (half + 1) * 128],
                identity=c["ident_bf"][:]))
            p.step("v", ["ps_T", f"xt{i}"], [f"xt{i}"],
                   lambda e, i=i, m2=m2, sc=scale: e.scalar_tensor_tensor(
                       out=c["xt"][i][:, m2 * 128:(m2 + 1) * 128], in0=c["ps_T"][:, 0:128],
                       scalar=sc, in1=c["xt"][i][:, m2 * 128:(m2 + 1) * 128],
                       op0=ALU.mult, op1=ALU.add))


def l0_ffn(p, c, wh, uid):
    T, nt = c["T"], c["nt"]
    NTAU = T // 256
    emit_ln_fm(p, c)
    x_fm, ident, ones = c["x_fm"], c["ident_bf"], c["ones_bf"]

    if "l0f_wg" not in c:
        c["l0f_wg"] = p.sb("l0f_wg", [128, 8 * 8], BF16)
        c["l0f_bg"] = p.sb("l0f_bg", [1, 8], BF16)
        c["l0f_b1"] = [p.sb(f"l0f_b1_{j}", [1, 1024], BF16) for j in range(2)]
        c["l0f_b2"] = p.sb("l0f_b2", [8, 1024], BF16)
        c["l0f_w1"] = [p.sb(f"l0f_w1_{j}", [128, 8 * 1024], BF16) for j in range(2)]
        c["l0f_w2"] = [p.sb(f"l0f_w2_{j}", [128, 8 * 1024], BF16) for j in range(2)]
        c["l0f_sc"] = p.sb("l0f_sc", [128, 8 * nt], F32)
        c["l0f_scw"] = p.sb("l0f_scw", [128, 8], F32)
        c["l0f_t8"] = p.sb("l0f_t8", [128, 8], F32)
        c["l0f_t8b"] = p.sb("l0f_t8b", [128, 8], F32)
        c["l0f_wtm"] = p.sb("l0f_wtm", [128, 8], BF16)
        c["l0f_w8"] = p.sb("l0f_w8", [8, 128], BF16)
        c["l0f_wfm"] = [p.sb(f"l0f_wfm_{j}", [1, T], BF16) for j in range(2)]
        c["l0f_wfm8"] = p.sb("l0f_wfm8", [8, T], BF16)
        c["l0f_g"] = [p.sb(f"l0f_g{m}", [128, 256], BF16) for m in range(8)]
        c["l0f_wb"] = p.sb("l0f_wb", [128, 256], BF16)
    wdram = p.dram(f"l0f_wdram_{uid}", [8, T], BF16)

    p.dma("g", ["l0f_wg"], ["IN"], lambda e: e.dma_start(
        out=c["l0f_wg"][:].rearrange("p (k e) -> p k e", e=8),
        in_=wh["W_gate"].rearrange("(k p) e -> p k e", p=128)), tag="l0wg")
    p.dma("g", ["l0f_bg"], ["IN"], lambda e: e.dma_start(
        out=c["l0f_bg"][:], in_=wh["b_gate"].unsqueeze(0)), tag="l0bg")
    p.dma("g", ["l0f_b2"], ["IN"], lambda e: e.dma_start(out=c["l0f_b2"][:], in_=wh["b2"]),
          tag="l0b2")

    sc, scw = c["l0f_sc"], c["l0f_scw"]
    for i in range(nt):
        def mm_sc(e, i=i):
            insts = []
            for k in range(8):
                insts.append(e.matmul(out=c["ps_a"][:, 0:8],
                                      lhsT=x_fm[:, k * T + i * 128: k * T + (i + 1) * 128],
                                      rhs=c["l0f_wg"][:, k * 8:(k + 1) * 8],
                                      start=(k == 0), stop=False))
            insts.append(e.matmul(out=c["ps_a"][:, 0:8], lhsT=ones[:, 0:128],
                                  rhs=c["l0f_bg"][:], start=False, stop=True))
            return insts
        p.step("t", ["x_fm", "l0f_wg", "l0f_bg"], ["ps_a"], mm_sc)
        p.step("v", ["ps_a"], ["l0f_sc"], lambda e, i=i: e.tensor_copy(
            out=sc[:, i * 8:(i + 1) * 8], in_=c["ps_a"][:, 0:8]))

    for i in range(nt):
        S = sc[:, i * 8:(i + 1) * 8]
        p.step("v", ["l0f_sc"], ["scw0"], lambda e, S=S: e.tensor_reduce(
            out=scw[:, 0:1], in_=S, axis=AX.X, op=ALU.max))
        p.step("v", ["l0f_sc", "scw0"], ["l0f_t8"], lambda e, S=S: e.tensor_scalar(
            out=c["l0f_t8"][:], in0=S, scalar1=scw[:, 0:1], scalar2=None, op0=ALU.is_ge))
        p.step("v", ["l0f_t8"], ["scw1"], lambda e: e.tensor_reduce(
            out=scw[:, 1:2], in_=c["l0f_t8"][:], axis=AX.X, op=ALU.add))
        p.step("v", ["l0f_t8", "l0f_sc"], ["l0f_t8b"], lambda e, S=S: e.scalar_tensor_tensor(
            out=c["l0f_t8b"][:], in0=c["l0f_t8"][:], scalar=-1e30, in1=S,
            op0=ALU.mult, op1=ALU.add))
        p.step("v", ["l0f_t8b"], ["scw3"], lambda e: e.tensor_reduce(
            out=scw[:, 3:4], in_=c["l0f_t8b"][:], axis=AX.X, op=ALU.max))
        p.step("v", ["scw1"], ["scw1"], lambda e: e.tensor_scalar(
            out=scw[:, 1:2], in0=scw[:, 1:2], scalar1=1.5, scalar2=None, op0=ALU.is_ge))
        p.step("v", ["scw0", "scw3"], ["scw4"], lambda e: e.tensor_scalar(
            out=scw[:, 4:5], in0=scw[:, 0:1], scalar1=scw[:, 3:4], scalar2=None,
            op0=ALU.subtract))
        p.step("v", ["scw4", "scw1", "scw3"], ["scw5"], lambda e: e.tensor_scalar(
            out=scw[:, 5:6], in0=scw[:, 4:5], scalar1=scw[:, 1:2], scalar2=scw[:, 3:4],
            op0=ALU.mult, op1=ALU.add))
        p.step("v", ["l0f_sc", "scw5"], ["l0f_t8"], lambda e, S=S: e.tensor_scalar(
            out=c["l0f_t8"][:], in0=S, scalar1=scw[:, 5:6], scalar2=None, op0=ALU.is_ge))
        p.step("v", ["l0f_t8", "l0f_sc"], ["l0f_wtm"], lambda e, S=S: e.scalar_tensor_tensor(
            out=c["l0f_wtm"][:], in0=S, scalar=0.5, in1=c["l0f_t8"][:],
            op0=ALU.mult, op1=ALU.mult))
        p.step("t", ["l0f_wtm", "ident_bf"], ["ps_T"], lambda e: e.transpose(
            out=c["ps_T"][0:8, 0:128], in_=c["l0f_wtm"][:], identity=ident[:]))
        p.step("v", ["ps_T"], ["l0f_w8"], lambda e: e.tensor_copy(
            out=c["l0f_w8"][:], in_=c["ps_T"][0:8, 0:128]))
        p.dma("q", [f"l0f_wd{i}"], ["l0f_w8"], lambda e, i=i, wdram=wdram: e.dma_start(
            out=wdram[:, i * 128:(i + 1) * 128], in_=c["l0f_w8"][:]))
    p.dma("q", ["l0f_wfm8"], [f"l0f_wd{i}" for i in range(nt)],
          lambda e, wdram=wdram: e.dma_start(out=c["l0f_wfm8"][:], in_=wdram))

    for e_i in range(8):
        j = e_i % 2
        W1K, W2K = f"l0f_w1_{j}", f"l0f_w2_{j}"
        p.dma("g", [W1K], ["IN"], lambda e, e_i=e_i, j=j: e.dma_start(
            out=c["l0f_w1"][j][:].rearrange("p (k h) -> p k h", k=8),
            in_=wh["W1"][e_i].rearrange("(k p) h -> p k h", p=128)), tag=f"l0w1e{e_i}")
        p.dma("g", [W2K], ["IN"], lambda e, e_i=e_i, j=j: e.dma_start(
            out=c["l0f_w2"][j][:].rearrange("p (m d) -> p m d", m=8),
            in_=wh["W2"][e_i].rearrange("(m p) d -> p m d", p=128)), tag=f"l0w2e{e_i}")
        B1K, WFMK = f"l0f_b1_{j}", f"l0f_wfm_{j}"
        p.dma("g", [B1K], ["IN"], lambda e, e_i=e_i, j=j: e.dma_start(
            out=c["l0f_b1"][j][:], in_=wh["b1"][e_i:e_i + 1, :]), tag=f"l0b1e{e_i}")
        p.dma("q", [WFMK], [f"l0f_wd{i}" for i in range(nt)],
              lambda e, e_i=e_i, j=j, wdram=wdram: e.dma_start(
                  out=c["l0f_wfm"][j][:], in_=wdram[e_i:e_i + 1, :]), tag=f"l0wfme{e_i}")
        for tau in range(NTAU):
            p.step("t", [WFMK, "ones_bf"], ["ps_a"], lambda e, j=j, tau=tau: e.matmul(
                out=c["ps_a"][:, 0:256], lhsT=ones[:, 0:128],
                rhs=c["l0f_wfm"][j][:, tau * 256:(tau + 1) * 256],
                start=True, stop=True))
            p.step("v", ["ps_a"], ["l0f_wb"], lambda e: e.tensor_copy(
                out=c["l0f_wb"][:], in_=c["ps_a"][:, 0:256]))
            for m in range(8):
                psh = c["ps_h1"] if m % 2 == 0 else c["ps_h2"]
                PSH = "ps_h1" if m % 2 == 0 else "ps_h2"
                def mm_w1(e, m=m, tau=tau, j=j, e_i=e_i, psh=psh):
                    insts = []
                    for k in range(8):
                        insts.append(e.matmul(
                            out=psh[:, 0:256],
                            lhsT=c["l0f_w1"][j][:, k * 1024 + m * 128: k * 1024 + (m + 1) * 128],
                            rhs=x_fm[:, k * T + tau * 256: k * T + (tau + 1) * 256],
                            start=(k == 0), stop=False))
                    insts.append(e.matmul(
                        out=psh[:, 0:256],
                        lhsT=c["l0f_b1"][j][:, m * 128:(m + 1) * 128],
                        rhs=ones[:, 0:256], start=False, stop=True))
                    return insts
                p.step("t", [W1K, "x_fm", B1K, "ones_bf"], [PSH], mm_w1)
                GK = f"l0f_g{m}"
                p.step("s", [PSH], [GK], lambda e, m=m, psh=psh: e.activation(
                    out=c["l0f_g"][m][:], in_=psh[:, 0:256], func=AF.Relu))
                p.step("v", [GK, "l0f_wb"], [GK], lambda e, m=m: e.tensor_tensor(
                    out=c["l0f_g"][m][:], in0=c["l0f_g"][m][:], in1=c["l0f_wb"][:], op=ALU.mult))
            for half in range(2):
                i = tau * 2 + half
                for db in range(2):
                    psy = c["ps_y1"] if db == 0 else c["ps_y2"]
                    PSY = "ps_y1" if db == 0 else "ps_y2"
                    def mm_w2(e, half=half, db=db, j=j, psy=psy):
                        insts = []
                        for m in range(8):
                            insts.append(e.matmul(
                                out=psy[:, 0:512],
                                lhsT=c["l0f_g"][m][:, half * 128:(half + 1) * 128],
                                rhs=c["l0f_w2"][j][:, m * 1024 + db * 512: m * 1024 + (db + 1) * 512],
                                start=(m == 0), stop=(m == 7)))
                        return insts
                    p.step("t", [W2K] + [f"l0f_g{m}" for m in range(8)], [PSY], mm_w2)
                    p.step("v", [PSY, f"xt{i}"], [f"xt{i}"], lambda e, i=i, db=db, psy=psy:
                           e.tensor_tensor(out=c["xt"][i][:, db * 512:(db + 1) * 512],
                                           in0=psy[:, 0:512],
                                           in1=c["xt"][i][:, db * 512:(db + 1) * 512], op=ALU.add))
    for i in range(nt):
        for db in range(2):
            psy = c["ps_y1"] if db == 0 else c["ps_y2"]
            PSY = "ps_y1" if db == 0 else "ps_y2"
            p.step("t", ["l0f_wfm8", "l0f_b2"], [PSY], lambda e, i=i, db=db, psy=psy: e.matmul(
                out=psy[:, 0:512], lhsT=c["l0f_wfm8"][:, i * 128:(i + 1) * 128],
                rhs=c["l0f_b2"][:, db * 512:(db + 1) * 512], start=True, stop=True))
            p.step("v", [PSY, f"xt{i}"], [f"xt{i}"], lambda e, i=i, db=db, psy=psy:
                   e.tensor_tensor(out=c["xt"][i][:, db * 512:(db + 1) * 512],
                                   in0=psy[:, 0:512],
                                   in1=c["xt"][i][:, db * 512:(db + 1) * 512], op=ALU.add))


def _in_maps_tokenshard(p, x_full, weights_by_name):
    consts = common_const_inputs(p)
    maps = []
    for core in range(N_CORES):
        m = {"x_in": np.ascontiguousarray(x_full[core * T_CORE:(core + 1) * T_CORE])}
        m.update(weights_by_name)
        m.update(consts)
        maps.append(m)
    return maps


def run_l0_ffn(x_full, Wg, bg, W1, b1, W2, b2):
    nc = make_nc()
    p = Prog(nc)
    c = setup_common(p)
    alloc_x(p, c)
    xh = p.dram("x_in", [T_CORE, D], F32, kind="ExternalInput")
    oh = p.dram("x_out", [T_CORE, D], F32, kind="ExternalOutput")
    wh = {
        "W_gate": p.dram("w_Wg", [D, 8], F32, kind="ExternalInput"),
        "b_gate": p.dram("w_bg", [8], F32, kind="ExternalInput"),
        "W1": p.dram("w_W1", [8, D, D], F32, kind="ExternalInput"),
        "b1": p.dram("w_b1", [8, D], F32, kind="ExternalInput"),
        "W2": p.dram("w_W2", [8, D, D], F32, kind="ExternalInput"),
        "b2": p.dram("w_b2", [8, D], F32, kind="ExternalInput"),
    }
    load_x(p, c, xh)
    l0_ffn(p, c, wh, uid="u")
    store_x(p, c, oh)
    p.build()
    wmap = {"w_Wg": Wg, "w_bg": bg, "w_W1": W1, "w_b1": b1, "w_W2": W2, "w_b2": b2}
    maps = _in_maps_tokenshard(p, x_full, wmap)
    res = run_bass_kernel_spmd(nc, maps, core_ids=list(range(N_CORES)))
    return np.concatenate([r["x_out"] for r in res.results], axis=0)


def kernel(**inputs):
    raise NotImplementedError("under construction")


def l2_ffn(p, c, wh, uid):
    T, nt = c["T"], c["nt"]
    NTAU = T // 256
    emit_ln_fm(p, c)
    x_fm, ident, ones = c["x_fm"], c["ident_bf"], c["ones_bf"]

    if "l2f_gwn" not in c:
        c["l2f_gwn"] = p.sb("l2f_gwn", [16, 1024], BF16)
        c["l2f_gw"] = p.sb("l2f_gw", [128, 8 * 16], BF16)
        c["l2f_ber"] = p.sb("l2f_ber", [1, 16], BF16)
        c["l2f_be"] = p.sb("l2f_be", [128, 16], F32)
        c["l2f_negidx"] = p.sb("l2f_negidx", [128, 16], F32)
        c["l2f_sc"] = p.sb("l2f_sc", [128, 16 * nt], F32)
        c["l2f_A"] = p.sb("l2f_A", [128, 16], F32)
        c["l2f_B"] = p.sb("l2f_B", [128, 16], F32)
        c["l2f_M"] = p.sb("l2f_M", [128, 16], F32)
        c["l2f_W"] = p.sb("l2f_W", [128, 16], F32)
        c["l2f_G1"] = p.sb("l2f_G1", [128, 4], F32)
        c["l2f_G2"] = p.sb("l2f_G2", [128, 4], F32)
        c["l2f_C4"] = p.sb("l2f_C4", [128, 4], F32)
        c["l2f_D4"] = p.sb("l2f_D4", [128, 4], F32)
        c["l2f_GS"] = p.sb("l2f_GS", [128, 4], F32)
        c["l2f_E4"] = p.sb("l2f_E4", [128, 4], F32)
        c["l2f_scw"] = p.sb("l2f_scw", [128, 8], F32)
        c["l2f_wtm"] = p.sb("l2f_wtm", [128, 16], BF16)
        c["l2f_w16"] = p.sb("l2f_w16", [16, 128], BF16)
        c["l2f_up"] = [p.sb(f"l2f_up{j}", [128, 2 * 8 * 512], BF16) for j in range(2)]
        c["l2f_dn"] = [p.sb(f"l2f_dn{j}", [128, 2 * 4 * 1024], BF16) for j in range(2)]
        c["l2f_wfm"] = [p.sb(f"l2f_wfm{j}", [1, 2 * T], BF16) for j in range(2)]
        c["l2f_g"] = [p.sb(f"l2f_g{m}", [128, 256], BF16) for m in range(8)]
        c["l2f_wb"] = p.sb("l2f_wb", [128, 2 * 256], BF16)
    wdram = p.dram(f"l2f_wdram_{uid}", [16, T], BF16)
    sc, scw = c["l2f_sc"], c["l2f_scw"]
    A, B, M, W = c["l2f_A"], c["l2f_B"], c["l2f_M"], c["l2f_W"]
    G1, G2, C4, D4, GS, E4 = (c["l2f_G1"], c["l2f_G2"], c["l2f_C4"],
                              c["l2f_D4"], c["l2f_GS"], c["l2f_E4"])

    p.dma("g", ["l2f_gwn"], ["IN"], lambda e: e.dma_start(
        out=c["l2f_gwn"][:], in_=wh["gate_w16"]), tag="l2gw")
    p.dma("g", ["l2f_ber"], ["IN"], lambda e: e.dma_start(
        out=c["l2f_ber"][:], in_=wh["bias_e"].unsqueeze(0)), tag="l2be")
    if "c_negidx16" not in p.dram_tensors:
        nix = p.dram("c_negidx16", [128, 16], F32, kind="ExternalInput")
    else:
        nix = p.dram_tensors["c_negidx16"].ap()
    p.dma("q", ["l2f_negidx"], ["IN"], lambda e: e.dma_start(out=c["l2f_negidx"][:], in_=nix))
    for k in range(8):
        p.step("t", ["l2f_gwn", "ident_bf"], ["ps_T"], lambda e, k=k: e.transpose(
            out=c["ps_T"][0:128, 0:16], in_=c["l2f_gwn"][0:16, k * 128:(k + 1) * 128],
            identity=c["ident_bf"][0:16, 0:16]))
        p.step("v", ["ps_T"], ["l2f_gw"], lambda e, k=k: e.tensor_copy(
            out=c["l2f_gw"][:, k * 16:(k + 1) * 16], in_=c["ps_T"][0:128, 0:16]))
    p.step("t", ["l2f_ber", "ones_bf"], ["ps_a"], lambda e: e.matmul(
        out=c["ps_a"][:, 0:16], lhsT=ones[:, 0:128], rhs=c["l2f_ber"][:],
        start=True, stop=True))
    p.step("v", ["ps_a"], ["l2f_be"], lambda e: e.tensor_copy(
        out=c["l2f_be"][:], in_=c["ps_a"][:, 0:16]))

    def rview(t):
        return t[:].rearrange("p (g r) -> p g r", r=4)
    for i in range(nt):
        S = sc[:, i * 16:(i + 1) * 16]
        def mm_sc(e, i=i):
            return [e.matmul(out=c["ps_a"][:, 0:16],
                             lhsT=x_fm[:, k * T + i * 128: k * T + (i + 1) * 128],
                             rhs=c["l2f_gw"][:, k * 16:(k + 1) * 16],
                             start=(k == 0), stop=(k == 7)) for k in range(8)]
        p.step("t", ["x_fm", "l2f_gw"], ["ps_a"], mm_sc)
        p.step("s", ["ps_a"], ["l2f_sc"], lambda e, S=S: e.activation(
            out=S, in_=c["ps_a"][:, 0:16], func=AF.Sigmoid))
        p.step("v", ["l2f_sc", "l2f_be"], ["l2f_A"], lambda e, S=S: e.tensor_tensor(
            out=A[:], in0=S, in1=c["l2f_be"][:], op=ALU.add))
        p.step("v", ["l2f_A"], ["l2f_G1"], lambda e: e.tensor_reduce(
            out=G1[:], in_=rview(A), axis=AX.X, op=ALU.max))
        p.step("v", ["l2f_A", "l2f_G1"], ["l2f_B"], lambda e: e.tensor_tensor(
            out=rview(B), in0=rview(A), in1=G1[:].unsqueeze(2).broadcast_to([128, 4, 4]),
            op=ALU.is_ge))
        p.step("v", ["l2f_B"], ["l2f_C4"], lambda e: e.tensor_reduce(
            out=C4[:], in_=rview(B), axis=AX.X, op=ALU.add))
        p.step("v", ["l2f_B", "l2f_A"], ["l2f_B"], lambda e: e.scalar_tensor_tensor(
            out=B[:], in0=B[:], scalar=-1e30, in1=A[:], op0=ALU.mult, op1=ALU.add))
        p.step("v", ["l2f_B"], ["l2f_G2"], lambda e: e.tensor_reduce(
            out=G2[:], in_=rview(B), axis=AX.X, op=ALU.max))
        p.step("v", ["l2f_C4"], ["l2f_C4"], lambda e: e.tensor_single_scalar(
            out=C4[:], in_=C4[:], scalar=1.5, op=ALU.is_ge))
        p.step("v", ["l2f_G1", "l2f_G2"], ["l2f_D4"], lambda e: e.tensor_tensor(
            out=D4[:], in0=G1[:], in1=G2[:], op=ALU.subtract))
        p.step("v", ["l2f_D4", "l2f_C4"], ["l2f_D4"], lambda e: e.tensor_tensor(
            out=D4[:], in0=D4[:], in1=C4[:], op=ALU.mult))
        p.step("v", ["l2f_G2", "l2f_D4"], ["l2f_G2"], lambda e: e.tensor_tensor(
            out=G2[:], in0=G2[:], in1=D4[:], op=ALU.add))
        p.step("v", ["l2f_G1", "l2f_G2"], ["l2f_GS"], lambda e: e.tensor_tensor(
            out=GS[:], in0=G1[:], in1=G2[:], op=ALU.add))
        p.step("v", ["l2f_GS"], ["scw0"], lambda e: e.tensor_reduce(
            out=scw[:, 0:1], in_=GS[:], axis=AX.X, op=ALU.max))
        p.step("v", ["l2f_GS", "scw0"], ["l2f_E4"], lambda e: e.tensor_scalar(
            out=E4[:], in0=GS[:], scalar1=scw[:, 0:1], scalar2=None, op0=ALU.is_ge))
        p.step("v", ["l2f_E4"], ["scw1"], lambda e: e.tensor_reduce(
            out=scw[:, 1:2], in_=E4[:], axis=AX.X, op=ALU.add))
        p.step("v", ["l2f_E4", "l2f_GS"], ["l2f_D4"], lambda e: e.scalar_tensor_tensor(
            out=D4[:], in0=E4[:], scalar=-1e30, in1=GS[:], op0=ALU.mult, op1=ALU.add))
        p.step("v", ["l2f_D4"], ["scw3"], lambda e: e.tensor_reduce(
            out=scw[:, 3:4], in_=D4[:], axis=AX.X, op=ALU.max))
        p.step("v", ["scw1"], ["scw1"], lambda e: e.tensor_single_scalar(
            out=scw[:, 1:2], in_=scw[:, 1:2], scalar=1.5, op=ALU.is_ge))
        p.step("v", ["scw0", "scw3"], ["scw4"], lambda e: e.tensor_scalar(
            out=scw[:, 4:5], in0=scw[:, 0:1], scalar1=scw[:, 3:4], scalar2=None,
            op0=ALU.subtract))
        p.step("v", ["scw4", "scw1", "scw3"], ["scw5"], lambda e: e.tensor_scalar(
            out=scw[:, 5:6], in0=scw[:, 4:5], scalar1=scw[:, 1:2], scalar2=scw[:, 3:4],
            op0=ALU.mult, op1=ALU.add))
        p.step("v", ["l2f_GS", "scw5"], ["l2f_E4"], lambda e: e.tensor_scalar(
            out=E4[:], in0=GS[:], scalar1=scw[:, 5:6], scalar2=None, op0=ALU.is_ge))
        p.step("v", ["l2f_A", "l2f_E4"], ["l2f_M"], lambda e: e.tensor_tensor(
            out=rview(M), in0=rview(A), in1=E4[:].unsqueeze(2).broadcast_to([128, 4, 4]),
            op=ALU.mult))
        p.step("v", ["l2f_M"], ["l2f_B"], lambda e: e.tensor_single_scalar(
            out=B[:], in_=M[:], scalar=0.0, op=ALU.is_equal))
        p.step("v", ["l2f_B", "l2f_negidx"], ["l2f_B"], lambda e: e.tensor_tensor(
            out=B[:], in0=B[:], in1=c["l2f_negidx"][:], op=ALU.mult))
        p.step("v", ["l2f_M", "l2f_B"], ["l2f_M"], lambda e: e.tensor_tensor(
            out=M[:], in0=M[:], in1=B[:], op=ALU.add))
        p.step("v", ["l2f_M"], ["l2f_W"], lambda e: e.tensor_copy(out=W[:], in_=M[:]))
        for r in range(4):
            p.step("v", ["l2f_W"], ["scw6"], lambda e: e.tensor_reduce(
                out=scw[:, 6:7], in_=W[:], axis=AX.X, op=ALU.max))
            if r < 3:
                p.step("v", ["l2f_W", "scw6"], ["l2f_B"], lambda e: e.tensor_scalar(
                    out=B[:], in0=W[:], scalar1=scw[:, 6:7], scalar2=None, op0=ALU.is_ge))
                p.step("v", ["l2f_B", "l2f_W"], ["l2f_W"], lambda e: e.scalar_tensor_tensor(
                    out=W[:], in0=B[:], scalar=-1e30, in1=W[:], op0=ALU.mult, op1=ALU.add))
        p.step("v", ["l2f_M", "scw6"], ["l2f_B"], lambda e: e.tensor_scalar(
            out=B[:], in0=M[:], scalar1=scw[:, 6:7], scalar2=None, op0=ALU.is_ge))
        p.step("v", ["l2f_B", "l2f_sc"], ["l2f_B"], lambda e, S=S: e.tensor_tensor(
            out=B[:], in0=B[:], in1=S, op=ALU.mult))
        p.step("v", ["l2f_B"], ["scw7"], lambda e: e.tensor_reduce(
            out=scw[:, 7:8], in_=B[:], axis=AX.X, op=ALU.add))
        p.step("v", ["scw7"], ["scw7"], lambda e: e.tensor_single_scalar(
            out=scw[:, 7:8], in_=scw[:, 7:8], scalar=1e-20, op=ALU.add))
        p.step("v", ["scw7"], ["scw7"], lambda e: e.reciprocal(
            out=scw[:, 7:8], in_=scw[:, 7:8]))
        p.step("v", ["l2f_B", "scw7"], ["l2f_wtm"], lambda e: e.tensor_scalar(
            out=c["l2f_wtm"][:], in0=B[:], scalar1=scw[:, 7:8], scalar2=1.25,
            op0=ALU.mult, op1=ALU.mult))
        p.step("t", ["l2f_wtm", "ident_bf"], ["ps_T"], lambda e: e.transpose(
            out=c["ps_T"][0:16, 0:128], in_=c["l2f_wtm"][:], identity=ident[:]))
        p.step("v", ["ps_T"], ["l2f_w16"], lambda e: e.tensor_copy(
            out=c["l2f_w16"][:], in_=c["ps_T"][0:16, 0:128]))
        p.dma("q", [f"l2f_wd{i}"], ["l2f_w16"], lambda e, i=i, wdram=wdram: e.dma_start(
            out=wdram[:, i * 128:(i + 1) * 128], in_=c["l2f_w16"][:]))

    for pe in range(8):
        j = pe % 2
        UPK, DNK, WFK = f"l2f_up{j}", f"l2f_dn{j}", f"l2f_wfm{j}"
        p.dma("g", [UPK], ["IN"], lambda e, pe=pe, j=j: e.dma_start(
            out=c["l2f_up"][j][:].rearrange("p (x k h) -> p x k h", x=2, k=8),
            in_=wh["upT"][2 * pe:2 * pe + 2].rearrange("x (k p) h -> p x k h", p=128)),
            tag=f"l2up{pe}")
        p.dma("g", [DNK], ["IN"], lambda e, pe=pe, j=j: e.dma_start(
            out=c["l2f_dn"][j][:].rearrange("p (x k d) -> p x k d", x=2, k=4),
            in_=wh["downT"][2 * pe:2 * pe + 2].rearrange("x (k p) d -> p x k d", p=128)),
            tag=f"l2dn{pe}")
        p.dma("q", [WFK], [f"l2f_wd{i}" for i in range(nt)],
              lambda e, pe=pe, j=j, wdram=wdram: e.dma_start(
                  out=c["l2f_wfm"][j][:].rearrange("o (x t) -> o x t", x=2),
                  in_=wdram[2 * pe:2 * pe + 2].unsqueeze(0)), tag=f"l2wf{pe}")
        for tau in range(NTAU):
            def mm_wb(e, j=j, tau=tau):
                return [e.matmul(out=c["ps_a"][:, x * 256:(x + 1) * 256], lhsT=ones[:, 0:128],
                                 rhs=c["l2f_wfm"][j][:, x * T + tau * 256: x * T + (tau + 1) * 256],
                                 start=True, stop=True) for x in range(2)]
            p.step("t", [WFK, "ones_bf"], ["ps_a"], mm_wb)
            p.step("v", ["ps_a"], ["l2f_wb"], lambda e: e.tensor_copy(
                out=c["l2f_wb"][:], in_=c["ps_a"][:, 0:512]))
            for x in range(2):
                for m in range(4):
                    psh = c["ps_h1"] if m % 2 == 0 else c["ps_h2"]
                    PSH = "ps_h1" if m % 2 == 0 else "ps_h2"
                    def mm_up(e, x=x, m=m, tau=tau, j=j, psh=psh):
                        return [e.matmul(
                            out=psh[:, 0:256],
                            lhsT=c["l2f_up"][j][:, (x * 8 + k) * 512 + m * 128:
                                               (x * 8 + k) * 512 + (m + 1) * 128],
                            rhs=x_fm[:, k * T + tau * 256: k * T + (tau + 1) * 256],
                            start=(k == 0), stop=(k == 7)) for k in range(8)]
                    p.step("t", [UPK, "x_fm"], [PSH], mm_up)
                    GK = f"l2f_g{x * 4 + m}"
                    p.step("s", [PSH], [GK], lambda e, x=x, m=m, psh=psh: e.activation(
                        out=c["l2f_g"][x * 4 + m][:], in_=psh[:, 0:256], func=AF.Relu))
                    p.step("v", [GK], [GK], lambda e, x=x, m=m: e.tensor_tensor(
                        out=c["l2f_g"][x * 4 + m][:], in0=c["l2f_g"][x * 4 + m][:],
                        in1=c["l2f_g"][x * 4 + m][:], op=ALU.mult))
                    p.step("v", [GK, "l2f_wb"], [GK], lambda e, x=x, m=m: e.tensor_tensor(
                        out=c["l2f_g"][x * 4 + m][:], in0=c["l2f_g"][x * 4 + m][:],
                        in1=c["l2f_wb"][:, x * 256:(x + 1) * 256], op=ALU.mult))
            for half in range(2):
                i = tau * 2 + half
                for db in range(2):
                    psy = c["ps_y1"] if db == 0 else c["ps_y2"]
                    PSY = "ps_y1" if db == 0 else "ps_y2"
                    def mm_dn(e, half=half, db=db, j=j, psy=psy):
                        insts = []
                        for x in range(2):
                            for m in range(4):
                                insts.append(e.matmul(
                                    out=psy[:, 0:512],
                                    lhsT=c["l2f_g"][x * 4 + m][:, half * 128:(half + 1) * 128],
                                    rhs=c["l2f_dn"][j][:, (x * 4 + m) * 1024 + db * 512:
                                                       (x * 4 + m) * 1024 + (db + 1) * 512],
                                    start=(x == 0 and m == 0), stop=(x == 1 and m == 3)))
                        return insts
                    p.step("t", [DNK] + [f"l2f_g{mm}" for mm in range(8)], [PSY], mm_dn)
                    p.step("v", [PSY, f"xt{i}"], [f"xt{i}"], lambda e, i=i, db=db, psy=psy:
                           e.tensor_tensor(out=c["xt"][i][:, db * 512:(db + 1) * 512],
                                           in0=psy[:, 0:512],
                                           in1=c["xt"][i][:, db * 512:(db + 1) * 512],
                                           op=ALU.add))

    for hs in range(4):
        j = hs % 2
        SUK, SDK = f"l2f_up{j}", f"l2f_dn{j}"
        p.dma("g", [SUK], ["IN"], lambda e, hs=hs, j=j: e.dma_start(
            out=c["l2f_up"][j][:, 0:8 * 512].rearrange("p (k h) -> p k h", k=8),
            in_=wh["supT"][:, hs * 512:(hs + 1) * 512].rearrange("(k p) h -> p k h", p=128)),
            tag=f"l2su{hs}")
        p.dma("g", [SDK], ["IN"], lambda e, hs=hs, j=j: e.dma_start(
            out=c["l2f_dn"][j][:, 0:4 * 1024].rearrange("p (k d) -> p k d", k=4),
            in_=wh["sdnT"][hs * 512:(hs + 1) * 512].rearrange("(k p) d -> p k d", p=128)),
            tag=f"l2sd{hs}")
        for tau in range(NTAU):
            for m in range(4):
                psh = c["ps_h1"] if m % 2 == 0 else c["ps_h2"]
                PSH = "ps_h1" if m % 2 == 0 else "ps_h2"
                def mm_sup(e, m=m, tau=tau, j=j, psh=psh):
                    return [e.matmul(
                        out=psh[:, 0:256],
                        lhsT=c["l2f_up"][j][:, k * 512 + m * 128: k * 512 + (m + 1) * 128],
                        rhs=x_fm[:, k * T + tau * 256: k * T + (tau + 1) * 256],
                        start=(k == 0), stop=(k == 7)) for k in range(8)]
                p.step("t", [SUK, "x_fm"], [PSH], mm_sup)
                GK = f"l2f_g{m}"
                p.step("s", [PSH], [GK], lambda e, m=m, psh=psh: e.activation(
                    out=c["l2f_g"][m][:], in_=psh[:, 0:256], func=AF.Relu))
                p.step("v", [GK], [GK], lambda e, m=m: e.scalar_tensor_tensor(
                    out=c["l2f_g"][m][:], in0=c["l2f_g"][m][:], scalar=0.5,
                    in1=c["l2f_g"][m][:], op0=ALU.mult, op1=ALU.mult))
            for half in range(2):
                i = tau * 2 + half
                for db in range(2):
                    psy = c["ps_y1"] if db == 0 else c["ps_y2"]
                    PSY = "ps_y1" if db == 0 else "ps_y2"
                    def mm_sdn(e, half=half, db=db, j=j, psy=psy):
                        return [e.matmul(
                            out=psy[:, 0:512],
                            lhsT=c["l2f_g"][m][:, half * 128:(half + 1) * 128],
                            rhs=c["l2f_dn"][j][:, m * 1024 + db * 512: m * 1024 + (db + 1) * 512],
                            start=(m == 0), stop=(m == 3)) for m in range(4)]
                    p.step("t", [SDK] + [f"l2f_g{mm}" for mm in range(4)], [PSY], mm_sdn)
                    p.step("v", [PSY, f"xt{i}"], [f"xt{i}"], lambda e, i=i, db=db, psy=psy:
                           e.tensor_tensor(out=c["xt"][i][:, db * 512:(db + 1) * 512],
                                           in0=psy[:, 0:512],
                                           in1=c["xt"][i][:, db * 512:(db + 1) * 512],
                                           op=ALU.add))


def run_l2_ffn(x_full, gate_w, bias_e, up_w, down_w, shared_up, shared_down):
    nc = make_nc()
    p = Prog(nc)
    c = setup_common(p)
    alloc_x(p, c)
    xh = p.dram("x_in", [T_CORE, D], F32, kind="ExternalInput")
    oh = p.dram("x_out", [T_CORE, D], F32, kind="ExternalOutput")
    wh = {
        "gate_w16": p.dram("w_gw", [16, D], F32, kind="ExternalInput"),
        "bias_e": p.dram("w_be", [16], F32, kind="ExternalInput"),
        "upT": p.dram("w_upT", [16, D, 512], F32, kind="ExternalInput"),
        "downT": p.dram("w_dnT", [16, 512, D], F32, kind="ExternalInput"),
        "supT": p.dram("w_supT", [D, 2048], F32, kind="ExternalInput"),
        "sdnT": p.dram("w_sdnT", [2048, D], F32, kind="ExternalInput"),
    }
    load_x(p, c, xh)
    l2_ffn(p, c, wh, uid="u")
    store_x(p, c, oh)
    p.build()
    wmap = {
        "w_gw": gate_w, "w_be": bias_e,
        "w_upT": np.ascontiguousarray(up_w.transpose(0, 2, 1)),
        "w_dnT": np.ascontiguousarray(down_w.transpose(0, 2, 1)),
        "w_supT": np.ascontiguousarray(shared_up.T),
        "w_sdnT": np.ascontiguousarray(shared_down.T),
        "c_negidx16": NEGIDX16,
    }
    maps = _in_maps_tokenshard(p, x_full, wmap)
    res = run_bass_kernel_spmd(nc, maps, core_ids=list(range(N_CORES)))
    return np.concatenate([r["x_out"] for r in res.results], axis=0)


NEGIDX16 = np.tile((-1e-33 * (np.arange(16, dtype=np.float32) + 1))[None, :], (128, 1))


QC = float(3 * 2 ** 22)


def alloc_quant(p, c, W=512):
    c["q_A"] = p.sb("q_A", [128, W], F32)
    c["q_T1"] = p.sb("q_T1", [128, W], F32)
    c["q_T2"] = p.sb("q_T2", [128, W], F32)
    c["q_T3"] = p.sb("q_T3", [128, W], F32)
    c["q_SG"] = p.sb("q_SG", [128, W], F32)
    c["q_R"] = p.sb("q_R", [128, W // 16], F32)
    c["q_RS"] = p.sb("q_RS", [128, W // 16], F32)


def emit_quant(p, c, src, src_key, dst, dst_key, W):
    nb = W // 16
    A, T1, T2, T3 = c["q_A"], c["q_T1"], c["q_T2"], c["q_T3"]
    SG, R, RS = c["q_SG"], c["q_R"], c["q_RS"]
    b3 = lambda t, w=W: t[:, 0:w].rearrange("p (b k) -> p b k", k=16)
    p.step("s", [src_key], ["q_A"], lambda e: e.activation(
        out=A[:, 0:W], in_=src, func=AF.Abs))
    p.step("v", ["q_A"], ["q_R"], lambda e: e.tensor_reduce(
        out=R[:, 0:nb], in_=b3(A), axis=AX.X, op=ALU.max))
    p.step("v", ["q_R"], ["q_R"], lambda e: e.tensor_scalar(
        out=R[:, 0:nb], in0=R[:, 0:nb], scalar1=1.0 / 6.0, scalar2=1e-12,
        op0=ALU.mult, op1=ALU.max))
    p.step("v", ["q_R"], ["q_RS"], lambda e: e.reciprocal(out=RS[:, 0:nb], in_=R[:, 0:nb]))
    p.step("v", ["q_A", "q_RS"], ["q_A"], lambda e: e.tensor_tensor(
        out=b3(A), in0=b3(A), in1=RS[:, 0:nb].unsqueeze(2).broadcast_to([128, nb, 16]),
        op=ALU.mult))
    p.step("v", ["q_A"], ["q_T1"], lambda e: e.tensor_scalar(
        out=T1[:, 0:W], in0=A[:, 0:W], scalar1=2.0, scalar2=4.0, op0=ALU.mult, op1=ALU.min))
    p.step("v", ["q_T1"], ["q_T1"], lambda e: e.tensor_single_scalar(
        out=T1[:, 0:W], in_=T1[:, 0:W], scalar=QC, op=ALU.add))
    p.step("v", ["q_T1"], ["q_T1"], lambda e: e.tensor_scalar(
        out=T1[:, 0:W], in0=T1[:, 0:W], scalar1=QC, scalar2=0.5, op0=ALU.subtract, op1=ALU.mult))
    p.step("v", ["q_A"], ["q_T2"], lambda e: e.tensor_scalar(
        out=T2[:, 0:W], in0=A[:, 0:W], scalar1=2.0, scalar2=2.0, op0=ALU.subtract, op1=ALU.min))
    p.step("v", ["q_T2"], ["q_T2"], lambda e: e.tensor_scalar(
        out=T2[:, 0:W], in0=T2[:, 0:W], scalar1=0.0, scalar2=QC, op0=ALU.max, op1=ALU.add))
    p.step("v", ["q_T2"], ["q_T2"], lambda e: e.tensor_single_scalar(
        out=T2[:, 0:W], in_=T2[:, 0:W], scalar=QC, op=ALU.subtract))
    p.step("v", ["q_A"], ["q_T3"], lambda e: e.tensor_scalar(
        out=T3[:, 0:W], in0=A[:, 0:W], scalar1=0.5, scalar2=2.0, op0=ALU.mult, op1=ALU.subtract))
    p.step("v", ["q_T3"], ["q_T3"], lambda e: e.tensor_scalar(
        out=T3[:, 0:W], in0=T3[:, 0:W], scalar1=1.0, scalar2=0.0, op0=ALU.min, op1=ALU.max))
    p.step("v", ["q_T3"], ["q_T3"], lambda e: e.tensor_single_scalar(
        out=T3[:, 0:W], in_=T3[:, 0:W], scalar=QC, op=ALU.add))
    p.step("v", ["q_T3"], ["q_T3"], lambda e: e.tensor_scalar(
        out=T3[:, 0:W], in0=T3[:, 0:W], scalar1=QC, scalar2=2.0, op0=ALU.subtract, op1=ALU.mult))
    p.step("v", ["q_T1", "q_T2"], ["q_T1"], lambda e: e.tensor_tensor(
        out=T1[:, 0:W], in0=T1[:, 0:W], in1=T2[:, 0:W], op=ALU.add))
    p.step("v", ["q_T3", "q_T1"], ["q_T1"], lambda e: e.tensor_tensor(
        out=T1[:, 0:W], in0=T1[:, 0:W], in1=T3[:, 0:W], op=ALU.add))
    p.step("s", [src_key], ["q_SG"], lambda e: e.activation(
        out=SG[:, 0:W], in_=src, func=AF.Sign))
    p.step("v", ["q_SG", "q_R"], ["q_SG"], lambda e: e.tensor_tensor(
        out=b3(SG), in0=b3(SG), in1=R[:, 0:nb].unsqueeze(2).broadcast_to([128, nb, 16]),
        op=ALU.mult))
    p.step("v", ["q_T1", "q_SG"], [dst_key], lambda e: e.tensor_tensor(
        out=dst, in0=T1[:, 0:W], in1=SG[:, 0:W], op=ALU.mult))


def run_l3_quant_weights(w_gate, w_up, w_down):
    nc = make_nc()
    p = Prog(nc)
    c = {}
    gu_h = p.dram("gu_in", [704, 1024], F32, kind="ExternalInput")
    d_h = p.dram("d_in", [128, 2816], F32, kind="ExternalInput")
    guo = p.dram("gu_out", [704, 1024], BF16, kind="ExternalOutput")
    do = p.dram("d_out", [128, 2816], BF16, kind="ExternalOutput")
    alloc_quant(p, c, W=512)
    src = p.sb("src", [128, 1024], F32)
    dst = p.sb("dst", [128, 1024], BF16)
    srcd = p.sb("srcd", [128, 2816], F32)
    dstd = p.sb("dstd", [128, 2816], BF16)
    for i in range(6):
        r0, r1 = i * 128, min((i + 1) * 128, 704)
        nr = r1 - r0
        p.dma("q", ["src"], ["IN"], lambda e, r0=r0, nr=nr: e.dma_start(
            out=src[0:nr, :], in_=gu_h[r0:r0 + nr, :]))
        for sl in range(2):
            emit_quant(p, c, src[:, sl * 512:(sl + 1) * 512], "src",
                       dst[:, sl * 512:(sl + 1) * 512], "dst", 512)
        p.dma("q", ["OUT"], ["dst"], lambda e, r0=r0, nr=nr: e.dma_start(
            out=guo[r0:r0 + nr, :], in_=dst[0:nr, :]))
    p.dma("q", ["srcd"], ["IN"], lambda e: e.dma_start(out=srcd[:], in_=d_h))
    for sl in range(11):
        emit_quant(p, c, srcd[:, sl * 256:(sl + 1) * 256], "srcd",
                   dstd[:, sl * 256:(sl + 1) * 256], "dstd", 256)
    p.dma("q", ["OUT"], ["dstd"], lambda e: e.dma_start(out=do[:], in_=dstd[:]))
    p.build()
    gu = np.concatenate([w_gate, w_up], axis=0)
    maps = []
    for core in range(N_CORES):
        maps.append({"gu_in": np.ascontiguousarray(gu[core * 704:(core + 1) * 704]),
                     "d_in": np.ascontiguousarray(w_down[core * 128:(core + 1) * 128])})
    res = run_bass_kernel_spmd(nc, maps, core_ids=list(range(N_CORES)))
    gu_q = np.concatenate([r["gu_out"] for r in res.results], axis=0)
    d_q = np.concatenate([r["d_out"] for r in res.results], axis=0)
    return gu_q[:2816], gu_q[2816:], d_q


def emit_ln_quant_fm(p, c):
    nt, T = c["nt"], c["T"]
    if "x_fm" not in c:
        c["x_fm"] = p.sb("x_fm", [128, 8 * T], BF16)
        c["ln_stats"] = p.sb("ln_stats", [128, 8], F32)
        c["ln_sq"] = p.sb("ln_sq", [128, D], BF16)
        c["ln_xc"] = p.sb("ln_xc", [128, D], BF16)
    if "lnq_f32" not in c:
        c["lnq_f32"] = p.sb("lnq_f32", [128, D], F32)
    if "q_A" not in c:
        alloc_quant(p, c, W=512)
    eps = p.K(1e-5)
    st = c["ln_stats"]
    for i in range(nt):
        X = f"xt{i}"
        xt = c["xt"][i]
        p.step("v", [X], ["lnsx"], lambda e, xt=xt: e.tensor_reduce(
            out=st[:, 0:1], in_=xt[:], axis=AX.X, op=ALU.add))
        p.step("s", [X], ["ln_sq", "lnsxx"], lambda e, xt=xt: e.activation(
            out=c["ln_sq"][:], in_=xt[:], func=AF.Square, accum_out=st[:, 1:2]))
        p.step("v", ["lnsx"], ["lnmu"], lambda e: e.tensor_scalar(
            out=st[:, 2:3], in0=st[:, 0:1], scalar1=1.0 / D, scalar2=None, op0=ALU.mult))
        p.step("v", ["lnsxx"], ["ln3"], lambda e: e.tensor_scalar(
            out=st[:, 3:4], in0=st[:, 1:2], scalar1=1.0 / D, scalar2=None, op0=ALU.mult))
        p.step("v", ["ln3", "lnmu"], ["ln3"], lambda e: e.scalar_tensor_tensor(
            out=st[:, 3:4], in0=st[:, 2:3], scalar=st[:, 2:3], in1=st[:, 3:4],
            op0=ALU.mult, op1=ALU.subtract))
        p.step("s", ["ln3"], ["ln4"], lambda e, b=eps: e.activation(
            out=st[:, 4:5], in_=st[:, 3:4], func=AF.Sqrt, bias=b, scale=-1.0))
        p.step("v", ["ln4"], ["ln4"], lambda e: e.reciprocal(out=st[:, 4:5], in_=st[:, 4:5]))
        p.step("v", ["ln4", "lnmu"], ["ln5"], lambda e: e.tensor_scalar(
            out=st[:, 5:6], in0=st[:, 2:3], scalar1=st[:, 4:5], scalar2=-1.0,
            op0=ALU.mult, op1=ALU.mult))
        p.step("v", [X, "ln4", "ln5"], ["lnq_f32"], lambda e, xt=xt: e.tensor_scalar(
            out=c["lnq_f32"][:], in0=xt[:], scalar1=st[:, 4:5], scalar2=st[:, 5:6],
            op0=ALU.mult, op1=ALU.add))
        for sl in range(2):
            emit_quant(p, c, c["lnq_f32"][:, sl * 512:(sl + 1) * 512], "lnq_f32",
                       c["ln_xc"][:, sl * 512:(sl + 1) * 512], "ln_xc", 512)
        for ch in range(8):
            p.step("t", ["ln_xc", "ident_bf"], ["ps_T"], lambda e, ch=ch: e.transpose(
                out=c["ps_T"][:, 0:128], in_=c["ln_xc"][:, ch * 128:(ch + 1) * 128],
                identity=c["ident_bf"][:]))
            p.step("v", ["ps_T"], ["x_fm"], lambda e, i=i, ch=ch: e.tensor_copy(
                out=c["x_fm"][:, ch * T + i * 128: ch * T + (i + 1) * 128],
                in_=c["ps_T"][:, 0:128]))


def l3_ffn(p, c, wh, uid):
    T, nt = c["T"], c["nt"]
    HB = [512, 512, 512, 512, 512, 256]
    emit_ln_quant_fm(p, c)
    x_fm, ident, ones = c["x_fm"], c["ident_bf"], c["ones_bf"]
    if "l3f_gT" not in c:
        c["l3f_gT"] = [p.sb(f"l3f_gT{j}", [128, 8 * 512], BF16) for j in range(2)]
        c["l3f_uT"] = [p.sb(f"l3f_uT{j}", [128, 8 * 512], BF16) for j in range(2)]
        c["l3f_dT"] = [p.sb(f"l3f_dT{j}", [128, 1024], BF16) for j in range(2)]
        c["l3f_hf"] = p.sb("l3f_hf", [128, 512], F32)
        c["l3f_yf"] = p.sb("l3f_yf", [128, 512], F32)
        c["l3f_hq"] = p.sb("l3f_hq", [128, 512], BF16)
        c["l3f_hqfm"] = p.sb("l3f_hqfm", [128, 22 * 256], BF16)
    for tb in range(8):
        tiles = [tb * 2, tb * 2 + 1]
        for hb in range(6):
            w = HB[hb]
            j = hb % 2
            GTK, UTK = f"l3f_gT{j}", f"l3f_uT{j}"
            p.dma("q", [GTK], ["IN"], lambda e, hb=hb, j=j, w=w: e.dma_start(
                out=c["l3f_gT"][j][:, 0:8 * w].rearrange("p (k h) -> p k h", k=8),
                in_=wh["wqgT"][:, hb * 512: hb * 512 + w].rearrange("(k p) h -> p k h", p=128)),
                tag=f"l3g{tb}_{hb}")
            p.dma("q", [UTK], ["IN"], lambda e, hb=hb, j=j, w=w: e.dma_start(
                out=c["l3f_uT"][j][:, 0:8 * w].rearrange("p (k h) -> p k h", k=8),
                in_=wh["wquT"][:, hb * 512: hb * 512 + w].rearrange("(k p) h -> p k h", p=128)),
                tag=f"l3u{tb}_{hb}")
            for ti, i in enumerate(tiles):
                psg = c["ps_h1"] if ti == 0 else c["ps_h2"]
                PSG = "ps_h1" if ti == 0 else "ps_h2"
                psy = c["ps_y1"] if ti == 0 else c["ps_y2"]
                PSY = "ps_y1" if ti == 0 else "ps_y2"
                def mm_g(e, i=i, j=j, w=w, psg=psg):
                    return [e.matmul(out=psg[:, 0:w],
                                     lhsT=x_fm[:, k * T + i * 128: k * T + (i + 1) * 128],
                                     rhs=c["l3f_gT"][j][:, k * w:(k + 1) * w],
                                     start=(k == 0), stop=(k == 7)) for k in range(8)]
                p.step("t", ["x_fm", GTK], [PSG], mm_g)
                def mm_u(e, i=i, j=j, w=w, psy=psy):
                    return [e.matmul(out=psy[:, 0:w],
                                     lhsT=x_fm[:, k * T + i * 128: k * T + (i + 1) * 128],
                                     rhs=c["l3f_uT"][j][:, k * w:(k + 1) * w],
                                     start=(k == 0), stop=(k == 7)) for k in range(8)]
                p.step("t", ["x_fm", UTK], [PSY], mm_u)
                p.step("s", [PSG], ["l3f_hf"], lambda e, w=w, psg=psg: e.activation(
                    out=c["l3f_hf"][:, 0:w], in_=psg[:, 0:w], func=AF.Silu))
                p.step("v", [PSY], ["l3f_yf"], lambda e, w=w, psy=psy: e.tensor_copy(
                    out=c["l3f_yf"][:, 0:w], in_=psy[:, 0:w]))
                p.step("v", ["l3f_hf", "l3f_yf"], ["l3f_hf"], lambda e, w=w: e.tensor_tensor(
                    out=c["l3f_hf"][:, 0:w], in0=c["l3f_hf"][:, 0:w], in1=c["l3f_yf"][:, 0:w],
                    op=ALU.mult))
                emit_quant(p, c, c["l3f_hf"][:, 0:w], "l3f_hf", c["l3f_hq"][:, 0:w], "l3f_hq", w)
                for hh in range(w // 128):
                    hk = hb * 4 + hh
                    p.step("t", ["l3f_hq", "ident_bf"], ["ps_T"], lambda e, hh=hh: e.transpose(
                        out=c["ps_T"][:, 0:128], in_=c["l3f_hq"][:, hh * 128:(hh + 1) * 128],
                        identity=ident[:]))
                    p.step("v", ["ps_T"], ["l3f_hqfm"], lambda e, hk=hk, ti=ti: e.tensor_copy(
                        out=c["l3f_hqfm"][:, hk * 256 + ti * 128: hk * 256 + (ti + 1) * 128],
                        in_=c["ps_T"][:, 0:128]))
        for hk in range(22):
            j2 = hk % 2
            DTK = f"l3f_dT{j2}"
            p.dma("q", [DTK], ["IN"], lambda e, hk=hk, j2=j2: e.dma_start(
                out=c["l3f_dT"][j2][:], in_=wh["wqdT"][hk * 128:(hk + 1) * 128, :]),
                tag=f"l3d{tb}_{hk}")
            for ti, i in enumerate(tiles):
                for db in range(2):
                    psd = [c["ps_h1"], c["ps_h2"], c["ps_y1"], c["ps_y2"]][ti * 2 + db]
                    PSD = ["ps_h1", "ps_h2", "ps_y1", "ps_y2"][ti * 2 + db]
                    p.step("t", ["l3f_hqfm", DTK], [PSD], lambda e, hk=hk, ti=ti, db=db,
                           j2=j2, psd=psd: e.matmul(
                        out=psd[:, 0:512],
                        lhsT=c["l3f_hqfm"][:, hk * 256 + ti * 128: hk * 256 + (ti + 1) * 128],
                        rhs=c["l3f_dT"][j2][:, db * 512:(db + 1) * 512],
                        start=(hk == 0), stop=(hk == 21)))
        for ti, i in enumerate(tiles):
            for db in range(2):
                psd = [c["ps_h1"], c["ps_h2"], c["ps_y1"], c["ps_y2"]][ti * 2 + db]
                PSD = ["ps_h1", "ps_h2", "ps_y1", "ps_y2"][ti * 2 + db]
                p.step("v", [PSD, f"xt{i}"], [f"xt{i}"], lambda e, i=i, db=db, psd=psd:
                       e.scalar_tensor_tensor(out=c["xt"][i][:, db * 512:(db + 1) * 512],
                                              in0=psd[:, 0:512], scalar=0.5,
                                              in1=c["xt"][i][:, db * 512:(db + 1) * 512],
                                              op0=ALU.mult, op1=ALU.add))


def run_l3_ffn(x_full, wq_gate, wq_up, wq_down):
    nc = make_nc()
    p = Prog(nc)
    c = setup_common(p)
    alloc_x(p, c)
    xh = p.dram("x_in", [T_CORE, D], F32, kind="ExternalInput")
    oh = p.dram("x_out", [T_CORE, D], F32, kind="ExternalOutput")
    wh = {
        "wqgT": p.dram("w_gT", [D, 2816], BF16, kind="ExternalInput"),
        "wquT": p.dram("w_uT", [D, 2816], BF16, kind="ExternalInput"),
        "wqdT": p.dram("w_dT", [2816, D], BF16, kind="ExternalInput"),
    }
    load_x(p, c, xh)
    l3_ffn(p, c, wh, uid="u")
    store_x(p, c, oh)
    p.build()
    wmap = {
        "w_gT": np.ascontiguousarray(wq_gate.T),
        "w_uT": np.ascontiguousarray(wq_up.T),
        "w_dT": np.ascontiguousarray(wq_down.T),
    }
    maps = _in_maps_tokenshard(p, x_full, wmap)
    res = run_bass_kernel_spmd(nc, maps, core_ids=list(range(N_CORES)))
    return np.concatenate([r["x_out"] for r in res.results], axis=0)


def run_l1_ffn_A(x_full, wg, bg):
    nc = make_nc()
    p = Prog(nc)
    c = setup_common(p)
    alloc_x(p, c)
    xh = p.dram("x_in", [T_CORE, D], F32, kind="ExternalInput")
    oh = p.dram("m_out", [T_CORE, 16], F32, kind="ExternalOutput")
    wgh = p.dram("w_wg", [D, 16], F32, kind="ExternalInput")
    bgh = p.dram("w_bg", [16], F32, kind="ExternalInput")
    load_x(p, c, xh)
    emit_ln_fm(p, c)
    T, nt = c["T"], c["nt"]
    x_fm, ones = c["x_fm"], c["ones_bf"]
    wgt = p.sb("wgt", [128, 8 * 16], BF16)
    bgt = p.sb("bgt", [1, 16], BF16)
    S = p.sb("S", [128, 16], F32)
    Sw = p.sb("Sw", [128, 8], F32)
    t16 = p.sb("t16", [128, 16], F32)
    mout = p.sb("mout", [128, 16], F32)
    p.dma("g", ["wgt"], ["IN"], lambda e: e.dma_start(
        out=wgt[:].rearrange("p (k e) -> p k e", e=16),
        in_=wgh.rearrange("(k p) e -> p k e", p=128)))
    p.dma("g", ["bgt"], ["IN"], lambda e: e.dma_start(out=bgt[:], in_=bgh.unsqueeze(0)))
    for i in range(nt):
        def mm(e, i=i):
            insts = [e.matmul(out=c["ps_a"][:, 0:16],
                              lhsT=x_fm[:, k * T + i * 128: k * T + (i + 1) * 128],
                              rhs=wgt[:, k * 16:(k + 1) * 16], start=(k == 0), stop=False)
                     for k in range(8)]
            insts.append(e.matmul(out=c["ps_a"][:, 0:16], lhsT=ones[:, 0:128],
                                  rhs=bgt[:], start=False, stop=True))
            return insts
        p.step("t", ["x_fm", "wgt", "bgt"], ["ps_a"], mm)
        p.step("v", ["ps_a"], ["Sw0"], lambda e: e.tensor_reduce(
            out=Sw[:, 0:1], in_=c["ps_a"][:, 0:16], axis=AX.X, op=ALU.max))
        p.step("v", ["Sw0"], ["Sw1"], lambda e: e.tensor_scalar(
            out=Sw[:, 1:2], in0=Sw[:, 0:1], scalar1=-1.0, scalar2=None, op0=ALU.mult))
        p.step("s", ["ps_a", "Sw1"], ["S", "Sw2"], lambda e: e.activation(
            out=S[:], in_=c["ps_a"][:, 0:16], func=AF.Exp, bias=Sw[:, 1:2],
            accum_out=Sw[:, 2:3]))
        p.step("v", ["Sw2"], ["Sw2"], lambda e: e.reciprocal(out=Sw[:, 2:3], in_=Sw[:, 2:3]))
        p.step("v", ["S", "Sw2"], ["S"], lambda e: e.tensor_scalar(
            out=S[:], in0=S[:], scalar1=Sw[:, 2:3], scalar2=None, op0=ALU.mult))
        p.step("v", ["S"], ["Sw0"], lambda e: e.tensor_reduce(
            out=Sw[:, 0:1], in_=S[:], axis=AX.X, op=ALU.max))
        p.step("v", ["S", "Sw0"], ["t16"], lambda e: e.tensor_scalar(
            out=t16[:], in0=S[:], scalar1=Sw[:, 0:1], scalar2=None, op0=ALU.is_ge))
        p.step("v", ["t16"], ["Sw1"], lambda e: e.tensor_reduce(
            out=Sw[:, 1:2], in_=t16[:], axis=AX.X, op=ALU.add))
        p.step("v", ["t16", "S"], ["t16"], lambda e: e.scalar_tensor_tensor(
            out=t16[:], in0=t16[:], scalar=-1e30, in1=S[:], op0=ALU.mult, op1=ALU.add))
        p.step("v", ["t16"], ["Sw3"], lambda e: e.tensor_reduce(
            out=Sw[:, 3:4], in_=t16[:], axis=AX.X, op=ALU.max))
        p.step("v", ["Sw1"], ["Sw1"], lambda e: e.tensor_single_scalar(
            out=Sw[:, 1:2], in_=Sw[:, 1:2], scalar=1.5, op=ALU.is_ge))
        p.step("v", ["Sw0", "Sw3"], ["Sw4"], lambda e: e.tensor_scalar(
            out=Sw[:, 4:5], in0=Sw[:, 0:1], scalar1=Sw[:, 3:4], scalar2=None, op0=ALU.subtract))
        p.step("v", ["Sw4", "Sw1", "Sw3"], ["Sw5"], lambda e: e.tensor_scalar(
            out=Sw[:, 5:6], in0=Sw[:, 4:5], scalar1=Sw[:, 1:2], scalar2=Sw[:, 3:4],
            op0=ALU.mult, op1=ALU.add))
        p.step("v", ["S", "Sw5"], ["t16"], lambda e: e.tensor_scalar(
            out=t16[:], in0=S[:], scalar1=Sw[:, 5:6], scalar2=None, op0=ALU.is_ge))
        p.step("v", ["S", "t16"], ["mout"], lambda e: e.tensor_tensor(
            out=mout[:], in0=S[:], in1=t16[:], op=ALU.mult))
        p.dma("q", ["OUT"], ["mout"], lambda e, i=i: e.dma_start(
            out=oh[i * 128:(i + 1) * 128, :], in_=mout[:]))
    p.build()
    maps = _in_maps_tokenshard(p, x_full, {"w_wg": wg, "w_bg": bg})
    res = run_bass_kernel_spmd(nc, maps, core_ids=list(range(N_CORES)))
    return np.concatenate([r["m_out"] for r in res.results], axis=0)


def run_l1_ffn_B(masked_full):
    K4 = 4096
    nc = make_nc()
    p = Prog(nc)
    mo_h = p.dram("m_own", [4, 8192], F32, kind="ExternalInput")
    mp_h = p.dram("m_par", [4, 8192], F32, kind="ExternalInput")
    pat_h = p.dram("c_grpsum", [128, 128], BF16, kind="ExternalInput")
    tri_h = p.dram("c_grptri", [128, 128], BF16, kind="ExternalInput")
    sel_h = p.dram("sel_out", [4, 8192], F32, kind="ExternalOutput")
    v4 = p.sb("v4", [128, 256], F32)
    vp = p.sb("vp", [128, 256], F32)
    pat = p.sb("pat", [128, 128], BF16)
    tri = p.sb("tri", [128, 128], BF16)
    cm = p.sb("cm", [128, 256], F32)
    cmb = p.sb("cmb", [128, 1], BF16)
    red = p.sb("red", [128, 1], F32)
    lo = p.sb("lo", [128, 1], F32)
    hi = p.sb("hi", [128, 1], F32)
    mid = p.sb("mid", [128, 1], F32)
    npos = p.sb("npos", [128, 1], F32)
    selu = p.sb("selu", [128, 1], U8)
    ca = p.sb("ca", [128, 256], F32)
    cb = p.sb("cb", [128, 256], F32)
    offs = p.sb("offs", [128, 1], F32)
    needA = p.sb("needA", [128, 1], F32)
    selA = p.sb("selA", [128, 256], F32)
    selB = p.sb("selB", [128, 256], F32)
    mz = p.sb("mz", [128, 256], F32)
    caseB = p.sb("caseB", [128, 256], U8)
    ps_c = p.ps("ps_c", [128, 512], F32)

    p.dma("q", ["v4"], ["IN"], lambda e: e.dma_start(
        out=v4[:], in_=mo_h.rearrange("u (r j) -> (u r) j", r=32)))
    p.dma("q", ["vp"], ["IN"], lambda e: e.dma_start(
        out=vp[:], in_=mp_h.rearrange("u (r j) -> (u r) j", r=32)))
    p.dma("q", ["pat"], ["IN"], lambda e: e.dma_start(out=pat[:], in_=pat_h))
    p.dma("q", ["tri"], ["IN"], lambda e: e.dma_start(out=tri[:], in_=tri_h))
    p.step("v", ["v4", "vp"], ["vp"], lambda e: e.tensor_tensor(
        out=vp[:], in0=v4[:], in1=vp[:], op=ALU.add))
    p.step("v", ["vp"], ["vp"], lambda e: e.tensor_single_scalar(
        out=vp[:], in_=vp[:], scalar=1e-6, op=ALU.add))
    p.step("v", ["vp"], ["vp"], lambda e: e.reciprocal(out=vp[:], in_=vp[:]))
    p.step("v", ["v4", "vp"], ["v4"], lambda e: e.tensor_tensor(
        out=v4[:], in0=v4[:], in1=vp[:], op=ALU.mult))
    p.step("v", ["v4"], ["v4"], lambda e: e.tensor_single_scalar(
        out=v4[:], in_=v4[:], scalar=2.0, op=ALU.mult))

    def count_ge(thr_ap, thr_key, out_key):
        p.step("v", ["v4", thr_key], ["cm"], lambda e: e.tensor_scalar(
            out=cm[:], in0=v4[:], scalar1=thr_ap, scalar2=None, op0=ALU.is_ge))
        p.step("v", ["cm"], ["red"], lambda e: e.tensor_reduce(
            out=red[:], in_=cm[:], axis=AX.X, op=ALU.add))
        p.step("v", ["red"], ["cmb"], lambda e: e.tensor_copy(out=cmb[:], in_=red[:]))
        p.step("t", ["cmb", "pat"], [out_key], lambda e: e.matmul(
            out=ps_c[:, 0:1], lhsT=pat[:], rhs=cmb[:], start=True, stop=True))

    p.step("v", [], ["lo"], lambda e: e.memset(lo[:], 0.0))
    p.step("v", [], ["hi"], lambda e: e.memset(hi[:], 4.0))
    p.step("v", [], ["mid"], lambda e: e.memset(mid[:], 1e-38))
    count_ge(mid[:, 0:1], "mid", "ps_c")
    p.step("v", ["ps_c"], ["npos"], lambda e: e.tensor_copy(out=npos[:], in_=ps_c[:, 0:1]))
    for it in range(45):
        p.step("v", ["lo", "hi"], ["mid"], lambda e: e.tensor_scalar(
            out=mid[:], in0=lo[:], scalar1=hi[:, 0:1], scalar2=0.5, op0=ALU.add, op1=ALU.mult))
        count_ge(mid[:, 0:1], "mid", "ps_c")
        p.step("v", ["ps_c"], ["selu"], lambda e: e.tensor_single_scalar(
            out=selu[:], in_=ps_c[:, 0:1], scalar=float(K4), op=ALU.is_ge))
        p.step("v", ["selu", "mid", "lo"], ["lo"], lambda e: e.select(
            out=lo[:], mask=selu[:], on_true=mid[:], on_false=lo[:]))
        p.step("v", ["selu", "mid", "hi"], ["hi"], lambda e: e.select(
            out=hi[:], mask=selu[:], on_true=hi[:], on_false=mid[:]))
    p.step("v", ["v4", "lo"], ["selB"], lambda e: e.tensor_scalar(
        out=selB[:], in0=v4[:], scalar1=lo[:, 0:1], scalar2=None, op0=ALU.is_ge))
    p.step("v", ["v4"], ["mz"], lambda e: e.tensor_single_scalar(
        out=mz[:], in_=v4[:], scalar=0.0, op=ALU.is_equal))
    p.step("v", ["mz"], ["ca"], lambda e: e.tensor_copy(out=ca[:], in_=mz[:]))
    srcbuf, dstbuf, sk, dk = ca, cb, "ca", "cb"
    for sh in [1, 2, 4, 8, 16, 32, 64, 128]:
        p.step("v", [sk], [dk], lambda e, sh=sh, s=srcbuf, d=dstbuf: [
            e.tensor_copy(out=d[:, 0:sh], in_=s[:, 0:sh]),
            e.tensor_tensor(out=d[:, sh:256], in0=s[:, sh:256], in1=s[:, 0:256 - sh],
                            op=ALU.add)][-1])
        srcbuf, dstbuf, sk, dk = dstbuf, srcbuf, dk, sk
    CUM, CUMK = srcbuf, sk
    p.step("v", [CUMK], ["cmb"], lambda e: e.tensor_copy(out=cmb[:], in_=CUM[:, 255:256]))
    p.step("t", ["cmb", "tri"], ["ps_c"], lambda e: e.matmul(
        out=ps_c[:, 1:2], lhsT=tri[:], rhs=cmb[:], start=True, stop=True))
    p.step("v", ["ps_c"], ["offs"], lambda e: e.tensor_copy(out=offs[:], in_=ps_c[:, 1:2]))
    p.step("v", [CUMK, "offs"], [CUMK], lambda e: e.tensor_scalar(
        out=CUM[:], in0=CUM[:], scalar1=offs[:, 0:1], scalar2=None, op0=ALU.add))
    p.step("v", ["npos"], ["needA"], lambda e: e.tensor_scalar(
        out=needA[:], in0=npos[:], scalar1=-1.0, scalar2=float(K4), op0=ALU.mult, op1=ALU.add))
    p.step("v", [CUMK, "needA"], ["ca2"], lambda e: e.tensor_scalar(
        out=ca[:] if CUM is cb else cb[:], in0=CUM[:], scalar1=needA[:, 0:1], scalar2=None,
        op0=ALU.is_le))
    OTH, OTHK = (ca, "ca2") if CUM is cb else (cb, "ca2")
    p.step("v", ["mz", "ca2"], ["selA"], lambda e: e.tensor_tensor(
        out=selA[:], in0=mz[:], in1=OTH[:], op=ALU.mult))
    p.step("v", ["v4"], ["cm"], lambda e: e.tensor_single_scalar(
        out=cm[:], in_=v4[:], scalar=0.0, op=ALU.is_gt))
    p.step("v", ["selA", "cm"], ["selA"], lambda e: e.tensor_tensor(
        out=selA[:], in0=selA[:], in1=cm[:], op=ALU.add))
    p.step("v", ["npos"], ["caseB"], lambda e: e.tensor_single_scalar(
        out=caseB[:, 0:1], in_=npos[:], scalar=float(K4) + 0.5, op=ALU.is_ge))
    p.step("v", ["caseB", "selB", "selA"], ["selA"], lambda e: e.select(
        out=selA[:], mask=caseB[:, 0:1].broadcast_to([128, 256]), on_true=selB[:],
        on_false=selA[:]))
    p.dma("q", ["OUT"], ["selA"], lambda e: e.dma_start(
        out=sel_h.rearrange("u (r j) -> (u r) j", r=32), in_=selA[:]))
    p.build()
    pat_np = np.zeros((128, 128), dtype=BF)
    tri_np = np.zeros((128, 128), dtype=BF)
    for q in range(128):
        for pp in range(128):
            if q // 32 == pp // 32:
                pat_np[q, pp] = 1
                if (q % 32) < (pp % 32):
                    tri_np[q, pp] = 1
    maps = []
    for core in range(N_CORES):
        own = np.zeros((4, 8192), dtype=np.float32)
        par = np.zeros((4, 8192), dtype=np.float32)
        for u in range(4):
            gi = 4 * core + u
            e_, b_ = gi % 16, gi // 16
            own[u] = masked_full[b_, :, e_]
            par[u] = masked_full[1 - b_, :, e_]
        maps.append({"m_own": own, "m_par": par, "c_grpsum": pat_np, "c_grptri": tri_np})
    res = run_bass_kernel_spmd(nc, maps, core_ids=list(range(N_CORES)))
    sel = np.zeros((2, 16, 8192), dtype=np.float32)
    for core in range(N_CORES):
        so = res.results[core]["sel_out"]
        for u in range(4):
            gi = 4 * core + u
            sel[gi // 16, gi % 16] = so[u]
    return sel


def l1_ffn_C(p, c, wh, uid):
    T, nt = c["T"], c["nt"]
    NTAU = T // 256
    emit_ln_fm(p, c, want_z=True)
    x_fm, ident, ones = c["x_fm"], c["ident_bf"], c["ones_bf"]
    if "l1f_w1" not in c:
        c["l1f_w1"] = [p.sb(f"l1f_w1_{j}", [128, 8 * 256], BF16) for j in range(2)]
        c["l1f_w2"] = [p.sb(f"l1f_w2_{j}", [128, 2 * 1024], BF16) for j in range(2)]
        c["l1f_b1"] = [p.sb(f"l1f_b1_{j}", [1, 256], BF16) for j in range(2)]
        c["l1f_b2"] = [p.sb(f"l1f_b2_{j}", [1, 1024], BF16) for j in range(2)]
        c["l1f_m"] = [p.sb(f"l1f_m{j}", [1, T], BF16) for j in range(2)]
        c["l1f_g"] = [p.sb(f"l1f_g{m}", [128, 256], BF16) for m in range(4)]
        c["l1f_wb"] = p.sb("l1f_wb", [128, 2 * 256], BF16)
        c["l1f_gam"] = p.sb("l1f_gam", [128, 1024], F32)
        c["l1f_bet"] = p.sb("l1f_bet", [128, 1024], F32)
        c["l1f_u"] = p.sb("l1f_u", [128, 1024], F32)
    p.dma("q", ["l1f_gam"], ["IN"], lambda e: e.dma_start(out=c["l1f_gam"][:], in_=wh["gamma_bc"]))
    p.dma("q", ["l1f_bet"], ["IN"], lambda e: e.dma_start(out=c["l1f_bet"][:], in_=wh["beta_bc"]))
    for pe in range(8):
        eA, eB = 2 * pe, 2 * pe + 1
        for x, e_i in ((0, eA), (1, eB)):
            j = e_i % 2
            W1K, W2K, B1K, B2K, MK = (f"l1f_w1_{j}", f"l1f_w2_{j}", f"l1f_b1_{j}",
                                      f"l1f_b2_{j}", f"l1f_m{j}")
            p.dma("g", [W1K], ["IN"], lambda e, e_i=e_i, j=j: e.dma_start(
                out=c["l1f_w1"][j][:].rearrange("p (k h) -> p k h", k=8),
                in_=wh["w1"][e_i].rearrange("(k p) h -> p k h", p=128)), tag=f"l1w1e{e_i}")
            p.dma("g", [W2K], ["IN"], lambda e, e_i=e_i, j=j: e.dma_start(
                out=c["l1f_w2"][j][:].rearrange("p (k d) -> p k d", k=2),
                in_=wh["w2"][e_i].rearrange("(k p) d -> p k d", p=128)), tag=f"l1w2e{e_i}")
            p.dma("g", [B1K], ["IN"], lambda e, e_i=e_i, j=j: e.dma_start(
                out=c["l1f_b1"][j][:], in_=wh["b1"][e_i:e_i + 1, :]), tag=f"l1b1e{e_i}")
            p.dma("g", [B2K], ["IN"], lambda e, e_i=e_i, j=j: e.dma_start(
                out=c["l1f_b2"][j][:], in_=wh["b2"][e_i:e_i + 1, :]), tag=f"l1b2e{e_i}")
            p.dma("g", [MK], ["IN"], lambda e, e_i=e_i, j=j: e.dma_start(
                out=c["l1f_m"][j][:], in_=wh["msel"][:, e_i * T:(e_i + 1) * T]), tag=f"l1me{e_i}")
        for tau in range(NTAU):
            for x, jj in ((0, 0), (1, 1)):
                MK = f"l1f_m{jj}"
                p.step("t", [MK, "ones_bf"], ["ps_a"], lambda e, jj=jj, x=x, tau=tau: e.matmul(
                    out=c["ps_a"][:, x * 256:(x + 1) * 256], lhsT=ones[:, 0:128],
                    rhs=c["l1f_m"][jj][:, tau * 256:(tau + 1) * 256], start=True, stop=True))
            p.step("v", ["ps_a"], ["l1f_wb"], lambda e: e.tensor_copy(
                out=c["l1f_wb"][:], in_=c["ps_a"][:, 0:512]))
            for x, jj in ((0, 0), (1, 1)):
                for m in range(2):
                    psh = c["ps_h1"] if (x * 2 + m) % 2 == 0 else c["ps_h2"]
                    PSH = "ps_h1" if (x * 2 + m) % 2 == 0 else "ps_h2"
                    def mm_w1(e, m=m, tau=tau, jj=jj, psh=psh):
                        insts = [e.matmul(
                            out=psh[:, 0:256],
                            lhsT=c["l1f_w1"][jj][:, k * 256 + m * 128: k * 256 + (m + 1) * 128],
                            rhs=x_fm[:, k * T + tau * 256: k * T + (tau + 1) * 256],
                            start=(k == 0), stop=False) for k in range(8)]
                        insts.append(e.matmul(
                            out=psh[:, 0:256], lhsT=c["l1f_b1"][jj][:, m * 128:(m + 1) * 128],
                            rhs=ones[:, 0:256], start=False, stop=True))
                        return insts
                    p.step("t", [f"l1f_w1_{jj}", "x_fm", f"l1f_b1_{jj}", "ones_bf"],
                           [PSH], mm_w1)
                    GK = f"l1f_g{x * 2 + m}"
                    p.step("s", [PSH], [GK], lambda e, x=x, m=m, psh=psh: e.activation(
                        out=c["l1f_g"][x * 2 + m][:], in_=psh[:, 0:256], func=AF.Gelu))
                    p.step("v", [GK, "l1f_wb"], [GK], lambda e, x=x, m=m: e.tensor_tensor(
                        out=c["l1f_g"][x * 2 + m][:], in0=c["l1f_g"][x * 2 + m][:],
                        in1=c["l1f_wb"][:, x * 256:(x + 1) * 256], op=ALU.mult))
            for half in range(2):
                i = tau * 2 + half
                for db in range(2):
                    psy = c["ps_y1"] if db == 0 else c["ps_y2"]
                    PSY = "ps_y1" if db == 0 else "ps_y2"
                    def mm_w2(e, half=half, db=db, psy=psy, tau=tau):
                        insts = []
                        for x, jj in ((0, 0), (1, 1)):
                            for m in range(2):
                                insts.append(e.matmul(
                                    out=psy[:, 0:512],
                                    lhsT=c["l1f_g"][x * 2 + m][:, half * 128:(half + 1) * 128],
                                    rhs=c["l1f_w2"][jj][:, m * 1024 + db * 512:
                                                        m * 1024 + (db + 1) * 512],
                                    start=(x == 0 and m == 0), stop=False))
                            insts.append(e.matmul(
                                out=psy[:, 0:512],
                                lhsT=c["l1f_m"][jj][:, tau * 256 + half * 128:
                                                    tau * 256 + (half + 1) * 128],
                                rhs=c["l1f_b2"][jj][:, db * 512:(db + 1) * 512],
                                start=False, stop=(x == 1)))
                        return insts
                    p.step("t", ["l1f_w2_0", "l1f_w2_1", "l1f_m0", "l1f_m1", "l1f_b2_0",
                                 "l1f_b2_1"] + [f"l1f_g{mm}" for mm in range(4)], [PSY], mm_w2)
                    p.step("v", [PSY, f"zt{i}"], [f"zt{i}"], lambda e, i=i, db=db, psy=psy:
                           e.tensor_tensor(out=c["zt"][i][:, db * 512:(db + 1) * 512],
                                           in0=psy[:, 0:512],
                                           in1=c["zt"][i][:, db * 512:(db + 1) * 512],
                                           op=ALU.add))
    eps = p.K(1e-5)
    st = c["ln_stats"]
    for i in range(nt):
        Z = f"zt{i}"
        zt = c["zt"][i]
        p.step("v", [Z], ["lnsx"], lambda e, zt=zt: e.tensor_reduce(
            out=st[:, 0:1], in_=zt[:], axis=AX.X, op=ALU.add))
        p.step("s", [Z], ["ln_sq", "lnsxx"], lambda e, zt=zt: e.activation(
            out=c["ln_sq"][:], in_=zt[:], func=AF.Square, accum_out=st[:, 1:2]))
        p.step("v", ["lnsx"], ["lnmu"], lambda e: e.tensor_scalar(
            out=st[:, 2:3], in0=st[:, 0:1], scalar1=1.0 / D, scalar2=None, op0=ALU.mult))
        p.step("v", ["lnsxx"], ["ln3"], lambda e: e.tensor_scalar(
            out=st[:, 3:4], in0=st[:, 1:2], scalar1=1.0 / D, scalar2=None, op0=ALU.mult))
        p.step("v", ["ln3", "lnmu"], ["ln3"], lambda e: e.scalar_tensor_tensor(
            out=st[:, 3:4], in0=st[:, 2:3], scalar=st[:, 2:3], in1=st[:, 3:4],
            op0=ALU.mult, op1=ALU.subtract))
        p.step("s", ["ln3"], ["ln4"], lambda e, b=eps: e.activation(
            out=st[:, 4:5], in_=st[:, 3:4], func=AF.Sqrt, bias=b, scale=-1.0))
        p.step("v", ["ln4"], ["ln4"], lambda e: e.reciprocal(out=st[:, 4:5], in_=st[:, 4:5]))
        p.step("v", ["ln4", "lnmu"], ["ln5"], lambda e: e.tensor_scalar(
            out=st[:, 5:6], in0=st[:, 2:3], scalar1=st[:, 4:5], scalar2=-1.0,
            op0=ALU.mult, op1=ALU.mult))
        p.step("v", [Z, "ln4", "ln5"], ["l1f_u"], lambda e, zt=zt: e.tensor_scalar(
            out=c["l1f_u"][:], in0=zt[:], scalar1=st[:, 4:5], scalar2=st[:, 5:6],
            op0=ALU.mult, op1=ALU.add))
        p.step("v", ["l1f_u", "l1f_gam"], ["l1f_u"], lambda e: e.tensor_tensor(
            out=c["l1f_u"][:], in0=c["l1f_u"][:], in1=c["l1f_gam"][:], op=ALU.mult))
        p.step("v", ["l1f_u", "l1f_bet"], ["l1f_u"], lambda e: e.tensor_tensor(
            out=c["l1f_u"][:], in0=c["l1f_u"][:], in1=c["l1f_bet"][:], op=ALU.add))
        p.step("v", ["l1f_u", f"xt{i}"], [f"xt{i}"], lambda e, i=i: e.scalar_tensor_tensor(
            out=c["xt"][i][:], in0=c["l1f_u"][:], scalar=0.5, in1=c["xt"][i][:],
            op0=ALU.mult, op1=ALU.add))


def run_l1_ffn_C(x_half, msel_half, w1, b1, w2, b2, gamma, beta):
    Tc = 1024
    nc = make_nc()
    p = Prog(nc)
    c = setup_common(p, T=Tc)
    alloc_x(p, c)
    xh = p.dram("x_in", [Tc, D], F32, kind="ExternalInput")
    oh = p.dram("x_out", [Tc, D], F32, kind="ExternalOutput")
    wh = {
        "w1": p.dram("w_w1", [16, D, 256], F32, kind="ExternalInput"),
        "b1": p.dram("w_b1", [16, 256], F32, kind="ExternalInput"),
        "w2": p.dram("w_w2", [16, 256, D], F32, kind="ExternalInput"),
        "b2": p.dram("w_b2", [16, D], F32, kind="ExternalInput"),
        "msel": p.dram("w_msel", [1, 16 * Tc], F32, kind="ExternalInput"),
        "gamma_bc": p.dram("w_gam", [128, D], F32, kind="ExternalInput"),
        "beta_bc": p.dram("w_bet", [128, D], F32, kind="ExternalInput"),
    }
    load_x(p, c, xh)
    l1_ffn_C(p, c, wh, uid="u")
    store_x(p, c, oh)
    p.build()
    consts = common_const_inputs(p)
    gb = np.ascontiguousarray(np.broadcast_to(gamma[None, :], (128, D)))
    bb = np.ascontiguousarray(np.broadcast_to(beta[None, :], (128, D)))
    maps = []
    for core in range(N_CORES):
        m = {"x_in": np.ascontiguousarray(x_half[core * Tc:(core + 1) * Tc]),
             "w_w1": w1, "w_b1": b1, "w_w2": w2, "w_b2": b2,
             "w_msel": msel_half[core], "w_gam": gb, "w_bet": bb}
        m.update(consts)
        maps.append(m)
    res = run_bass_kernel_spmd(nc, maps, core_ids=list(range(N_CORES)))
    return np.concatenate([r["x_out"] for r in res.results], axis=0)


def run_l0_attn_P(x_full, Wqk, Wv):
    nc = make_nc()
    p = Prog(nc)
    c = setup_common(p)
    alloc_x(p, c)
    xh = p.dram("x_in", [T_CORE, D], F32, kind="ExternalInput")
    qkh = p.dram("qk_out", [T_CORE, D], BF16, kind="ExternalOutput")
    vh = p.dram("v_out", [T_CORE, D], BF16, kind="ExternalOutput")
    wqk_h = p.dram("w_qk", [D, D], F32, kind="ExternalInput")
    wv_h = p.dram("w_v", [D, D], F32, kind="ExternalInput")
    load_x(p, c, xh)
    emit_ln_fm(p, c)
    T, nt = c["T"], c["nt"]
    x_fm = c["x_fm"]
    wqk = p.sb("wqk", [128, 8 * D], BF16)
    wv = p.sb("wv", [128, 8 * D], BF16)
    ob = p.sb("ob", [128, D], BF16)
    ob2 = p.sb("ob2", [128, D], BF16)
    p.dma("g", ["wqk"], ["IN"], lambda e: e.dma_start(
        out=wqk[:].rearrange("p (k d) -> p k d", k=8),
        in_=wqk_h.rearrange("(k p) d -> p k d", p=128)))
    p.dma("g", ["wv"], ["IN"], lambda e: e.dma_start(
        out=wv[:].rearrange("p (k d) -> p k d", k=8),
        in_=wv_h.rearrange("(k p) d -> p k d", p=128)))
    for i in range(nt):
        for (WK, wt, OB, OBK, oh) in (("wqk", wqk, ob, "ob", qkh), ("wv", wv, ob2, "ob2", vh)):
            for db in range(2):
                psy = c["ps_y1"] if db == 0 else c["ps_y2"]
                PSY = "ps_y1" if db == 0 else "ps_y2"
                def mm(e, i=i, db=db, wt=wt, psy=psy):
                    return [e.matmul(out=psy[:, 0:512],
                                     lhsT=x_fm[:, k * T + i * 128: k * T + (i + 1) * 128],
                                     rhs=wt[:, k * D + db * 512: k * D + (db + 1) * 512],
                                     start=(k == 0), stop=(k == 7)) for k in range(8)]
                p.step("t", ["x_fm", WK], [PSY], mm)
                p.step("v", [PSY], [OBK], lambda e, db=db, OB=OB, psy=psy: e.tensor_copy(
                    out=OB[:, db * 512:(db + 1) * 512], in_=psy[:, 0:512]))
            p.dma("q", ["OUT"], [OBK], lambda e, i=i, OB=OB, oh=oh: e.dma_start(
                out=oh[i * 128:(i + 1) * 128, :], in_=OB[:]))
    p.build()
    maps = _in_maps_tokenshard(p, x_full, {"w_qk": Wqk, "w_v": Wv})
    res = run_bass_kernel_spmd(nc, maps, core_ids=list(range(N_CORES)))
    qk = np.concatenate([r["qk_out"] for r in res.results], axis=0)
    v = np.concatenate([r["v_out"] for r in res.results], axis=0)
    return qk, v


def run_l0_attn_H(qk_bf, v_bf, tau):
    Nn = 8192
    NCH = Nn // 128
    nc = make_nc()
    p = Prog(nc)
    qk_h = p.dram("qk_in", [4, Nn, 64], BF16, kind="ExternalInput")
    v_h = p.dram("v_in", [4, 64, Nn], BF16, kind="ExternalInput")
    tau_h = p.dram("tau_in", [1, 4], F32, kind="ExternalInput")
    w_h = p.dram("w_out", [4, 64, Nn], BF16, kind="ExternalOutput")
    idf_h = p.dram("c_ident_f32", [64, 64], F32, kind="ExternalInput")
    idb_h = p.dram("c_ib", [64, 64], BF16, kind="ExternalInput")
    onf_h = p.dram("c_ones_f32", [1, 64], F32, kind="ExternalInput")
    qkt = p.sb("qkt", [128, NCH * 64], BF16)
    vfm = p.sb("vfm", [64, Nn], BF16)
    idf = p.sb("idf", [64, 64], F32)
    idb = p.sb("idb", [64, 64], BF16)
    onf = p.sb("onf", [1, 64], F32)
    taut = p.sb("taut", [1, 4], F32)
    taub = p.sb("taub", [64, 4], F32)
    dsb = p.sb("dsb", [64, 64], F32)
    q2 = p.sb("q2", [64, 8], F32)
    diagm = p.sb("diagm", [64, 64], F32)
    q2r = p.sb("q2r", [1, 64], F32)
    pexp = p.sb("pexp", [64, 64], BF16)
    pT = p.sb("pT", [64, 64], BF16)
    wsb = p.sb("wsb", [64, Nn], BF16)
    ps_d = p.ps("ps_d", [128, 512], F32)
    ps_w = p.ps("ps_w", [128, 512], F32)
    ps_t = p.ps("ps_t", [128, 512], BF16)
    p.dma("q", ["idf"], ["IN"], lambda e: e.dma_start(out=idf[:], in_=idf_h))
    p.dma("q", ["idb"], ["IN"], lambda e: e.dma_start(out=idb[:], in_=idb_h))
    p.dma("q", ["onf"], ["IN"], lambda e: e.dma_start(out=onf[:], in_=onf_h))
    p.dma("q", ["taut"], ["IN"], lambda e: e.dma_start(out=taut[:], in_=tau_h))
    p.step("t", ["taut", "onf"], ["ps_d"], lambda e: e.matmul(
        out=ps_d[0:64, 0:4], lhsT=onf[:, 0:64], rhs=taut[:], start=True, stop=True))
    p.step("v", ["ps_d"], ["taub"], lambda e: e.tensor_copy(
        out=taub[0:64, :], in_=ps_d[0:64, 0:4]))
    for hh in range(4):
        QT, VF = f"qkt", f"vfm"
        p.dma("q", [QT], ["IN"], lambda e, hh=hh: e.dma_start(
            out=qkt[:].rearrange("p (n j) -> p n j", j=64),
            in_=qk_h[hh].rearrange("(n p) j -> p n j", p=128)), tag=f"qk{hh}")
        p.dma("q", [VF], ["IN"], lambda e, hh=hh: e.dma_start(
            out=vfm[:], in_=v_h[hh]), tag=f"v{hh}")
        def mm_dot(e, hh=hh):
            return [e.matmul(out=ps_d[0:64, 0:64], lhsT=qkt[:, n * 64:(n + 1) * 64],
                             rhs=qkt[:, n * 64:(n + 1) * 64],
                             start=(n == 0), stop=(n == NCH - 1)) for n in range(NCH)]
        p.step("t", [QT], ["ps_d"], mm_dot)
        p.step("v", ["ps_d"], ["dsb"], lambda e: e.tensor_copy(out=dsb[:], in_=ps_d[0:64, 0:64]))
        p.step("v", ["dsb", "idf"], ["diagm"], lambda e: e.tensor_tensor(
            out=diagm[:], in0=dsb[:], in1=idf[:], op=ALU.mult))
        p.step("v", ["diagm"], ["q2b"], lambda e: e.tensor_reduce(
            out=q2[:, 0:1], in_=diagm[:], axis=AX.X, op=ALU.add))
        p.step("t", ["q2b", "idf"], ["ps_w"], lambda e: e.transpose(
            out=ps_w[0:1, 0:64], in_=q2[0:64, 0:1], identity=idf[:]))
        p.step("v", ["ps_w"], ["q2r"], lambda e: e.tensor_copy(
            out=q2r[:], in_=ps_w[0:1, 0:64]))
        p.step("v", ["dsb", "q2b"], ["dsb"], lambda e: e.tensor_scalar(
            out=dsb[:], in0=dsb[:], scalar1=2.0, scalar2=q2[:, 0:1],
            op0=ALU.mult, op1=ALU.subtract))
        p.step("t", ["q2r", "onf"], ["ps_w"], lambda e: e.matmul(
            out=ps_w[0:64, 0:64], lhsT=onf[:], rhs=q2r[:], start=True, stop=True))
        p.step("v", ["dsb", "ps_w"], ["dsb"], lambda e: e.tensor_tensor(
            out=dsb[:], in0=dsb[:], in1=ps_w[0:64, 0:64], op=ALU.subtract))
        p.step("v", ["dsb", "taub"], ["dsb"], lambda e, hh=hh: e.tensor_scalar(
            out=dsb[:], in0=dsb[:], scalar1=taub[0:64, hh:hh + 1],
            scalar2=float(1.0 / np.sqrt(Nn)), op0=ALU.mult, op1=ALU.mult))
        p.step("v", ["dsb"], ["q2m"], lambda e: e.tensor_reduce(
            out=q2[:, 1:2], in_=dsb[:], axis=AX.X, op=ALU.max))
        p.step("v", ["q2m"], ["q2n"], lambda e: e.tensor_scalar(
            out=q2[:, 2:3], in0=q2[:, 1:2], scalar1=-1.0, scalar2=None, op0=ALU.mult))
        p.step("s", ["dsb", "q2n"], ["pexp", "q2s"], lambda e: e.activation(
            out=pexp[:], in_=dsb[:], func=AF.Exp, bias=q2[:, 2:3], accum_out=q2[:, 3:4]))
        p.step("v", ["q2s"], ["q2s"], lambda e: e.reciprocal(out=q2[:, 3:4], in_=q2[:, 3:4]))
        p.step("t", ["pexp", "idb"], ["ps_t"], lambda e: e.transpose(
            out=ps_t[0:64, 0:64], in_=pexp[:], identity=idb[:]))
        p.step("v", ["ps_t"], ["pT"], lambda e: e.tensor_copy(out=pT[:], in_=ps_t[0:64, 0:64]))
        for nchunk in range(Nn // 512):
            p.step("t", ["pT", "vfm"], ["ps_w"], lambda e, nchunk=nchunk: e.matmul(
                out=ps_w[0:64, 0:512], lhsT=pT[:],
                rhs=vfm[:, nchunk * 512:(nchunk + 1) * 512], start=True, stop=True))
            p.step("v", ["ps_w", "q2s"], ["wsb"], lambda e, nchunk=nchunk: e.tensor_scalar(
                out=wsb[:, nchunk * 512:(nchunk + 1) * 512], in0=ps_w[0:64, 0:512],
                scalar1=q2[:, 3:4], scalar2=None, op0=ALU.mult))
        p.dma("q", ["OUT"], ["wsb"], lambda e, hh=hh: e.dma_start(
            out=w_h[hh], in_=wsb[:]), tag=f"wout{hh}")
    p.build()
    maps = []
    identf = np.eye(64, dtype=np.float32)
    identb = np.eye(64, dtype=BF)
    onesf = np.ones((1, 64), dtype=np.float32)
    for core in range(N_CORES):
        b = core // 4
        hset = [(core % 4) * 4 + k for k in range(4)]
        qk_c = np.stack([np.ascontiguousarray(
            qk_bf[b * 8192:(b + 1) * 8192, h * 64:(h + 1) * 64]) for h in hset])
        v_c = np.stack([np.ascontiguousarray(
            v_bf[b * 8192:(b + 1) * 8192, h * 64:(h + 1) * 64].T) for h in hset])
        tau_c = np.ascontiguousarray(tau.reshape(16)[hset])[None, :]
        maps.append({"qk_in": qk_c, "v_in": v_c, "tau_in": tau_c,
                     "c_ident_f32": identf, "c_ib": identb, "c_ones_f32": onesf})
    res = run_bass_kernel_spmd(nc, maps, core_ids=list(range(N_CORES)))
    w_all = np.zeros((2, 1024, 8192), dtype=BF)
    for core in range(N_CORES):
        b = core // 4
        for k in range(4):
            h = (core % 4) * 4 + k
            w_all[b, h * 64:(h + 1) * 64, :] = res.results[core]["w_out"][k]
    return w_all


def run_l0_attn_O(x_full, w_all, Wo):
    nc = make_nc()
    p = Prog(nc)
    c = setup_common(p)
    alloc_x(p, c)
    xh = p.dram("x_in", [T_CORE, D], F32, kind="ExternalInput")
    oh = p.dram("x_out", [T_CORE, D], F32, kind="ExternalOutput")
    wfm_h = p.dram("wfm_in", [128, 8 * T_CORE], BF16, kind="ExternalInput")
    wo_h = p.dram("w_wo", [D, D], F32, kind="ExternalInput")
    load_x(p, c, xh)
    T, nt = c["T"], c["nt"]
    wfm = p.sb("wfm", [128, 8 * T], BF16)
    wo = p.sb("wo", [128, 8 * D], BF16)
    p.dma("q", ["wfm"], ["IN"], lambda e: e.dma_start(out=wfm[:], in_=wfm_h))
    p.dma("g", ["wo"], ["IN"], lambda e: e.dma_start(
        out=wo[:].rearrange("p (k d) -> p k d", k=8),
        in_=wo_h.rearrange("(k p) d -> p k d", p=128)))
    for i in range(nt):
        for db in range(2):
            psy = c["ps_y1"] if db == 0 else c["ps_y2"]
            PSY = "ps_y1" if db == 0 else "ps_y2"
            def mm(e, i=i, db=db, psy=psy):
                return [e.matmul(out=psy[:, 0:512],
                                 lhsT=wfm[:, k * T + i * 128: k * T + (i + 1) * 128],
                                 rhs=wo[:, k * D + db * 512: k * D + (db + 1) * 512],
                                 start=(k == 0), stop=(k == 7)) for k in range(8)]
            p.step("t", ["wfm", "wo"], [PSY], mm)
            p.step("v", [PSY, f"xt{i}"], [f"xt{i}"], lambda e, i=i, db=db, psy=psy:
                   e.tensor_tensor(out=c["xt"][i][:, db * 512:(db + 1) * 512],
                                   in0=psy[:, 0:512],
                                   in1=c["xt"][i][:, db * 512:(db + 1) * 512], op=ALU.add))
    store_x(p, c, oh)
    p.build()
    maps = []
    consts = common_const_inputs(p)
    for core in range(N_CORES):
        b = core // 4
        s0 = (core % 4) * 2048
        wfm_c = np.ascontiguousarray(
            w_all[b].reshape(8, 128, 8192)[:, :, s0:s0 + 2048].transpose(1, 0, 2).reshape(128, 8 * 2048))
        m = {"x_in": np.ascontiguousarray(x_full[core * T_CORE:(core + 1) * T_CORE]),
             "wfm_in": wfm_c, "w_wo": Wo}
        m.update(consts)
        maps.append(m)
    res = run_bass_kernel_spmd(nc, maps, core_ids=list(range(N_CORES)))
    return np.concatenate([r["x_out"] for r in res.results], axis=0)


def l1_attn(p, c, wh, uid):
    T, nt = c["T"], c["nt"]
    NE = nt + 2
    TE = NE * 128
    ident, ones = c["ident_bf"], c["ones_bf"]
    if "l1a_xfme" not in c:
        c["l1a_xfme"] = p.sb("l1a_xfme", [128, 8 * TE], BF16)
        c["l1a_hA"] = p.sb("l1a_hA", [128, D], F32)
        c["l1a_hB"] = p.sb("l1a_hB", [128, D], F32)
        c["l1a_w"] = [p.sb(f"l1a_w{j}", [128, 8 * D], BF16) for j in range(2)]
        c["l1a_bia"] = p.sb("l1a_bia", [1, 4 * D], BF16)
        c["l1a_ev"] = p.sb("l1a_ev", [128, D], BF16)
        c["l1a_kw"] = [p.sb(f"l1a_kw{j}", [128, 8 * 384], BF16) for j in range(2)]
        c["l1a_vw"] = [p.sb(f"l1a_vw{j}", [128, 3 * D], BF16) for j in range(2)]
        c["l1a_band"] = p.sb("l1a_band", [128, 384], F32)
        c["l1a_valid"] = p.sb("l1a_valid", [128, TE], F32)
        c["l1a_ssb"] = p.sb("l1a_ssb", [128, 384], F32)
        c["l1a_pe"] = p.sb("l1a_pe", [128, 384], BF16)
        c["l1a_pT"] = p.sb("l1a_pT", [128, 384], BF16)
        c["l1a_rec"] = p.sb("l1a_rec", [128, 16], F32)
        c["l1a_st"] = p.sb("l1a_st", [128, 8], F32)
        c["l1a_ao"] = p.sb("l1a_ao", [128, D], BF16)
        c["l1a_aofm"] = p.sb("l1a_aofm", [128, 8 * 128], BF16)
    xfme = c["l1a_xfme"]
    kdram = p.dram(f"l1a_kdram_{uid}", [128, 8, TE], BF16)
    vdram = p.dram(f"l1a_vdram_{uid}", [NE, 128, D], BF16)
    p.dma("q", ["l1a_band"], ["IN"], lambda e: e.dma_start(out=c["l1a_band"][:], in_=wh["band"]))
    p.dma("q", ["l1a_valid"], ["IN"], lambda e: e.dma_start(out=c["l1a_valid"][:], in_=wh["valid"]))
    p.dma("g", ["l1a_bia"], ["IN"], lambda e: e.dma_start(
        out=c["l1a_bia"][:].rearrange("o (f d) -> o f d", f=4), in_=wh["b4"]))

    eps = p.K(1e-5)
    st = c["l1a_st"]
    p.dma("q", ["l1a_hA"], ["IN"], lambda e: e.dma_start(out=c["l1a_hA"][:], in_=wh["x_ext"][0:128, :]))
    p.dma("q", ["l1a_hB"], ["IN"], lambda e: e.dma_start(out=c["l1a_hB"][:], in_=wh["x_ext"][TE - 128:TE, :]))
    srcs = [(c["l1a_hA"], "l1a_hA")] + [(c["xt"][i], f"xt{i}") for i in range(nt)] + \
           [(c["l1a_hB"], "l1a_hB")]
    for te, (xt, XK) in enumerate(srcs):
        p.step("v", [XK], ["ast0"], lambda e, xt=xt: e.tensor_reduce(
            out=st[:, 0:1], in_=xt[:], axis=AX.X, op=ALU.add))
        p.step("s", [XK], ["ln_sq", "ast1"], lambda e, xt=xt: e.activation(
            out=c["ln_sq"][:], in_=xt[:], func=AF.Square, accum_out=st[:, 1:2]))
        p.step("v", ["ast0"], ["ast2"], lambda e: e.tensor_scalar(
            out=st[:, 2:3], in0=st[:, 0:1], scalar1=1.0 / D, scalar2=None, op0=ALU.mult))
        p.step("v", ["ast1"], ["ast3"], lambda e: e.tensor_scalar(
            out=st[:, 3:4], in0=st[:, 1:2], scalar1=1.0 / D, scalar2=None, op0=ALU.mult))
        p.step("v", ["ast3", "ast2"], ["ast3"], lambda e: e.scalar_tensor_tensor(
            out=st[:, 3:4], in0=st[:, 2:3], scalar=st[:, 2:3], in1=st[:, 3:4],
            op0=ALU.mult, op1=ALU.subtract))
        p.step("s", ["ast3"], ["ast4"], lambda e, b=eps: e.activation(
            out=st[:, 4:5], in_=st[:, 3:4], func=AF.Sqrt, bias=b, scale=-1.0))
        p.step("v", ["ast4"], ["ast4"], lambda e: e.reciprocal(out=st[:, 4:5], in_=st[:, 4:5]))
        p.step("v", ["ast4", "ast2"], ["ast5"], lambda e: e.tensor_scalar(
            out=st[:, 5:6], in0=st[:, 2:3], scalar1=st[:, 4:5], scalar2=-1.0,
            op0=ALU.mult, op1=ALU.mult))
        p.step("v", [XK, "ast4", "ast5"], ["ln_xc"], lambda e, xt=xt: e.tensor_scalar(
            out=c["ln_xc"][:], in0=xt[:], scalar1=st[:, 4:5], scalar2=st[:, 5:6],
            op0=ALU.mult, op1=ALU.add))
        for ch in range(8):
            p.step("t", ["ln_xc", "ident_bf"], ["ps_T"], lambda e, ch=ch: e.transpose(
                out=c["ps_T"][:, 0:128], in_=c["ln_xc"][:, ch * 128:(ch + 1) * 128],
                identity=ident[:]))
            p.step("v", ["ps_T"], ["l1a_xfme"], lambda e, te=te, ch=ch: e.tensor_copy(
                out=xfme[:, ch * TE + te * 128: ch * TE + (te + 1) * 128],
                in_=c["ps_T"][:, 0:128]))

    for pi, (wname, bofs) in enumerate((("Wk", 1), ("Wv", 2), ("Wq", 0))):
        j = pi % 2
        WK = f"l1a_w{j}"
        p.dma("g", [WK], ["IN"], lambda e, wname=wname, j=j: e.dma_start(
            out=c["l1a_w"][j][:].rearrange("p (k d) -> p k d", k=8),
            in_=wh[wname].rearrange("(k p) d -> p k d", p=128)), tag=f"l1aw{wname}")
        tiles = range(NE) if wname in ("Wk", "Wv") else range(1, NE - 1)
        for te in tiles:
            for db in range(2):
                psy = c["ps_y1"] if db == 0 else c["ps_y2"]
                PSY = "ps_y1" if db == 0 else "ps_y2"
                def mm(e, te=te, db=db, j=j, bofs=bofs):
                    insts = [e.matmul(
                        out=(c["ps_y1"] if db == 0 else c["ps_y2"])[:, 0:512],
                        lhsT=xfme[:, k * TE + te * 128: k * TE + (te + 1) * 128],
                        rhs=c["l1a_w"][j][:, k * D + db * 512: k * D + (db + 1) * 512],
                        start=(k == 0), stop=False) for k in range(8)]
                    insts.append(e.matmul(
                        out=(c["ps_y1"] if db == 0 else c["ps_y2"])[:, 0:512],
                        lhsT=ones[:, 0:128],
                        rhs=c["l1a_bia"][:, bofs * D + db * 512: bofs * D + (db + 1) * 512],
                        start=False, stop=True))
                    return insts
                p.step("t", ["l1a_xfme", WK, "l1a_bia", "ones_bf"], [PSY], mm)
                p.step("v", [PSY], ["l1a_ev"], lambda e, db=db, psy=psy: e.tensor_copy(
                    out=c["l1a_ev"][:, db * 512:(db + 1) * 512], in_=psy[:, 0:512]))
            if wname == "Wv":
                p.dma("q", ["VD"], ["l1a_ev"], lambda e, te=te: e.dma_start(
                    out=vdram[te], in_=c["l1a_ev"][:]), tag=f"vst{te}")
            else:
                for ch in range(8):
                    p.step("t", ["l1a_ev", "ident_bf"], ["ps_T"], lambda e, ch=ch: e.transpose(
                        out=c["ps_T"][:, 0:128], in_=c["l1a_ev"][:, ch * 128:(ch + 1) * 128],
                        identity=ident[:]))
                    if wname == "Wk":
                        p.step("v", ["ps_T"], ["l1a_aofm"], lambda e, ch=ch: e.tensor_copy(
                            out=c["l1a_aofm"][:, ch * 128:(ch + 1) * 128],
                            in_=c["ps_T"][:, 0:128]))
                    else:
                        p.step("v", ["ps_T"], ["l1a_xfme"], lambda e, te=te, ch=ch: e.tensor_copy(
                            out=xfme[:, ch * TE + te * 128: ch * TE + (te + 1) * 128],
                            in_=c["ps_T"][:, 0:128]))
                if wname == "Wk":
                    p.dma("q", ["KD"], ["l1a_aofm"], lambda e, te=te: e.dma_start(
                        out=kdram[:, :, te * 128:(te + 1) * 128],
                        in_=c["l1a_aofm"][:].rearrange("p (ch q) -> p ch q", ch=8)),
                        tag=f"kst{te}")

    wo_j = 1
    p.dma("g", ["l1a_w1"], ["IN"], lambda e: e.dma_start(
        out=c["l1a_w"][1][:].rearrange("p (k d) -> p k d", k=8),
        in_=wh["Wo"].rearrange("(k p) d -> p k d", p=128)), tag="l1awo")
    sc128 = float(1.0 / 8.0)
    for i in range(nt):
        jj = i % 2
        KWK, VWK = f"l1a_kw{jj}", f"l1a_vw{jj}"
        p.dma("q", [KWK], ["KD"], lambda e, i=i, jj=jj: e.dma_start(
            out=c["l1a_kw"][jj][:].rearrange("p (ch q) -> p ch q", ch=8),
            in_=kdram[:, :, i * 128: i * 128 + 384]), tag=f"kld{i}")
        p.dma("q", [VWK], ["VD"], lambda e, i=i, jj=jj: e.dma_start(
            out=c["l1a_vw"][jj][:].rearrange("p (w d) -> p w d", w=3),
            in_=vdram[i:i + 3].rearrange("w p d -> p w d")), tag=f"vld{i}")
        for h in range(16):
            ch, vof = h // 2, 64 * (h % 2)
            p.step("t", ["l1a_xfme", KWK], ["ps_a"], lambda e, i=i, jj=jj, ch=ch, vof=vof: e.matmul(
                out=c["ps_a"][:, 0:384],
                lhsT=xfme[vof:vof + 64, ch * TE + (i + 1) * 128: ch * TE + (i + 2) * 128],
                rhs=c["l1a_kw"][jj][vof:vof + 64, ch * 384:(ch + 1) * 384],
                start=True, stop=True))
            p.step("v", ["ps_a", "l1a_band"], ["l1a_ssb"], lambda e: e.tensor_tensor(
                out=c["l1a_ssb"][:], in0=c["ps_a"][:, 0:384], in1=c["l1a_band"][:], op=ALU.add))
            if i == 0 or i == nt - 1:
                p.step("v", ["l1a_ssb", "l1a_valid"], ["l1a_ssb"], lambda e, i=i: e.tensor_tensor(
                    out=c["l1a_ssb"][:], in0=c["l1a_ssb"][:],
                    in1=c["l1a_valid"][:, i * 128: i * 128 + 384], op=ALU.add))
            p.step("v", ["l1a_ssb"], ["ast6"], lambda e: e.tensor_reduce(
                out=st[:, 6:7], in_=c["l1a_ssb"][:], axis=AX.X, op=ALU.max))
            p.step("v", ["ast6"], ["ast7"], lambda e: e.tensor_scalar(
                out=st[:, 7:8], in0=st[:, 6:7], scalar1=-sc128, scalar2=None, op0=ALU.mult))
            p.step("s", ["l1a_ssb", "ast7"], ["l1a_pe", "arec"], lambda e, h=h: e.activation(
                out=c["l1a_pe"][:], in_=c["l1a_ssb"][:], func=AF.Exp, bias=st[:, 7:8],
                scale=sc128, accum_out=c["l1a_rec"][:, h:h + 1]))
            if "dbg1" in wh and i == 5 and h == 3 and False:
                p.dma("q", ["DBG"], ["l1a_ssb"], lambda e: e.dma_start(
                    out=wh["dbg1"], in_=c["l1a_ssb"][:]))
                p.step("v", ["l1a_pe"], ["l1a_ssb"], lambda e: e.tensor_copy(
                    out=c["l1a_ssb"][:], in_=c["l1a_pe"][:]))
                p.dma("q", ["DBG"], ["l1a_ssb"], lambda e: e.dma_start(
                    out=wh["dbg2"], in_=c["l1a_ssb"][:]))
            for kc in range(3):
                p.step("t", ["l1a_pe", "ident_bf"], ["ps_T"], lambda e, kc=kc: e.transpose(
                    out=c["ps_T"][:, 0:128], in_=c["l1a_pe"][:, kc * 128:(kc + 1) * 128],
                    identity=ident[:]))
                p.step("v", ["ps_T"], ["l1a_pT"], lambda e, kc=kc: e.tensor_copy(
                    out=c["l1a_pT"][:, kc * 128:(kc + 1) * 128], in_=c["ps_T"][:, 0:128]))
            pso = c["ps_y1"] if h < 8 else c["ps_y2"]
            PSO = "ps_y1" if h < 8 else "ps_y2"
            def mm_pv(e, h=h, jj=jj, pso=pso):
                return [e.matmul(
                    out=pso[:, (h % 8) * 64:(h % 8) * 64 + 64],
                    lhsT=c["l1a_pT"][:, kc * 128:(kc + 1) * 128],
                    rhs=c["l1a_vw"][jj][:, kc * D + h * 64: kc * D + (h + 1) * 64],
                    start=(kc == 0), stop=(kc == 2)) for kc in range(3)]
            p.step("t", ["l1a_pT", VWK], [PSO], mm_pv)
            if "dbg3" in wh and i == nt - 1 and h == 15:
                p.step("v", [PSO], ["DBGS"], lambda e, pso=pso, h=h: e.tensor_copy(
                    out=c["l1a_ssb"][:, 0:64], in_=pso[:, (h % 8) * 64:(h % 8) * 64 + 64]))
                p.dma("q", ["DBG"], ["DBGS"], lambda e: e.dma_start(
                    out=wh["dbg3"], in_=c["l1a_ssb"][:, 0:64]))
                for kc2 in range(3):
                    p.step("v", [VWK], ["DBGS"], lambda e, jj=jj, h=h, kc2=kc2: e.tensor_copy(
                        out=c["l1a_ssb"][:, kc2 * 64:(kc2 + 1) * 64],
                        in_=c["l1a_vw"][jj][:, kc2 * D + h * 64: kc2 * D + (h + 1) * 64]))
                p.dma("q", ["DBG"], ["DBGS"], lambda e: e.dma_start(
                    out=wh["dbg4"], in_=c["l1a_ssb"][:, 0:192]))
                p.step("v", ["l1a_pT"], ["DBGS2"], lambda e: e.tensor_copy(
                    out=c["l1a_pe"][:], in_=c["l1a_pT"][:]))
                p.dma("q", ["DBG"], ["DBGS2"], lambda e: e.dma_start(
                    out=wh["dbg5"], in_=c["l1a_pe"][:]))
        p.step("v", ["arec"], ["arec"], lambda e: e.reciprocal(
            out=c["l1a_rec"][:], in_=c["l1a_rec"][:]))
        for h in range(16):
            pso = c["ps_y1"] if h < 8 else c["ps_y2"]
            PSO = "ps_y1" if h < 8 else "ps_y2"
            p.step("v", [PSO, "arec"], ["l1a_ao"], lambda e, h=h, pso=pso: e.tensor_scalar(
                out=c["l1a_ao"][:, h * 64:(h + 1) * 64], in0=pso[:, (h % 8) * 64:(h % 8) * 64 + 64],
                scalar1=c["l1a_rec"][:, h:h + 1], scalar2=None, op0=ALU.mult))
        for ch2 in range(8):
            p.step("t", ["l1a_ao", "ident_bf"], ["ps_T"], lambda e, ch2=ch2: e.transpose(
                out=c["ps_T"][:, 0:128], in_=c["l1a_ao"][:, ch2 * 128:(ch2 + 1) * 128],
                identity=ident[:]))
            p.step("v", ["ps_T"], ["l1a_aofm"], lambda e, ch2=ch2: e.tensor_copy(
                out=c["l1a_aofm"][:, ch2 * 128:(ch2 + 1) * 128], in_=c["ps_T"][:, 0:128]))
        for db in range(2):
            psy = c["ps_h1"] if db == 0 else c["ps_h2"]
            PSY = "ps_h1" if db == 0 else "ps_h2"
            def mm_o(e, db=db, psy=psy):
                insts = [e.matmul(
                    out=psy[:, 0:512], lhsT=c["l1a_aofm"][:, k2 * 128:(k2 + 1) * 128],
                    rhs=c["l1a_w"][1][:, k2 * D + db * 512: k2 * D + (db + 1) * 512],
                    start=(k2 == 0), stop=False) for k2 in range(8)]
                insts.append(e.matmul(
                    out=psy[:, 0:512], lhsT=ones[:, 0:128],
                    rhs=c["l1a_bia"][:, 3 * D + db * 512: 3 * D + (db + 1) * 512],
                    start=False, stop=True))
                return insts
            p.step("t", ["l1a_aofm", "l1a_w1", "l1a_bia", "ones_bf"], [PSY], mm_o)
            p.step("v", [PSY, f"xt{i}"], [f"xt{i}"], lambda e, i=i, db=db, psy=psy:
                   e.tensor_tensor(out=c["xt"][i][:, db * 512:(db + 1) * 512],
                                   in0=psy[:, 0:512],
                                   in1=c["xt"][i][:, db * 512:(db + 1) * 512], op=ALU.add))


def run_l1_attn(x_full, Wq, bq, Wk, bk, Wv, bv, Wo, bo):
    nc = make_nc()
    p = Prog(nc)
    c = setup_common(p)
    alloc_x(p, c)
    c["ln_sq"] = p.sb("ln_sq", [128, D], BF16)
    c["ln_xc"] = p.sb("ln_xc", [128, D], BF16)
    xh = p.dram("x_in", [T_CORE, D], F32, kind="ExternalInput")
    oh = p.dram("x_out", [T_CORE, D], F32, kind="ExternalOutput")
    wh = {
        "x_ext": p.dram("x_ext", [T_CORE + 256, D], F32, kind="ExternalInput"),
        "Wq": p.dram("w_q", [D, D], F32, kind="ExternalInput"),
        "Wk": p.dram("w_k", [D, D], F32, kind="ExternalInput"),
        "Wv": p.dram("w_v", [D, D], F32, kind="ExternalInput"),
        "Wo": p.dram("w_o", [D, D], F32, kind="ExternalInput"),
        "b4": p.dram("w_b4", [1, 4, D], F32, kind="ExternalInput"),
        "band": p.dram("c_band", [128, 384], F32, kind="ExternalInput"),
        "valid": p.dram("c_valid", [128, T_CORE + 256], F32, kind="ExternalInput"),
        "dbg1": p.dram("dbg1", [128, 384], F32, kind="ExternalOutput"),
        "dbg2": p.dram("dbg2", [128, 384], F32, kind="ExternalOutput"),
        "dbg3": p.dram("dbg3", [128, 64], F32, kind="ExternalOutput"),
        "dbg4": p.dram("dbg4", [128, 192], F32, kind="ExternalOutput"),
        "dbg5": p.dram("dbg5", [128, 384], BF16, kind="ExternalOutput"),
    }
    load_x(p, c, xh)
    l1_attn(p, c, wh, uid="u")
    store_x(p, c, oh)
    p.build()
    band = np.full((128, 384), -1e30, dtype=np.float32)
    for qi in range(128):
        for kj in range(384):
            if abs(qi + 128 - kj) <= 128:
                band[qi, kj] = 0.0
    consts = common_const_inputs(p)
    xr = x_full.reshape(2, 8192, D)
    maps = []
    for core in range(N_CORES):
        b, s0 = core // 4, (core % 4) * 2048
        ext = np.zeros((2304, D), dtype=np.float32)
        lo, hi = s0 - 128, s0 + 2048 + 128
        slo, shi = max(lo, 0), min(hi, 8192)
        ext[slo - lo: shi - lo] = xr[b, slo:shi]
        valid = np.zeros((128, 2304), dtype=np.float32)
        if lo < 0:
            valid[:, :128] = -1e30
        if hi > 8192:
            valid[:, -128:] = -1e30
        m = {"x_in": np.ascontiguousarray(xr[b, s0:s0 + 2048]),
             "x_ext": ext, "w_q": Wq, "w_k": Wk, "w_v": Wv, "w_o": Wo,
             "w_b4": np.ascontiguousarray(np.stack([bq, bk, bv, bo])[None]),
             "c_band": band, "c_valid": valid}
        m.update(consts)
        maps.append(m)
    res = run_bass_kernel_spmd(nc, maps, core_ids=list(range(N_CORES)))
    globals()['L1A_DBG'] = [(r.get("dbg1"), r.get("dbg2"), r.get("dbg3"), r.get("dbg4"), r.get("dbg5")) for r in res.results]
    return np.concatenate([r["x_out"] for r in res.results], axis=0)


def run_l2_attn_P(x_full, WwT):
    nc = make_nc()
    p = Prog(nc)
    c = setup_common(p)
    alloc_x(p, c)
    xh = p.dram("x_in", [T_CORE, D], F32, kind="ExternalInput")
    xlh = p.dram("xl_out", [T_CORE, D], BF16, kind="ExternalOutput")
    vh = p.dram("v_out", [T_CORE, D], BF16, kind="ExternalOutput")
    wwh = p.dram("w_ww", [D, D], F32, kind="ExternalInput")
    load_x(p, c, xh)
    emit_ln_fm(p, c, want_tm=True)
    T, nt = c["T"], c["nt"]
    x_fm = c["x_fm"]
    ww = p.sb("ww", [128, 8 * D], BF16)
    ob = p.sb("ob", [128, D], BF16)
    p.dma("g", ["ww"], ["IN"], lambda e: e.dma_start(
        out=ww[:].rearrange("p (k d) -> p k d", k=8),
        in_=wwh.rearrange("(k p) d -> p k d", p=128)))
    for i in range(nt):
        p.dma("q", ["OUT"], [f"xc{i}"], lambda e, i=i: e.dma_start(
            out=xlh[i * 128:(i + 1) * 128, :], in_=c["xc"][i][:]))
        for db in range(2):
            psy = c["ps_y1"] if db == 0 else c["ps_y2"]
            PSY = "ps_y1" if db == 0 else "ps_y2"
            def mm(e, i=i, db=db, psy=psy):
                return [e.matmul(out=psy[:, 0:512],
                                 lhsT=x_fm[:, k * T + i * 128: k * T + (i + 1) * 128],
                                 rhs=ww[:, k * D + db * 512: k * D + (db + 1) * 512],
                                 start=(k == 0), stop=(k == 7)) for k in range(8)]
            p.step("t", ["x_fm", "ww"], [PSY], mm)
            p.step("v", [PSY], ["ob"], lambda e, db=db, psy=psy: e.tensor_copy(
                out=ob[:, db * 512:(db + 1) * 512], in_=psy[:, 0:512]))
        p.dma("q", ["OUT"], ["ob"], lambda e, i=i: e.dma_start(
            out=vh[i * 128:(i + 1) * 128, :], in_=ob[:]))
    p.build()
    maps = _in_maps_tokenshard(p, x_full, {"w_ww": WwT})
    res = run_bass_kernel_spmd(nc, maps, core_ids=list(range(N_CORES)))
    xl = np.concatenate([r["xl_out"] for r in res.results], axis=0)
    v = np.concatenate([r["v_out"] for r in res.results], axis=0)
    return xl, v


def run_l2_attn_H(xl_bf, v_bf, decay, log_alpha):
    S, d, C = 8192, 256, 64
    NCH, NT = S // C, S // 128
    nc = make_nc()
    p = Prog(nc)
    xh_h = p.dram("xh_in", [S, d], BF16, kind="ExternalInput")
    vv_h = p.dram("vv_in", [S, d], BF16, kind="ExternalInput")
    dec_h = p.dram("dec_in", [1, 2], F32, kind="ExternalInput")
    o_h = p.dram("o_out", [S, d], BF16, kind="ExternalOutput")
    dbgS_h = p.dram("dbgS", [8, 128, 2 * d], F32, kind="ExternalOutput")
    dbgA_h = p.dram("dbgA", [8, 64, 64], F32, kind="ExternalOutput")
    dbgG_h = p.dram("dbgG", [64, 16], F32, kind="ExternalOutput")
    i64f_h = p.dram("c_i64f", [64, 64], F32, kind="ExternalInput")
    dL_h = p.dram("c_dL", [64, 64], F32, kind="ExternalInput")
    dU_h = p.dram("c_dU", [64, 64], F32, kind="ExternalInput")
    sL_h = p.dram("c_sL", [64, 64], F32, kind="ExternalInput")
    sU_h = p.dram("c_sU", [64, 64], F32, kind="ExternalInput")
    zrc_h = p.dram("c_zrc", [64, 64], F32, kind="ExternalInput")
    zr_h = p.dram("c_zr", [64, 64], F32, kind="ExternalInput")
    pcol_h = p.dram("c_pcol", [64, 2], F32, kind="ExternalInput")
    onesf_h = p.dram("c_onesf1", [1, 128], F32, kind="ExternalInput")

    xt = p.sb("xt", [128, d], F32)
    xsq = p.sb("xsq", [128, d], F32)
    rk_fm = p.sb("rk_fm", [128, 2 * S], F32)
    wk_fm = p.sb("wk_fm", [128, 2 * S], F32)
    vrow = [p.sb(f"vrow{j}", [64, d], F32) for j in range(2)]
    rrow = [p.sb(f"rrow{j}", [64, d], F32) for j in range(2)]
    wrow = [p.sb(f"wrow{j}", [64, d], F32) for j in range(2)]
    nrm = p.sb("nrm", [128, 4], F32)
    i64f = p.sb("i64f", [64, 64], F32)
    dL = p.sb("dL", [64, 64], F32)
    dU = p.sb("dU", [64, 64], F32)
    sL = p.sb("sL", [64, 64], F32)
    sU = p.sb("sU", [64, 64], F32)
    zrc = p.sb("zrc", [64, 64], F32)
    zr = p.sb("zr", [64, 64], F32)
    pcol = p.sb("pcol", [64, 2], F32)
    onesf = p.sb("onesf", [1, 128], F32)
    gam = p.sb("gam", [1, 8], F32)
    gamc = p.sb("gamc", [128, 4], F32)
    LLn = p.sb("LLn", [64, 64], F32)
    LLn0 = p.sb("LLn0", [64, 64], F32)
    LUn = p.sb("LUn", [64, 64], F32)
    LUn0 = p.sb("LUn0", [64, 64], F32)
    LUp = p.sb("LUp", [64, 64], F32)
    LUp0 = p.sb("LUp0", [64, 64], F32)
    gp = p.sb("gp", [64, 2], F32)
    NXf = p.sb("NXf", [64, 64], F32)
    Pk = p.sb("Pk", [64, 64], F32)
    PkT = p.sb("PkT", [64, 64], F32)
    XT = p.sb("XT", [64, 64], F32)
    AT = [p.sb(f"AT{j}", [64, 64], F32) for j in range(2)]
    ITT = [p.sb(f"ITT{j}", [64, 64], F32) for j in range(2)]
    rkg = p.sb("rkg", [64, d], F32)
    rkgT = p.sb("rkgT", [128, 2 * 64], F32)
    wkg = p.sb("wkg", [64, d], F32)
    wkcT = p.sb("wkcT", [128, 2 * 64], F32)
    vnew = p.sb("vnew", [64, d], F32)
    Ssb = p.sb("Ssb", [128, 2 * d], F32)
    
    osb = p.sb("osb", [64, d], BF16)
    identb = p.sb("identb", [128, 128], F32)
    idb_h = p.dram("c_identf", [128, 128], F32, kind="ExternalInput")
    ps1 = p.ps("ps1", [128, 512], F32)
    ps2 = p.ps("ps2", [128, 512], F32)
    ps3 = p.ps("ps3", [128, 512], F32)
    ps4 = p.ps("ps4", [128, 512], F32)
    pst = p.ps("pst", [128, 512], F32)

    for (t, key, hh) in ((i64f, "i64f", i64f_h), (dL, "dL", dL_h), (dU, "dU", dU_h),
                         (sL, "sL", sL_h), (sU, "sU", sU_h), (zrc, "zrc", zrc_h),
                         (zr, "zr", zr_h), (pcol, "pcol", pcol_h), (onesf, "onesf", onesf_h)):
        p.dma("q", [key], ["IN"], lambda e, t=t, hh=hh: e.dma_start(out=t[:], in_=hh))
    p.dma("q", ["dec"], ["IN"], lambda e: e.dma_start(out=gam[:, 0:2], in_=dec_h))
    p.dma("q", ["identf"], ["IN"], lambda e: e.dma_start(out=identb[:], in_=idb_h))
    p.step("s", ["dec"], ["gam2"], lambda e: e.activation(
        out=gam[:, 2:3], in_=gam[:, 0:1], func=AF.Sigmoid))
    p.step("s", ["gam2"], ["gam3"], lambda e: e.activation(
        out=gam[:, 3:4], in_=gam[:, 2:3], func=AF.Ln))
    p.step("s", ["gam3"], ["gam4"], lambda e: e.activation(
        out=gam[:, 4:5], in_=gam[:, 3:4], func=AF.Exp, scale=64.0))
    p.step("s", ["dec"], ["gam5"], lambda e: e.activation(
        out=gam[:, 5:6], in_=gam[:, 1:2], func=AF.Exp))
    def mm_bc(e):
        return [e.matmul(out=ps1[0:128, 0:1], lhsT=onesf[:], rhs=gam[:, 3:4], start=True, stop=False),
                e.matmul(out=ps1[0:128, 1:2], lhsT=onesf[:], rhs=gam[:, 4:5], start=False, stop=False),
                e.matmul(out=ps1[0:128, 2:3], lhsT=onesf[:], rhs=gam[:, 5:6], start=False, stop=True)]
    p.step("t", ["gam3", "gam4", "gam5", "onesf"], ["ps1"], mm_bc)
    p.step("v", ["ps1"], ["gamc"], lambda e: e.tensor_copy(out=gamc[:, 0:3], in_=ps1[0:128, 0:3]))

    for (dst, dst_key, dd, ss_, sign) in ((LLn, "LLn", dL, sL, -1.0), (LUn, "LUn", dU, sU, -1.0),
                                          (LUp, "LUp", dU, sU, 1.0)):
        p.step("v", ["gamc", "dL", "dU"], [dst_key], lambda e, dst=dst, dd=dd: e.tensor_scalar(
            out=dst[:], in0=dd[:], scalar1=gamc[0:64, 0:1], scalar2=None, op0=ALU.mult))
        p.step("s", [dst_key], [dst_key], lambda e, dst=dst: e.activation(
            out=dst[:], in_=dst[:], func=AF.Exp))
        p.step("v", [dst_key, "sL", "sU"], [dst_key], lambda e, dst=dst, ss_=ss_, sign=sign:
               e.scalar_tensor_tensor(out=dst[:], in0=dst[:], scalar=sign, in1=ss_[:],
                                      op0=ALU.mult, op1=ALU.mult))
    p.step("v", ["LLn", "zrc"], ["LLn0"], lambda e: e.tensor_tensor(
        out=LLn0[:], in0=LLn[:], in1=zrc[:], op=ALU.mult))
    p.step("v", ["LUn", "zrc"], ["LUn0"], lambda e: e.tensor_tensor(
        out=LUn0[:], in0=LUn[:], in1=zrc[:], op=ALU.mult))
    p.step("v", ["LUp", "zr"], ["LUp0"], lambda e: e.tensor_tensor(
        out=LUp0[:], in0=LUp[:], in1=zr[:], op=ALU.mult))
    p.step("v", ["pcol", "gamc"], ["gp"], lambda e: e.tensor_scalar(
        out=gp[:], in0=pcol[:], scalar1=gamc[0:64, 0:1], scalar2=None, op0=ALU.mult))
    p.step("s", ["gp"], ["gp"], lambda e: e.activation(out=gp[:], in_=gp[:], func=AF.Exp))
    p.dma("q", ["DBGG"], ["gamc"], lambda e: e.dma_start(out=dbgG_h[:, 0:4], in_=gamc[0:64, :], allow_slow_non_contiguous=True))
    p.dma("q", ["DBGG"], ["gp"], lambda e: e.dma_start(out=dbgG_h[:, 4:6], in_=gp[:], allow_slow_non_contiguous=True))
    p.dma("q", ["DBGG"], ["LLn"], lambda e: e.dma_start(out=dbgG_h[:, 6:7], in_=LLn[:, 1:2], allow_slow_non_contiguous=True))
    p.dma("q", ["DBGG"], ["LUp"], lambda e: e.dma_start(out=dbgG_h[:, 7:8], in_=LUp[:, 1:2], allow_slow_non_contiguous=True))

    rkdram = p.dram("rkdram", [S, d], F32)
    for i in range(NT):
        p.dma("g", ["xt"], ["IN"], lambda e, i=i: e.dma_start(
            out=xt[:], in_=xh_h[i * 128:(i + 1) * 128, :]), tag=f"xh{i}")
        p.step("s", ["xt"], ["xsq", "nrm0"], lambda e: e.activation(
            out=xsq[:], in_=xt[:], func=AF.Square, accum_out=nrm[:, 0:1]))
        p.step("s", ["nrm0"], ["nrm1"], lambda e: e.activation(
            out=nrm[:, 1:2], in_=nrm[:, 0:1], func=AF.Sqrt))
        p.step("v", ["nrm1"], ["nrm1"], lambda e: e.tensor_single_scalar(
            out=nrm[:, 1:2], in_=nrm[:, 1:2], scalar=1e-12, op=ALU.max))
        p.step("v", ["nrm1"], ["nrm2"], lambda e: e.reciprocal(out=nrm[:, 2:3], in_=nrm[:, 1:2]))
        p.step("v", ["xt", "nrm2"], ["xsq"], lambda e: e.tensor_scalar(
            out=xsq[:], in0=xt[:], scalar1=nrm[:, 2:3], scalar2=None, op0=ALU.mult))
        p.dma("q", ["rkd"], ["xsq"], lambda e, i=i: e.dma_start(
            out=rkdram[i * 128:(i + 1) * 128, :], in_=xsq[:]), tag=f"rks{i}")
        for cch in range(2):
            p.step("t", ["xsq", "identf"], ["pst"], lambda e, cch=cch: e.transpose(
                out=pst[:, 0:128], in_=xsq[:, cch * 128:(cch + 1) * 128],
                identity=identb[:]))
            p.step("v", ["pst"], ["rk_fm"], lambda e, i=i, cch=cch: e.tensor_copy(
                out=rk_fm[:, cch * S + i * 128:cch * S + (i + 1) * 128], in_=pst[:, 0:128]))
    for i in range(NT):
        if i == 0:
            p.step("v", [], ["xt"], lambda e: e.memset(xt[:], 0.0))
            p.dma("q", ["xt"], ["rkd"], lambda e: e.dma_start(
                out=xt[1:128, :], in_=rkdram[0:127, :]), tag="wkl0")
        else:
            p.dma("q", ["xt"], ["rkd"], lambda e, i=i: e.dma_start(
                out=xt[:], in_=rkdram[i * 128 - 1: i * 128 + 127, :]), tag=f"wkl{i}")
        for cch in range(2):
            p.step("t", ["xt", "identf"], ["pst"], lambda e, cch=cch: e.transpose(
                out=pst[:, 0:128], in_=xt[:, cch * 128:(cch + 1) * 128],
                identity=identb[:]))
            p.step("v", ["pst"], ["wk_fm"], lambda e, i=i, cch=cch: e.tensor_copy(
                out=wk_fm[:, cch * S + i * 128:cch * S + (i + 1) * 128], in_=pst[:, 0:128]))

    p.step("v", [], ["Ssb"], lambda e: e.memset(Ssb[:], 0.0))

    for j in range(NCH):
        jb = j % 2
        ATK, ITK = f"AT{jb}", f"ITT{jb}"
        LNT = "LUn0" if j == 0 else "LUn"
        LNTt = LUn0 if j == 0 else LUn
        LIP = "LUp0" if j == 0 else "LUp"
        LIPt = LUp0 if j == 0 else LUp
        def mm_gram(e, j=j):
            return [e.matmul(out=ps1[0:64, 0:64],
                             lhsT=wk_fm[:, cch * S + j * 64: cch * S + (j + 1) * 64],
                             rhs=wk_fm[:, cch * S + j * 64: cch * S + (j + 1) * 64],
                             start=(cch == 0), stop=(cch == 1)) for cch in range(2)]
        p.step("t", ["wk_fm"], ["ps1"], mm_gram)
        p.step("v", ["ps1", LNT], ["NXf"], lambda e, LNTt=LNTt: e.tensor_tensor(
            out=NXf[:], in0=ps1[0:64, 0:64], in1=LNTt[:], op=ALU.mult))
        p.step("v", ["NXf", "i64f"], ["XT"], lambda e: e.tensor_tensor(
            out=XT[:], in0=NXf[:], in1=i64f[:], op=ALU.add))
        p.step("v", ["NXf"], ["PkT"], lambda e: e.tensor_copy(out=PkT[:], in_=NXf[:]))
        LNL = "LLn0" if j == 0 else "LLn"
        LNLt = LLn0 if j == 0 else LLn
        p.step("v", ["ps1", LNL], ["NXf"], lambda e, LNLt=LNLt: e.tensor_tensor(
            out=NXf[:], in0=ps1[0:64, 0:64], in1=LNLt[:], op=ALU.mult))
        p.step("v", ["NXf"], ["Pk"], lambda e: e.tensor_copy(out=Pk[:], in_=NXf[:]))
        for lev in range(1, 6):
            p.step("t", ["Pk", "PkT"], ["ps2"], lambda e: [
                e.matmul(out=ps2[0:64, 64:128], lhsT=PkT[:], rhs=Pk[:], start=True, stop=True),
                e.matmul(out=ps2[0:64, 128:192], lhsT=Pk[:], rhs=PkT[:], start=True, stop=True)][-1])
            p.step("v", ["ps2"], ["Pk"], lambda e: e.tensor_copy(
                out=Pk[:], in_=ps2[0:64, 64:128]))
            p.step("v", ["ps2"], ["PkT"], lambda e: e.tensor_copy(
                out=PkT[:], in_=ps2[0:64, 128:192]))
            p.step("t", ["Pk", "XT"], ["ps2"], lambda e: e.matmul(
                out=ps2[0:64, 0:64], lhsT=Pk[:], rhs=XT[:], start=True, stop=True))
            p.step("v", ["ps2", "XT"], ["XT"], lambda e: e.tensor_tensor(
                out=XT[:], in0=ps2[0:64, 0:64], in1=XT[:], op=ALU.add))
        p.step("v", ["XT"], [ATK], lambda e, jb=jb: e.tensor_copy(out=AT[jb][:], in_=XT[:]))
        def mm_gram2(e, j=j):
            return [e.matmul(out=ps1[0:64, 64:128],
                             lhsT=wk_fm[:, cch * S + j * 64: cch * S + (j + 1) * 64],
                             rhs=rk_fm[:, cch * S + j * 64: cch * S + (j + 1) * 64],
                             start=(cch == 0), stop=(cch == 1)) for cch in range(2)]
        p.step("t", ["wk_fm", "rk_fm"], ["ps1"], mm_gram2)
        p.step("v", ["ps1", LIP], [ITK], lambda e, LIPt=LIPt, jb=jb: e.tensor_tensor(
            out=ITT[jb][:], in0=ps1[0:64, 64:128], in1=LIPt[:], op=ALU.mult))

        jb2 = j % 2
        VR, RR, WR = f"vrow{jb2}", f"rrow{jb2}", f"wrow{jb2}"
        p.dma("g", [VR], ["IN"], lambda e, j=j, jb2=jb2: e.dma_start(
            out=vrow[jb2][:], in_=vv_h[j * 64:(j + 1) * 64, :]), tag=f"vl{j}")
        p.dma("q", [RR], ["rkd"], lambda e, j=j, jb2=jb2: e.dma_start(
            out=rrow[jb2][:], in_=rkdram[j * 64:(j + 1) * 64, :]), tag=f"rl{j}")
        if j == 0:
            p.step("v", [], [WR], lambda e: e.memset(wrow[0][:], 0.0))
            p.dma("q", [WR], ["rkd"], lambda e: e.dma_start(
                out=wrow[0][1:64, :], in_=rkdram[0:63, :]), tag="wl0")
        else:
            p.dma("q", [WR], ["rkd"], lambda e, j=j, jb2=jb2: e.dma_start(
                out=wrow[jb2][:], in_=rkdram[j * 64 - 1: (j + 1) * 64 - 1, :]), tag=f"wl{j}")
        if j == 8:
            p.dma("q", ["DBGR"], [RR], lambda e, jb2=jb2: e.dma_start(
                out=dbgG_h[:, 8:10], in_=rrow[jb2][0:64, 0:2], allow_slow_non_contiguous=True))
            p.dma("q", ["DBGR"], [WR], lambda e, jb2=jb2: e.dma_start(
                out=dbgG_h[:, 10:12], in_=wrow[jb2][0:64, 0:2], allow_slow_non_contiguous=True))
            p.dma("q", ["DBGR"], [VR], lambda e, jb2=jb2: e.dma_start(
                out=dbgG_h[:, 12:14], in_=vrow[jb2][0:64, 0:2], allow_slow_non_contiguous=True))
        def mm_wkc(e, jb=jb, jb2=jb2):
            return [e.matmul(out=ps2[:, 0 + cch * 64: (cch + 1) * 64],
                             lhsT=wrow[jb2][:, cch * 128:(cch + 1) * 128],
                             rhs=AT[jb][:], start=True, stop=True) for cch in range(2)]
        p.step("t", [WR, ATK], ["ps2"], mm_wkc)
        p.step("v", ["ps2"], ["wkcT"], lambda e: e.tensor_scalar(
            out=wkcT[:], in0=ps2[:, 0:128], scalar1=-1.0, scalar2=None, op0=ALU.mult))
        def mm_vnew(e, jb=jb, jb2=jb2):
            insts = [e.matmul(out=ps3[0:64, 0:256], lhsT=AT[jb][:], rhs=vrow[jb2][:],
                              start=True, stop=False)]
            for cch in range(2):
                insts.append(e.matmul(
                    out=ps3[0:64, 0:256], lhsT=wkcT[:, cch * 64:(cch + 1) * 64],
                    rhs=Ssb[:, cch * d:(cch + 1) * d], start=False, stop=(cch == 1)))
            return insts
        p.step("t", [ATK, VR, "wkcT", "Ssb"], ["ps3"], mm_vnew)
        p.step("v", ["ps3"], ["vnew"], lambda e: e.tensor_copy(out=vnew[:], in_=ps3[0:64, 0:256]))
        p.step("v", [RR, "gp"], ["rkg"], lambda e, jb2=jb2: e.tensor_scalar(
            out=rkg[:], in0=rrow[jb2][:], scalar1=gp[:, 0:1], scalar2=None, op0=ALU.mult))
        p.step("v", [WR, "gp"], ["wkg"], lambda e, jb2=jb2: e.tensor_scalar(
            out=wkg[:], in0=wrow[jb2][:], scalar1=gp[:, 1:2], scalar2=None, op0=ALU.mult))
        for cch in range(2):
            p.step("t", ["rkg", "identf"], ["pst"], lambda e, cch=cch: e.transpose(
                out=pst[:, 0:64], in_=rkg[:, cch * 128:(cch + 1) * 128],
                identity=identb[0:64, 0:64]))
            p.step("v", ["pst"], ["rkgT"], lambda e, cch=cch: e.tensor_copy(
                out=rkgT[:, cch * 64:(cch + 1) * 64], in_=pst[:, 0:64]))
        def mm_o(e, jb=jb):
            insts = [e.matmul(out=ps4[0:64, 0:256], lhsT=rkgT[:, cch * 64:(cch + 1) * 64],
                              rhs=Ssb[:, cch * d:(cch + 1) * d],
                              start=(cch == 0), stop=False) for cch in range(2)]
            insts.append(e.matmul(out=ps4[0:64, 0:256], lhsT=ITT[jb][:], rhs=vnew[:],
                                  start=False, stop=True))
            return insts
        p.step("t", ["rkgT", "Ssb", ITK, "vnew"], ["ps4"], mm_o)
        p.step("v", ["ps4", "gamc"], ["osb"], lambda e: e.tensor_scalar(
            out=osb[:], in0=ps4[0:64, 0:256], scalar1=gamc[0:64, 2:3], scalar2=None, op0=ALU.mult))
        p.dma("q", ["OUT"], ["osb"], lambda e, j=j: e.dma_start(
            out=o_h[j * 64:(j + 1) * 64, :], in_=osb[:]), tag=f"ost{j}")
        if j == 8:
            p.dma("q", ["DBGR"], ["vnew"], lambda e: e.dma_start(
                out=dbgG_h[:, 14:16], in_=vnew[:, 0:2], allow_slow_non_contiguous=True))
            p.dma("g", ["DBGR"], ["osb"], lambda e: e.dma_start(
                out=dbgG_h[:, 6:8], in_=osb[:, 0:2], allow_slow_non_contiguous=True))
        def mm_su(e):
            return [e.matmul(out=(ps1 if cch == 0 else ps2)[:, 256:512],
                             lhsT=wkg[:, cch * 128:(cch + 1) * 128], rhs=vnew[:],
                             start=True, stop=True) for cch in range(2)]
        p.step("t", ["wkg", "vnew"], ["ps1", "ps2"], mm_su)
        if j in (0, 1, 2, 3, 8, 16, 32, 64):
            _dbi = (0, 1, 2, 3, 8, 16, 32, 64).index(j)
            p.dma("q", ["DBG"], [ATK], lambda e, jb=jb, _dbi=_dbi: e.dma_start(
                out=dbgA_h[_dbi], in_=AT[jb][:]))
        for cch in range(2):
            psx = ps1 if cch == 0 else ps2
            p.step("v", ["ps1", "ps2", "Ssb", "gamc"], ["Ssb"], lambda e, cch=cch, psx=psx:
                   e.scalar_tensor_tensor(out=Ssb[:, cch * d:(cch + 1) * d],
                                          in0=Ssb[:, cch * d:(cch + 1) * d],
                                          scalar=gamc[:, 1:2], in1=psx[:, 256:512],
                                          op0=ALU.mult, op1=ALU.add))

    p.build()
    pq = np.arange(64, dtype=np.float32)
    consts = {
        "c_i64f": np.eye(64, dtype=np.float32),
        "c_dL": np.maximum(pq[:, None] - pq[None, :], 0).astype(np.float32),
        "c_dU": np.maximum(pq[None, :] - pq[:, None], 0).astype(np.float32),
        "c_sL": np.tril(np.ones((64, 64), np.float32), -1),
        "c_sU": np.triu(np.ones((64, 64), np.float32), 1),
        "c_zrc": np.pad(np.ones((63, 63), np.float32), ((1, 0), (1, 0))),
        "c_zr": np.pad(np.ones((63, 64), np.float32), ((1, 0), (0, 0))),
        "c_pcol": np.stack([pq, 63 - pq], axis=1).astype(np.float32),
        "c_onesf1": np.ones((1, 128), np.float32),
        "c_identf": np.eye(128, dtype=np.float32),
    }
    maps = []
    xr = xl_bf.reshape(2, 8192, D)
    vr = v_bf.reshape(2, 8192, D)
    for core in range(N_CORES):
        b, h = core // 4, core % 4
        m = {"xh_in": np.ascontiguousarray(xr[b, :, h * 256:(h + 1) * 256]),
             "vv_in": np.ascontiguousarray(vr[b, :, h * 256:(h + 1) * 256]),
             "dec_in": np.array([[decay[h], log_alpha[h]]], dtype=np.float32)}
        m.update(consts)
        if p.konst_tile is not None:
            m["c_konst"] = p.konst_input()
        maps.append(m)
    res = run_bass_kernel_spmd(nc, maps, core_ids=list(range(N_CORES)))
    globals()["L2_DBG"] = [(r.get("dbgS"), r.get("dbgA"), r.get("dbgG")) for r in res.results]
    o = np.zeros((2, 8192, D), dtype=BF)
    for core in range(N_CORES):
        b, h = core // 4, core % 4
        o[b, :, h * 256:(h + 1) * 256] = res.results[core]["o_out"]
    return o.reshape(16384, D)


def run_l2_attn_O(x_full, o_all, xl_bf, WrT):
    nc = make_nc()
    p = Prog(nc)
    c = setup_common(p)
    alloc_x(p, c)
    xh = p.dram("x_in", [T_CORE, D], F32, kind="ExternalInput")
    oh = p.dram("x_out", [T_CORE, D], F32, kind="ExternalOutput")
    ofm_h = p.dram("ofm_in", [128, 8 * T_CORE], BF16, kind="ExternalInput")
    xl_h = p.dram("xl_in", [T_CORE, D], BF16, kind="ExternalInput")
    wr_h = p.dram("w_wr", [D, D], F32, kind="ExternalInput")
    load_x(p, c, xh)
    T, nt = c["T"], c["nt"]
    ofm = p.sb("ofm", [128, 8 * T], BF16)
    wr = p.sb("wr", [128, 8 * D], BF16)
    xlt = p.sb("xlt", [128, D], BF16)
    for i in range(nt):
        p.dma("q", ["xlt"], ["IN"], lambda e, i=i: e.dma_start(
            out=xlt[:], in_=xl_h[i * 128:(i + 1) * 128, :]), tag=f"xl{i}")
        p.step("v", ["xlt", f"xt{i}"], [f"xt{i}"], lambda e, i=i: e.tensor_tensor(
            out=c["xt"][i][:], in0=xlt[:], in1=c["xt"][i][:], op=ALU.add))
    p.dma("q", ["ofm"], ["IN"], lambda e: e.dma_start(out=ofm[:], in_=ofm_h))
    p.dma("g", ["wr"], ["IN"], lambda e: e.dma_start(
        out=wr[:].rearrange("p (k d) -> p k d", k=8),
        in_=wr_h.rearrange("(k p) d -> p k d", p=128)))
    for i in range(nt):
        for db in range(2):
            psy = c["ps_y1"] if db == 0 else c["ps_y2"]
            PSY = "ps_y1" if db == 0 else "ps_y2"
            def mm(e, i=i, db=db, psy=psy):
                return [e.matmul(out=psy[:, 0:512],
                                 lhsT=ofm[:, k * T + i * 128: k * T + (i + 1) * 128],
                                 rhs=wr[:, k * D + db * 512: k * D + (db + 1) * 512],
                                 start=(k == 0), stop=(k == 7)) for k in range(8)]
            p.step("t", ["ofm", "wr"], [PSY], mm)
            p.step("v", [PSY, f"xt{i}"], [f"xt{i}"], lambda e, i=i, db=db, psy=psy:
                   e.tensor_tensor(out=c["xt"][i][:, db * 512:(db + 1) * 512],
                                   in0=psy[:, 0:512],
                                   in1=c["xt"][i][:, db * 512:(db + 1) * 512], op=ALU.add))
    store_x(p, c, oh)
    p.build()
    maps = []
    consts = common_const_inputs(p)
    orr = o_all.reshape(2, 8192, D)
    for core in range(N_CORES):
        b, s0 = core // 4, (core % 4) * 2048
        ofm_c = np.ascontiguousarray(
            orr[b, s0:s0 + 2048, :].T.reshape(8, 128, 2048).transpose(1, 0, 2).reshape(128, 8 * 2048))
        m = {"x_in": np.ascontiguousarray(x_full[core * T_CORE:(core + 1) * T_CORE]),
             "ofm_in": ofm_c, "w_wr": WrT,
             "xl_in": np.ascontiguousarray(
                 xl_bf.reshape(2, 8192, D)[b, s0:s0 + 2048])}
        m.update(consts)
        maps.append(m)
    res = run_bass_kernel_spmd(nc, maps, core_ids=list(range(N_CORES)))
    return np.concatenate([r["x_out"] for r in res.results], axis=0)


def _l3a_lambda_tables(p, c, wh):
    TP = 2 * np.pi
    C1, C2 = 6.28125, float(2 * np.pi - 6.28125)
    fcol = c["l3_fcol"]; dcol = c["l3_dcol"]
    p.step("s", ["l3_dcol"], ["l3_dec"], lambda e: e.activation(
        out=c["l3_dec"][:], in_=dcol[:], func=AF.Exp))
    p.step("v", ["l3_dec"], ["l3_dec"], lambda e: e.tensor_single_scalar(
        out=c["l3_dec"][:], in_=c["l3_dec"][:], scalar=-1.0, op=ALU.mult))
    p.step("v", ["l3_fcol"], ["l3_fq"], lambda e: e.tensor_single_scalar(
        out=c["l3_fq"][:], in_=fcol[:], scalar=TP, op=ALU.mult))

    def build(dst, dst_key, expo, expo_key, nparts, use_offs, use_sgn, scratch, scratch_key):
        W = expo.shape[1] if hasattr(expo, "shape") else 64
        S1, S2 = scratch, c["l3_s2"]
        fq = c["l3_fq"]; dc = c["l3_dec"]
        p.step("v", [expo_key, "l3_fq"], [scratch_key], lambda e: e.tensor_scalar(
            out=S1[0:nparts, 0:W], in0=expo, scalar1=fq[0:nparts, 0:1], scalar2=None,
            op0=ALU.mult))
        if use_offs:
            p.step("v", [scratch_key, "l3_offs"], [scratch_key], lambda e: e.tensor_scalar(
                out=S1[0:nparts, 0:W], in0=S1[0:nparts, 0:W],
                scalar1=c["l3_offs"][0:nparts, 0:1], scalar2=None, op0=ALU.add))
        p.step("v", [scratch_key], ["l3_s2"], lambda e: e.tensor_scalar(
            out=S2[0:nparts, 0:W], in0=S1[0:nparts, 0:W], scalar1=1.0 / TP, scalar2=QC,
            op0=ALU.mult, op1=ALU.add))
        p.step("v", ["l3_s2"], ["l3_s2"], lambda e: e.tensor_single_scalar(
            out=S2[0:nparts, 0:W], in_=S2[0:nparts, 0:W], scalar=QC, op=ALU.subtract))
        p.step("v", ["l3_s2", scratch_key], [scratch_key], lambda e: e.scalar_tensor_tensor(
            out=S1[0:nparts, 0:W], in0=S2[0:nparts, 0:W], scalar=-C1, in1=S1[0:nparts, 0:W],
            op0=ALU.mult, op1=ALU.add))
        p.step("v", ["l3_s2", scratch_key], [scratch_key], lambda e: e.scalar_tensor_tensor(
            out=S1[0:nparts, 0:W], in0=S2[0:nparts, 0:W], scalar=-C2, in1=S1[0:nparts, 0:W],
            op0=ALU.mult, op1=ALU.add))
        p.step("s", [scratch_key], [scratch_key], lambda e: e.activation(
            out=S1[0:nparts, 0:W], in_=S1[0:nparts, 0:W], func=AF.Sin))
        p.step("v", [expo_key, "l3_dec"], ["l3_s2"], lambda e: e.tensor_scalar(
            out=S2[0:nparts, 0:W], in0=expo, scalar1=dc[0:nparts, 0:1], scalar2=None,
            op0=ALU.mult))
        p.step("s", ["l3_s2"], ["l3_s2"], lambda e: e.activation(
            out=S2[0:nparts, 0:W], in_=S2[0:nparts, 0:W], func=AF.Exp))
        if use_sgn:
            p.step("v", ["l3_s2", "l3_sgn"], ["l3_s2"], lambda e: e.tensor_scalar(
                out=S2[0:nparts, 0:W], in0=S2[0:nparts, 0:W],
                scalar1=c["l3_sgn"][0:nparts, 0:1], scalar2=None, op0=ALU.mult))
        p.step("v", [scratch_key, "l3_s2"], [dst_key], lambda e: e.tensor_tensor(
            out=dst, in0=S1[0:nparts, 0:W], in1=S2[0:nparts, 0:W], op=ALU.mult))

    build(c["l3_tre63"][:], "l3_tre63", c["l3_e63"][0:64, :], "l3_e63", 64, True, False,
          c["l3_s1"], "l3_s1")
    build(c["l3_tim63"][:], "l3_tim63", c["l3_e63"][0:64, :], "l3_e63", 64, False, False,
          c["l3_s1"], "l3_s1")
    build(c["l3_lemre"][:], "l3_lemre", c["l3_e1"][0:64, :], "l3_e1", 64, True, False,
          c["l3_s1"], "l3_s1")
    build(c["l3_lemim"][:], "l3_lemim", c["l3_e1"][0:64, :], "l3_e1", 64, False, False,
          c["l3_s1"], "l3_s1")
    p.step("v", ["l3_lemim"], ["l3_lemim"], lambda e: e.tensor_single_scalar(
        out=c["l3_lemim"][:], in_=c["l3_lemim"][:], scalar=-1.0, op=ALU.mult))
    build(c["l3_ldir"][:], "l3_ldir", c["l3_edir"][:], "l3_edir", 64, True, False,
          c["l3_s1"], "l3_s1")
    build(c["l3_l64"][:, 0:1], "l3_l64", c["l3_e64"][0:64, :], "l3_e64", 64, True, False,
          c["l3_s1"], "l3_s1")
    build(c["l3_l64"][:, 1:2], "l3_l64", c["l3_e64"][0:64, :], "l3_e64", 64, False, False,
          c["l3_s1"], "l3_s1")
    p.step("v", ["l3_l64"], ["l3_l64"], lambda e: e.tensor_scalar(
        out=c["l3_l64"][:, 2:3], in0=c["l3_l64"][:, 1:2], scalar1=-1.0, scalar2=None,
        op0=ALU.mult))
    for (srck, dstk) in (("l3_tre63", "l3_lamsre"), ("l3_tim63", "l3_lamsim")):
        p.step("t", [srck, "ident_bf"], ["ps_T"], lambda e, srck=srck: e.transpose(
            out=c["ps_T"][0:64, 0:64], in_=c[srck][:], identity=c["ident_bf"][0:64, 0:64]))
        p.step("v", ["ps_T"], [dstk], lambda e, dstk=dstk: e.tensor_copy(
            out=c[dstk][:], in_=c["ps_T"][0:64, 0:64]))
    p.dma("g", ["l3_ipT"], ["IN"], lambda e: e.dma_start(
        out=c["l3_srebf"][0][:], in_=wh["in_projT"]))
    p.dma("g", ["l3_op"], ["IN"], lambda e: e.dma_start(
        out=c["l3_simbf"][0][:], in_=wh["out_proj"]))
    p.step("v", ["l3_ipT", "l3_op"], ["l3_w"], lambda e: e.tensor_tensor(
        out=c["l3_w"][:], in0=c["l3_srebf"][0][:], in1=c["l3_simbf"][0][:], op=ALU.mult))


def _l3a_alloc(p, c, T):
    c["l3_fcol"] = p.sb("l3_fcol", [128, 1], F32)
    c["l3_dcol"] = p.sb("l3_dcol", [128, 1], F32)
    c["l3_dec"] = p.sb("l3_dec", [128, 1], F32)
    c["l3_fq"] = p.sb("l3_fq", [128, 1], F32)
    c["l3_offs"] = p.sb("l3_offs", [128, 1], F32)
    c["l3_sgn"] = p.sb("l3_sgn", [128, 1], F32)
    c["l3_e63"] = p.sb("l3_e63", [128, 64], F32)
    c["l3_e1"] = p.sb("l3_e1", [128, 64], F32)
    c["l3_edir"] = p.sb("l3_edir", [64, 64], F32)
    c["l3_e64"] = p.sb("l3_e64", [128, 1], F32)
    c["l3_s1"] = p.sb("l3_s1", [128, 64], F32)
    c["l3_s2"] = p.sb("l3_s2", [128, 64], F32)
    c["l3_tre63"] = p.sb("l3_tre63", [64, 64], BF16)
    c["l3_tim63"] = p.sb("l3_tim63", [64, 64], BF16)
    c["l3_lamsre"] = p.sb("l3_lamsre", [64, 64], BF16)
    c["l3_lamsim"] = p.sb("l3_lamsim", [64, 64], BF16)
    c["l3_lemre"] = p.sb("l3_lemre", [64, 64], BF16)
    c["l3_lemim"] = p.sb("l3_lemim", [64, 64], BF16)
    c["l3_ldir"] = p.sb("l3_ldir", [64, 64], BF16)
    c["l3_l64"] = p.sb("l3_l64", [64, 8], F32)
    c["l3_w"] = p.sb("l3_w", [64, 1024], BF16)
    c["l3_sre"] = p.sb("l3_sre", [64, 1024], F32)
    c["l3_sim"] = p.sb("l3_sim", [64, 1024], F32)
    c["l3_srebf"] = p.sb("l3_srebf", [64, 1024], BF16)
    c["l3_simbf"] = p.sb("l3_simbf", [64, 1024], BF16)
    c["l3_mw"] = p.sb("l3_mw", [64, 2048], F32)
    c["l3_urow"] = [p.sb(f"l3_urow{j}", [64, 1024], BF16) for j in range(2)]
    c["ps_m"] = p.ps("ps_m", [64, 1024], F32)


def l3a_consts(frequencies, decays):
    fdup = np.concatenate([frequencies, frequencies]).astype(np.float32)[:, None]
    ddup = np.concatenate([decays, decays]).astype(np.float32)[:, None]
    t = np.arange(64, dtype=np.float32)
    return {
        "c_fcol": fdup, "c_dcol": ddup,
        "c_e63": np.tile((63 - t)[None, :], (128, 1)),
        "c_e1": np.tile((t + 1)[None, :], (128, 1)),
        "c_edir": np.tile(t[None, :], (64, 1)),
        "c_e64": np.full((128, 1), 64.0, np.float32),
        "c_offs": np.concatenate([np.full(64, np.pi / 2), np.zeros(64)]).astype(np.float32)[:, None],
        "c_sgn": np.concatenate([np.ones(64), -np.ones(64)]).astype(np.float32)[:, None],
    }


def _l3a_load_consts(p, c, wh):
    for key, name in (("l3_fcol", "c_fcol"), ("l3_dcol", "c_dcol"), ("l3_offs", "c_offs"),
                      ("l3_sgn", "c_sgn"), ("l3_e63", "c_e63"), ("l3_e1", "c_e1"),
                      ("l3_edir", "c_edir"), ("l3_e64", "c_e64")):
        p.dma("q", [key], ["IN"], lambda e, key=key, name=name: e.dma_start(
            out=c[key][:], in_=wh[name]))


def _l3a_states(p, c, T, xldram, init_zero=True):
    pass


def _l3a_chunk_state(p, c, j, jb, urow_src_key, emit_psa=False):
    sre_p, sim_p = c["l3_sre"][1 - jb], c["l3_sim"][1 - jb]
    SREP, SIMP = f"l3_sre{1 - jb}", f"l3_sim{1 - jb}"
    sre_n, sim_n = c["l3_sre"][jb], c["l3_sim"][jb]
    SREN, SIMN = f"l3_sre{jb}", f"l3_sim{jb}"
    srebf_p, simbf_p = c["l3_srebf"][1 - jb], c["l3_simbf"][1 - jb]
    SREBFP, SIMBFP = f"l3_srebf{1 - jb}", f"l3_simbf{1 - jb}"
    ur = c["l3_urow"][jb]
    URK = f"l3_urow{jb}"
    if emit_psa:
        def mm_em(e):
            insts = []
            for g in range(8):
                insts.append(e.matmul(out=c["ps_a"][:, g * 64:(g + 1) * 64],
                                      lhsT=srebf_p[:, g * 128:(g + 1) * 128],
                                      rhs=c["l3_lemre"][:], start=True, stop=False))
                insts.append(e.matmul(out=c["ps_a"][:, g * 64:(g + 1) * 64],
                                      lhsT=simbf_p[:, g * 128:(g + 1) * 128],
                                      rhs=c["l3_lemim"][:], start=False, stop=True))
            return insts
        p.step("t", [SREBFP, SIMBFP, "l3_lemre", "l3_lemim"], ["ps_a"], mm_em)
    def mm_m(e):
        return [
            e.matmul(out=c["ps_h1"][0:64, 0:512], lhsT=c["l3_lamsre"][:], rhs=ur[:, 0:512],
                     start=True, stop=True),
            e.matmul(out=c["ps_h2"][0:64, 0:512], lhsT=c["l3_lamsre"][:], rhs=ur[:, 512:1024],
                     start=True, stop=True),
            e.matmul(out=c["ps_y1"][0:64, 0:512], lhsT=c["l3_lamsim"][:], rhs=ur[:, 0:512],
                     start=True, stop=True),
            e.matmul(out=c["ps_y2"][0:64, 0:512], lhsT=c["l3_lamsim"][:], rhs=ur[:, 512:1024],
                     start=True, stop=True)]
    p.step("t", ["l3_lamsre", "l3_lamsim", URK], ["ps_h1", "ps_h2", "ps_y1", "ps_y2"], mm_m)
    mw = c["l3_mw"]
    p.step("v", ["ps_h1", "l3_w"], ["l3_mw"], lambda e: e.tensor_tensor(
        out=mw[:, 0:512], in0=c["ps_h1"][0:64, 0:512], in1=c["l3_w"][:, 0:512], op=ALU.mult))
    p.step("v", ["ps_h2", "l3_w"], ["l3_mw"], lambda e: e.tensor_tensor(
        out=mw[:, 512:1024], in0=c["ps_h2"][0:64, 0:512], in1=c["l3_w"][:, 512:1024],
        op=ALU.mult))
    p.step("v", ["ps_y1", "l3_w"], ["l3_mw2"], lambda e: e.tensor_tensor(
        out=mw[:, 1024:1536], in0=c["ps_y1"][0:64, 0:512], in1=c["l3_w"][:, 0:512],
        op=ALU.mult))
    p.step("v", ["ps_y2", "l3_w"], ["l3_mw2"], lambda e: e.tensor_tensor(
        out=mw[:, 1536:2048], in0=c["ps_y2"][0:64, 0:512], in1=c["l3_w"][:, 512:1024],
        op=ALU.mult))
    l64 = c["l3_l64"]
    t1, t2 = c["l3_t1"], c["l3_t2"]
    p.step("v", [SIMP, "l3_mw", "l3_l64"], ["l3_t1"], lambda e: e.scalar_tensor_tensor(
        out=t1[:], in0=sim_p[:], scalar=l64[:, 2:3], in1=mw[:, 0:1024],
        op0=ALU.mult, op1=ALU.add))
    p.step("v", [SREP, "l3_t1", "l3_l64"], [SREN], lambda e: e.scalar_tensor_tensor(
        out=sre_n[:], in0=sre_p[:], scalar=l64[:, 0:1], in1=t1[:], op0=ALU.mult, op1=ALU.add))
    p.step("v", [SREP, "l3_mw2", "l3_l64"], ["l3_t2"], lambda e: e.scalar_tensor_tensor(
        out=t2[:], in0=sre_p[:], scalar=l64[:, 1:2], in1=mw[:, 1024:2048],
        op0=ALU.mult, op1=ALU.add))
    p.step("v", [SIMP, "l3_t2", "l3_l64"], [SIMN], lambda e: e.scalar_tensor_tensor(
        out=sim_n[:], in0=sim_p[:], scalar=l64[:, 0:1], in1=t2[:], op0=ALU.mult, op1=ALU.add))
    p.step("v", [SREN], [f"l3_srebf{jb}"], lambda e: e.tensor_copy(
        out=c["l3_srebf"][jb][:], in_=sre_n[:]))
    p.step("v", [SIMN], [f"l3_simbf{jb}"], lambda e: e.tensor_copy(
        out=c["l3_simbf"][jb][:], in_=sim_n[:]))


def run_l3_attn_S(x_full, in_proj, out_proj, frequencies, decays):
    nc = make_nc()
    p = Prog(nc)
    c = setup_common(p)
    alloc_x(p, c)
    xh = p.dram("x_in", [T_CORE, D], F32, kind="ExternalInput")
    sre_h = p.dram("sre_out", [64, D], F32, kind="ExternalOutput")
    sim_h = p.dram("sim_out", [64, D], F32, kind="ExternalOutput")
    wh = {
        "in_projT": p.dram("w_ipT", [64, D], F32, kind="ExternalInput"),
        "out_proj": p.dram("w_op", [64, D], F32, kind="ExternalInput"),
    }
    for name in ("c_fcol", "c_dcol", "c_offs", "c_sgn", "c_e64"):
        wh[name] = p.dram(name, [128, 1], F32, kind="ExternalInput")
    for name in ("c_e63", "c_e1"):
        wh[name] = p.dram(name, [128, 64], F32, kind="ExternalInput")
    wh["c_edir"] = p.dram("c_edir", [64, 64], F32, kind="ExternalInput")
    _l3a_alloc(p, c, T_CORE)
    c["l3_sre"] = [p.sb(f"l3_sre{j}", [64, 1024], F32) for j in range(2)]
    c["l3_sim"] = [p.sb(f"l3_sim{j}", [64, 1024], F32) for j in range(2)]
    c["l3_srebf"] = [p.sb(f"l3_srebf{j}", [64, 1024], BF16) for j in range(2)]
    c["l3_simbf"] = [p.sb(f"l3_simbf{j}", [64, 1024], BF16) for j in range(2)]
    c["l3_t1"] = p.sb("l3_t1", [64, 1024], F32)
    c["l3_t2"] = p.sb("l3_t2", [64, 1024], F32)
    xldram = p.dram("xldram", [T_CORE, D], BF16)
    load_x(p, c, xh)
    emit_ln_fm(p, c, store_dram=xldram)
    _l3a_load_consts(p, c, wh)
    _l3a_lambda_tables(p, c, wh)
    p.step("v", [], ["l3_sre1"], lambda e: e.memset(c["l3_sre"][1][:], 0.0))
    p.step("v", [], ["l3_sim1"], lambda e: e.memset(c["l3_sim"][1][:], 0.0))
    for j in range(T_CORE // 64):
        jb = j % 2
        p.dma("q", [f"l3_urow{jb}"], ["xld"], lambda e, j=j, jb=jb: e.dma_start(
            out=c["l3_urow"][jb][:], in_=xldram[j * 64:(j + 1) * 64, :]), tag=f"ur{j}")
        _l3a_chunk_state(p, c, j, jb, f"l3_urow{jb}")
    jlast = (T_CORE // 64 - 1) % 2
    p.dma("q", ["OUT"], [f"l3_sre{jlast}"], lambda e: e.dma_start(
        out=sre_h, in_=c["l3_sre"][jlast][:]))
    p.dma("q", ["OUT"], [f"l3_sim{jlast}"], lambda e: e.dma_start(
        out=sim_h, in_=c["l3_sim"][jlast][:]))
    p.build()
    consts = l3a_consts(frequencies, decays)
    consts.update(common_const_inputs(p))
    maps = []
    for core in range(N_CORES):
        m = {"x_in": np.ascontiguousarray(x_full[core * T_CORE:(core + 1) * T_CORE]),
             "w_ipT": np.ascontiguousarray(in_proj.T), "w_op": out_proj}
        m.update(consts)
        maps.append(m)
    res = run_bass_kernel_spmd(nc, maps, core_ids=list(range(N_CORES)))
    return [(r["sre_out"], r["sim_out"]) for r in res.results]


def run_l3_attn_F(x_full, slots_re, slots_im, gapexp, in_proj, out_proj, frequencies,
                  decays, ssWT, ss_b, gateWT, gate_b, finWT, fin_b):
    SEG = 256
    NSEG = T_CORE // SEG
    nc = make_nc()
    p = Prog(nc)
    c = setup_common(p)
    alloc_x(p, c)
    xh = p.dram("x_in", [T_CORE, D], F32, kind="ExternalInput")
    oh = p.dram("x_out", [T_CORE, D], F32, kind="ExternalOutput")
    wh = {
        "in_projT": p.dram("w_ipT", [64, D], F32, kind="ExternalInput"),
        "out_proj": p.dram("w_op", [64, D], F32, kind="ExternalInput"),
        "ssWT": p.dram("w_ssT", [D, D], F32, kind="ExternalInput"),
        "gateWT": p.dram("w_gT", [D, D], F32, kind="ExternalInput"),
        "finWT": p.dram("w_fT", [D, D], F32, kind="ExternalInput"),
        "b3": p.dram("w_b3", [1, 3, D], F32, kind="ExternalInput"),
        "slre": p.dram("w_slre", [3, 64, D], F32, kind="ExternalInput"),
        "slim": p.dram("w_slim", [3, 64, D], F32, kind="ExternalInput"),
        "gapexp": p.dram("c_gapexp", [3, 64], F32, kind="ExternalInput"),
    }
    for name in ("c_fcol", "c_dcol", "c_offs", "c_sgn", "c_e64"):
        wh[name] = p.dram(name, [128, 1], F32, kind="ExternalInput")
    for name in ("c_e63", "c_e1"):
        wh[name] = p.dram(name, [128, 64], F32, kind="ExternalInput")
    wh["c_edir"] = p.dram("c_edir", [64, 64], F32, kind="ExternalInput")
    _l3a_alloc(p, c, T_CORE)
    c["l3_sre"] = [p.sb(f"l3_sre{j}", [64, 1024], F32) for j in range(2)]
    c["l3_sim"] = [p.sb(f"l3_sim{j}", [64, 1024], F32) for j in range(2)]
    c["l3_srebf"] = [p.sb(f"l3_srebf{j}", [64, 1024], BF16) for j in range(2)]
    c["l3_simbf"] = [p.sb(f"l3_simbf{j}", [64, 1024], BF16) for j in range(2)]
    c["l3_t1"] = p.sb("l3_t1", [64, 1024], F32)
    c["l3_t2"] = p.sb("l3_t2", [64, 1024], F32)
    c["l3_kdir"] = p.sb("l3_kdir", [128, 8 * 64], F32)
    c["l3_y"] = p.sb("l3_y", [128, 8 * SEG], F32)
    c["l3_ybf"] = p.sb("l3_ybf", [128, 8 * SEG], BF16)
    c["l3_wbuf"] = p.sb("l3_wbuf", [128, 8 * D], BF16)
    c["l3_b3"] = p.sb("l3_b3", [1, 3 * D], BF16)
    c["l3_sso"] = p.sb("l3_sso", [128, 2 * D], BF16)
    c["l3_gso"] = p.sb("l3_gso", [128, D], BF16)
    c["l3_gfm"] = p.sb("l3_gfm", [128, 8 * 128], BF16)
    xldram = p.dram("xldram_f", [T_CORE, D], BF16)
    load_x(p, c, xh)
    emit_ln_fm(p, c, store_dram=xldram)
    _l3a_load_consts(p, c, wh)
    _l3a_lambda_tables(p, c, wh)
    def load_wbuf(wn):
        p.dma("g", ["l3_wbuf"], ["IN"], lambda e, wn=wn: e.dma_start(
            out=c["l3_wbuf"][:].rearrange("p (k d) -> p k d", k=8),
            in_=wh[wn].rearrange("(k p) d -> p k d", p=128)), tag=f"l3{wn}")
    p.dma("g", ["l3_b3"], ["IN"], lambda e: e.dma_start(
        out=c["l3_b3"][:].rearrange("o (f d) -> o f d", f=3), in_=wh["b3"]))
    for g in range(8):
        p.step("t", ["l3_w", "l3_ldir"], ["ps_a"], lambda e, g=g: e.matmul(
            out=c["ps_a"][:, 0:64], lhsT=c["l3_w"][:, g * 128:(g + 1) * 128],
            rhs=c["l3_ldir"][:], start=True, stop=True))
        p.step("v", ["ps_a"], ["l3_kdir"], lambda e, g=g: e.tensor_copy(
            out=c["l3_kdir"][:, g * 64:(g + 1) * 64], in_=c["ps_a"][:, 0:64]))
    if True:
        dbg_k = p.dram("dbg_k", [128, 8 * 64], F32, kind="ExternalOutput")
        p.dma("q", ["DBG"], ["l3_kdir"], lambda e: e.dma_start(out=dbg_k, in_=c["l3_kdir"][:]))
        dbg_y = p.dram("dbg_y", [128, 8 * 256], F32, kind="ExternalOutput")
    p.step("v", [], ["l3_sre1"], lambda e: e.memset(c["l3_sre"][1][:], 0.0))
    p.step("v", [], ["l3_sim1"], lambda e: e.memset(c["l3_sim"][1][:], 0.0))
    for m in range(3):
        build_src = c["l3_gap"] if False else None
        p.dma("q", ["l3_gapc"], ["IN"], lambda e, m=m: e.dma_start(
            out=c["l3_s1"][0:64, 0:1], in_=wh["gapexp"][m].unsqueeze(1)))
        TP = 2 * np.pi
        C1, C2 = 6.28125, float(2 * np.pi - 6.28125)
        for (idx, use_offs) in ((0, True), (1, False)):
            p.step("v", ["l3_gapc", "l3_fq"], ["l3_gp1"], lambda e: e.tensor_scalar(
                out=c["l3_t1"][0:64, 4:5], in0=c["l3_s1"][0:64, 0:1],
                scalar1=c["l3_fq"][0:64, 0:1], scalar2=None, op0=ALU.mult))
            if use_offs:
                p.step("v", ["l3_gp1", "l3_offs"], ["l3_gp1"], lambda e: e.tensor_scalar(
                    out=c["l3_t1"][0:64, 4:5], in0=c["l3_t1"][0:64, 4:5],
                    scalar1=c["l3_offs"][0:64, 0:1], scalar2=None, op0=ALU.add))
            p.step("v", ["l3_gp1"], ["l3_gp2"], lambda e: e.tensor_scalar(
                out=c["l3_t1"][0:64, 5:6], in0=c["l3_t1"][0:64, 4:5], scalar1=1.0 / TP,
                scalar2=QC, op0=ALU.mult, op1=ALU.add))
            p.step("v", ["l3_gp2"], ["l3_gp2"], lambda e: e.tensor_single_scalar(
                out=c["l3_t1"][0:64, 5:6], in_=c["l3_t1"][0:64, 5:6], scalar=QC,
                op=ALU.subtract))
            p.step("v", ["l3_gp2", "l3_gp1"], ["l3_gp1"], lambda e: e.scalar_tensor_tensor(
                out=c["l3_t1"][0:64, 4:5], in0=c["l3_t1"][0:64, 5:6], scalar=-C1,
                in1=c["l3_t1"][0:64, 4:5], op0=ALU.mult, op1=ALU.add))
            p.step("v", ["l3_gp2", "l3_gp1"], ["l3_gp1"], lambda e: e.scalar_tensor_tensor(
                out=c["l3_t1"][0:64, 4:5], in0=c["l3_t1"][0:64, 5:6], scalar=-C2,
                in1=c["l3_t1"][0:64, 4:5], op0=ALU.mult, op1=ALU.add))
            p.step("s", ["l3_gp1"], ["l3_gp1"], lambda e: e.activation(
                out=c["l3_t1"][0:64, 4:5], in_=c["l3_t1"][0:64, 4:5], func=AF.Sin))
            p.step("v", ["l3_gapc", "l3_dec"], ["l3_gp2"], lambda e: e.tensor_scalar(
                out=c["l3_t1"][0:64, 5:6], in0=c["l3_s1"][0:64, 0:1],
                scalar1=c["l3_dec"][0:64, 0:1], scalar2=None, op0=ALU.mult))
            p.step("s", ["l3_gp2"], ["l3_gp2"], lambda e: e.activation(
                out=c["l3_t1"][0:64, 5:6], in_=c["l3_t1"][0:64, 5:6], func=AF.Exp))
            p.step("v", ["l3_gp1", "l3_gp2"], [f"l3_gap{idx}"], lambda e, idx=idx:
                   e.tensor_tensor(out=c["l3_l64"][:, 4 + idx:5 + idx],
                                   in0=c["l3_t1"][0:64, 4:5], in1=c["l3_t1"][0:64, 5:6],
                                   op=ALU.mult))
        p.dma("q", ["l3_slot"], ["IN"], lambda e, m=m: e.dma_start(
            out=c["l3_t2"][:], in_=wh["slre"][m]))
        p.step("v", ["l3_slot", "l3_gap0", "l3_sre1"], ["l3_sre1"], lambda e:
               e.scalar_tensor_tensor(out=c["l3_sre"][1][:], in0=c["l3_t2"][:],
                                      scalar=c["l3_l64"][:, 4:5], in1=c["l3_sre"][1][:],
                                      op0=ALU.mult, op1=ALU.add))
        p.step("v", ["l3_slot", "l3_gap1", "l3_sim1"], ["l3_sim1"], lambda e:
               e.scalar_tensor_tensor(out=c["l3_sim"][1][:], in0=c["l3_t2"][:],
                                      scalar=c["l3_l64"][:, 5:6], in1=c["l3_sim"][1][:],
                                      op0=ALU.mult, op1=ALU.add))
        p.dma("q", ["l3_slot"], ["IN"], lambda e, m=m: e.dma_start(
            out=c["l3_t2"][:], in_=wh["slim"][m]))
        p.step("v", ["l3_slot", "l3_gap1", "l3_sre1"], ["l3_t1"], lambda e:
               e.scalar_tensor_tensor(out=c["l3_t1"][:], in0=c["l3_t2"][:],
                                      scalar=c["l3_l64"][:, 5:6], in1=c["l3_sre"][1][:],
                                      op0=ALU.mult, op1=ALU.subtract))
        p.step("v", ["l3_t1"], ["l3_sre1"], lambda e: e.tensor_scalar(
            out=c["l3_sre"][1][:], in0=c["l3_t1"][:], scalar1=-1.0, scalar2=None,
            op0=ALU.mult))
        p.step("v", ["l3_slot", "l3_gap0", "l3_sim1"], ["l3_sim1"], lambda e:
               e.scalar_tensor_tensor(out=c["l3_sim"][1][:], in0=c["l3_t2"][:],
                                      scalar=c["l3_l64"][:, 4:5], in1=c["l3_sim"][1][:],
                                      op0=ALU.mult, op1=ALU.add))
    p.step("v", ["l3_sre1"], ["l3_srebf1"], lambda e: e.tensor_copy(
        out=c["l3_srebf"][1][:], in_=c["l3_sre"][1][:]))
    p.step("v", ["l3_sim1"], ["l3_simbf1"], lambda e: e.tensor_copy(
        out=c["l3_simbf"][1][:], in_=c["l3_sim"][1][:]))

    ident, ones = c["ident_bf"], c["ones_bf"]
    for seg in range(NSEG):
        seg0 = seg * SEG
        for jj in range(SEG // 64):
            j = seg0 // 64 + jj
            jb = j % 2
            p.dma("q", [f"l3_urow{jb}"], ["xld"], lambda e, j=j, jb=jb: e.dma_start(
                out=c["l3_urow"][jb][:], in_=xldram[j * 64:(j + 1) * 64, :]), tag=f"ur{j}")
            _l3a_chunk_state(p, c, j, jb, f"l3_urow{jb}", emit_psa=True)
            p.step("v", ["ps_a"], ["l3_y"], lambda e, jj=jj: e.tensor_copy(
                out=c["l3_y"][:].rearrange("p (g t) -> p g t", g=8)[:, :, jj * 64:(jj + 1) * 64],
                in_=c["ps_a"][:, 0:512].rearrange("p (g t) -> p g t", g=8)))
        for g in range(8):
            eng = "v"
            for dl in range(64):
                def op(e, g=g, dl=dl, seg0=seg0):
                    yv = c["l3_y"][:].rearrange("p (g c t) -> p g c t", g=8, c=SEG // 64)
                    uv = c["x_fm"]
                    if dl == 0:
                        return e.scalar_tensor_tensor(
                            out=yv[:, g, :, 0:64], in0=uv[:, g * T_CORE + seg0:
                                                         g * T_CORE + seg0 + SEG
                                                         ].rearrange("p (c t) -> p c t", c=SEG // 64),
                            scalar=c["l3_kdir"][:, g * 64:g * 64 + 1],
                            in1=yv[:, g, :, 0:64], op0=ALU.mult, op1=ALU.add)
                    return e.scalar_tensor_tensor(
                        out=yv[:, g, :, dl:64],
                        in0=uv[:, g * T_CORE + seg0: g * T_CORE + seg0 + SEG
                               ].rearrange("p (c t) -> p c t", c=SEG // 64)[:, :, 0:64 - dl],
                        scalar=c["l3_kdir"][:, g * 64 + dl:g * 64 + dl + 1],
                        in1=yv[:, g, :, dl:64], op0=ALU.mult, op1=ALU.add)
                p.step(eng, ["x_fm", "l3_kdir", "l3_y"], ["l3_y"], op)
        if seg == 0:
            p.dma("q", ["DBGY"], ["l3_y"], lambda e: e.dma_start(out=dbg_y, in_=c["l3_y"][:]))
        p.step("v", ["l3_y"], ["l3_ybf"], lambda e: e.tensor_copy(
            out=c["l3_ybf"][:], in_=c["l3_y"][:]))
        ntile = SEG // 128
        load_wbuf("ssWT")
        for ti in range(ntile):
            i = (seg0 // 128) + ti
            toff = ti * 128
            for db in range(2):
                psy = c["ps_y1"] if db == 0 else c["ps_y2"]
                PSY = "ps_y1" if db == 0 else "ps_y2"
                def mm_ss(e, db=db, toff=toff, psy=psy):
                    insts = [e.matmul(
                        out=psy[:, 0:512],
                        lhsT=c["l3_ybf"][:, k * SEG + toff: k * SEG + toff + 128],
                        rhs=c["l3_wbuf"][:, k * D + db * 512: k * D + (db + 1) * 512],
                        start=(k == 0), stop=False) for k in range(8)]
                    insts.append(e.matmul(
                        out=psy[:, 0:512], lhsT=ones[:, 0:128],
                        rhs=c["l3_b3"][:, 1 * D + db * 512: 1 * D + (db + 1) * 512],
                        start=False, stop=True))
                    return insts
                p.step("t", ["l3_ybf", "l3_wbuf", "l3_b3", "ones_bf"], [PSY], mm_ss)
                p.step("s", [PSY], [f"l3_sso{ti}"], lambda e, db=db, ti=ti, psy=psy:
                       e.activation(out=c["l3_sso"][:, ti * D + db * 512: ti * D + (db + 1) * 512],
                                    in_=psy[:, 0:512], func=AF.Gelu))
        load_wbuf("gateWT")
        for ti in range(ntile):
            i = (seg0 // 128) + ti
            for db in range(2):
                psy = c["ps_y1"] if db == 0 else c["ps_y2"]
                PSY = "ps_y1" if db == 0 else "ps_y2"
                def mm_g(e, db=db, i=i, psy=psy):
                    insts = [e.matmul(
                        out=psy[:, 0:512],
                        lhsT=c["x_fm"][:, k * T_CORE + i * 128: k * T_CORE + (i + 1) * 128],
                        rhs=c["l3_wbuf"][:, k * D + db * 512: k * D + (db + 1) * 512],
                        start=(k == 0), stop=False) for k in range(8)]
                    insts.append(e.matmul(
                        out=psy[:, 0:512], lhsT=ones[:, 0:128],
                        rhs=c["l3_b3"][:, 2 * D + db * 512: 2 * D + (db + 1) * 512],
                        start=False, stop=True))
                    return insts
                p.step("t", ["x_fm", "l3_wbuf", "l3_b3", "ones_bf"], [PSY], mm_g)
                p.step("s", [PSY], ["l3_gso"], lambda e, db=db, psy=psy:
                       e.activation(out=c["l3_gso"][:, db * 512:(db + 1) * 512],
                                    in_=psy[:, 0:512], func=AF.Gelu))
            p.step("v", [f"l3_sso{ti}", "l3_gso"], [f"l3_gs{ti}"], lambda e, ti=ti:
                   e.tensor_tensor(out=c["l3_sso"][:, ti * D:(ti + 1) * D],
                                   in0=c["l3_gso"][:],
                                   in1=c["l3_sso"][:, ti * D:(ti + 1) * D], op=ALU.mult))
        load_wbuf("finWT")
        for ti in range(ntile):
            i = (seg0 // 128) + ti
            for ch in range(8):
                p.step("t", [f"l3_gs{ti}", "ident_bf"], ["ps_T"], lambda e, ti=ti, ch=ch:
                       e.transpose(out=c["ps_T"][:, 0:128],
                                   in_=c["l3_sso"][:, ti * D + ch * 128: ti * D + (ch + 1) * 128],
                                   identity=ident[:]))
                p.step("v", ["ps_T"], ["l3_gfm"], lambda e, ch=ch: e.tensor_copy(
                    out=c["l3_gfm"][:, ch * 128:(ch + 1) * 128], in_=c["ps_T"][:, 0:128]))
            for db in range(2):
                psy = c["ps_h1"] if db == 0 else c["ps_h2"]
                PSY = "ps_h1" if db == 0 else "ps_h2"
                def mm_f(e, db=db, psy=psy):
                    insts = [e.matmul(
                        out=psy[:, 0:512], lhsT=c["l3_gfm"][:, k * 128:(k + 1) * 128],
                        rhs=c["l3_wbuf"][:, k * D + db * 512: k * D + (db + 1) * 512],
                        start=(k == 0), stop=False) for k in range(8)]
                    insts.append(e.matmul(
                        out=psy[:, 0:512], lhsT=ones[:, 0:128],
                        rhs=c["l3_b3"][:, 0 * D + db * 512: 0 * D + (db + 1) * 512],
                        start=False, stop=True))
                    return insts
                p.step("t", ["l3_gfm", "l3_wbuf", "l3_b3", "ones_bf"], [PSY], mm_f)
                p.step("v", [PSY, f"xt{i}"], [f"xt{i}"], lambda e, i=i, db=db, psy=psy:
                       e.tensor_tensor(out=c["xt"][i][:, db * 512:(db + 1) * 512],
                                       in0=psy[:, 0:512],
                                       in1=c["xt"][i][:, db * 512:(db + 1) * 512],
                                       op=ALU.add))
    store_x(p, c, oh)
    p.build()
    consts = l3a_consts(frequencies, decays)
    consts.update(common_const_inputs(p))
    maps = []
    for core in range(N_CORES):
        m = {"x_in": np.ascontiguousarray(x_full[core * T_CORE:(core + 1) * T_CORE]),
             "w_ipT": np.ascontiguousarray(in_proj.T), "w_op": out_proj,
             "w_ssT": ssWT, "w_gT": gateWT, "w_fT": finWT,
             "w_b3": np.ascontiguousarray(np.stack([fin_b, ss_b, gate_b])[None]),
             "w_slre": slots_re[core], "w_slim": slots_im[core],
             "c_gapexp": gapexp[core]}
        m.update(consts)
        maps.append(m)
    res = run_bass_kernel_spmd(nc, maps, core_ids=list(range(N_CORES)))
    globals()['L3_DBG'] = [(r.get("dbg_k"), r.get("dbg_y")) for r in res.results]
    return np.concatenate([r["x_out"] for r in res.results], axis=0)


def _l1_ffn_full(x, wg, bg, w1, b1, w2, b2, gamma, beta):
    masked = run_l1_ffn_A(x, wg, bg).reshape(2, 8192, 16)
    sel = run_l1_ffn_B(masked)
    out = np.empty_like(x)
    for half in range(2):
        xh = np.zeros((8192, 1024), dtype=np.float32)
        msel = np.zeros((8, 1, 16 * 1024), dtype=np.float32)
        for core in range(8):
            b = core // 4
            s0 = (core % 4) * 2048 + half * 1024
            xh[core * 1024:(core + 1) * 1024] = x[b * 8192 + s0: b * 8192 + s0 + 1024]
            for e in range(16):
                msel[core, 0, e * 1024:(e + 1) * 1024] = sel[b, e, s0:s0 + 1024]
        oh = run_l1_ffn_C(xh, msel, w1, b1, w2, b2, gamma, beta)
        for core in range(8):
            b = core // 4
            s0 = (core % 4) * 2048 + half * 1024
            out[b * 8192 + s0: b * 8192 + s0 + 1024] = oh[core * 1024:(core + 1) * 1024]
    return out


def _l3_attn_full(x, in_proj, out_proj, frequencies, decays, ss_W, ss_b,
                  gate_W, gate_b, fin_W, fin_b):
    sl = run_l3_attn_S(x, in_proj, out_proj, frequencies, decays)
    slots_re, slots_im, gapexp = [], [], []
    for core in range(8):
        pos = core % 4
        sre = np.zeros((3, 64, 1024), np.float32)
        sim = np.zeros((3, 64, 1024), np.float32)
        ge = np.zeros((3, 64), np.float32)
        for m in range(3):
            if m < pos:
                src_core = (core // 4) * 4 + m
                sre[m], sim[m] = sl[src_core]
                ge[m, :] = 2048.0 * (pos - 1 - m)
        slots_re.append(sre); slots_im.append(sim); gapexp.append(ge)
    return run_l3_attn_F(x, slots_re, slots_im, gapexp, in_proj, out_proj,
                         frequencies, decays, np.ascontiguousarray(ss_W.T), ss_b,
                         np.ascontiguousarray(gate_W.T), gate_b,
                         np.ascontiguousarray(fin_W.T), fin_b)


def kernel(**inputs):
    g = lambda k: np.asarray(inputs[k])
    x = np.ascontiguousarray(g("L0_attn_x").reshape(16384, 1024).astype(np.float32))
    l0f = (g("L0_ffn_W_gate"), g("L0_ffn_b_gate"), g("L0_ffn_W1"), g("L0_ffn_b1"),
           g("L0_ffn_W2"), g("L0_ffn_b2"))
    x = run_l0_ffn(x, *l0f)
    qk, v = run_l0_attn_P(x, g("L0_attn_Wqk"), g("L0_attn_Wv"))
    w_all = run_l0_attn_H(qk, v, g("L0_attn_tau"))
    x = run_l0_attn_O(x, w_all, g("L0_attn_Wo"))
    x = run_l0_ffn(x, *l0f)
    l1f = (g("L1_ffn_wg"), g("L1_ffn_bg"), g("L1_ffn_w1"), g("L1_ffn_b1"),
           g("L1_ffn_w2"), g("L1_ffn_b2"), g("L1_ffn_gamma"), g("L1_ffn_beta"))
    x = _l1_ffn_full(x, *l1f)
    x = run_l1_attn(x, g("L1_attn_Wq"), g("L1_attn_bq"), g("L1_attn_Wk"), g("L1_attn_bk"),
                    g("L1_attn_Wv"), g("L1_attn_bv"), g("L1_attn_Wo"), g("L1_attn_bo"))
    x = _l1_ffn_full(x, *l1f)
    l2f = (g("L2_ffn_gate_w"), g("L2_ffn_bias_e"), g("L2_ffn_up_w"), g("L2_ffn_down_w"),
           g("L2_ffn_shared_up"), g("L2_ffn_shared_down"))
    x = run_l2_ffn(x, *l2f)
    xl, vv = run_l2_attn_P(x, np.ascontiguousarray(g("L2_attn_Ww").T))
    o = run_l2_attn_H(xl, vv, g("L2_attn_decay"), g("L2_attn_log_alpha"))
    x = run_l2_attn_O(x, o, xl, np.ascontiguousarray(g("L2_attn_Wr").T))
    x = run_l2_ffn(x, *l2f)
    wqg, wqu, wqd = run_l3_quant_weights(g("L3_ffn_w_gate"), g("L3_ffn_w_up"),
                                         g("L3_ffn_w_down"))
    x = run_l3_ffn(x, wqg, wqu, wqd)
    x = _l3_attn_full(x, g("L3_attn_in_proj"), g("L3_attn_out_proj"),
                      g("L3_attn_frequencies"), g("L3_attn_decays"),
                      g("L3_attn_ss_W"), g("L3_attn_ss_b"), g("L3_attn_gate_W"),
                      g("L3_attn_gate_b"), g("L3_attn_fin_W"), g("L3_attn_fin_b"))
    x = run_l3_ffn(x, wqg, wqu, wqd)
    return x.reshape(2, 8192, 1024)
